# version 06
# baseline (speedup 1.0000x reference)
import contextlib
import numpy as np

import concourse.bass as bass
import concourse.mybir as mybir
from concourse.bass_utils import run_bass_kernel_spmd

F32 = mybir.dt.float32
BF16 = mybir.dt.bfloat16
I32 = mybir.dt.int32
AF = mybir.ActivationFunctionType
ALU = mybir.AluOpType

D = 1024
NCH = 8
T = 2048
B_PER_CORE = 2
N_CORES = 8
HEADS = 16
HD = 64
LN_EPS = 1e-5
RMS_EPS = 1e-6
LIN_EPS = 1e-6
WINDOW = 64
CHUNK = 128
FEAT = 16
NQUAD = 136
NPHI = 153

ENGINES = ("sync", "tensor", "scalar", "vector", "gpsimd")

SUBLAYERS = ["ffn0a", "attn0", "ffn0b", "ffn1a", "attn1", "ffn1b"]


class Prog:

    def __init__(self, nc):
        self.nc = nc
        self.ops = []
        self.sems = {}
        self.counts = {}
        self.last_wait = {e: {} for e in ENGINES}
        self.es = contextlib.ExitStack()

    def sem(self, name):
        h = self.es.enter_context(self.nc.semaphore(name))
        self.sems[name] = h
        self.counts[name] = 0
        return name

    def tok(self, semname):
        return (semname, self.counts[semname])

    def emit(self, eng, fn, deps=(), inc=None, inc_n=1):
        deps = [d for d in deps if d is not None]
        needed = []
        for s, v in deps:
            if v <= 0:
                continue
            if self.last_wait[eng].get(s, 0) >= v:
                continue
            self.last_wait[eng][s] = v
            needed.append((s, v))
        if inc is not None:
            self.counts[inc] += inc_n
        tok = (inc, self.counts[inc]) if inc is not None else None
        sems = self.sems

        def run(e):
            for s, v in needed:
                e.wait_ge(sems[s], v)
            ins = fn(e)
            if inc is not None:
                if ins is None:
                    ins = e.nop()
                ins.then_inc(sems[inc], inc_n)

        self.ops.append((eng, run))
        return tok

    def wait(self, eng, deps):
        self.emit(eng, lambda e: None, deps=deps)

    def replay(self, eng_name, e):
        for name, fn in self.ops:
            if name == eng_name:
                fn(e)



def build_program(sublayers=None):
    if sublayers is None:
        sublayers = SUBLAYERS
    nc = bass.Bass("TRN2", target_bir_lowering=False)

    dram = {}

    def din(name, shape):
        dram[name] = nc.dram_tensor(name, shape, F32, kind="ExternalInput").ap()
        return dram[name]

    x_hbm = din("x", [B_PER_CORE, T, D])
    y_hbm = nc.dram_tensor("y", [B_PER_CORE, T, D], F32, kind="ExternalOutput").ap()

    din("l0_cn_w1", [D, D]);     din("l0_cn_b1", [128, 8])
    din("l0_cn_w2", [D, 512]);   din("l0_cn_b2", [128, 4])
    din("l0_cn_w3", [512, 256]); din("l0_cn_b3", [128, 2])
    din("l0_cn_w4", [128, 2]);   din("l0_cn_b4", [1, 1])
    din("l0_gc_w1", [D, 512]);   din("l0_gc_b1", [128, 4])
    din("l0_gc_w2", [512, 256]); din("l0_gc_b2", [128, 2])
    din("l0_gc_w3", [128, 2]);   din("l0_gc_b3", [1, 1])
    din("l0_ar", [1, 1])
    din("l0_wq", [D, D]); din("l0_wk", [D, D]); din("l0_wv", [D, D])
    din("l0_wo", [D, D]); din("l0_bo", [128, 8])
    din("l1_wq", [D, D]); din("l1_wk", [D, D]); din("l1_wv", [D, D])
    din("l1_wout", [2 * D, D])
    din("l1_wqf", [2 * HD, FEAT]); din("l1_wkf", [2 * HD, FEAT])
    din("l1_normw", [128, 8])
    din("l1_w1", [D, 4 * D]); din("l1_b1", [128, 32])
    din("l1_w2", [4 * D, D]); din("l1_b2", [128, 8])
    din("ident", [128, 128])
    din("kf_blockdiag", [128, 32])
    din("tril", [128, 128])
    din("winmask", [128, 3, 256])
    din("sel1", [FEAT, NQUAD])
    din("sel2", [FEAT, NQUAD])
    din("patz", [32, NCH, 128])
    din("id2", [2, 2])

    p = Prog(nc)
    es = p.es

    def sb(name, shape, dtype=F32):
        return es.enter_context(nc.sbuf_tensor(name, shape, dtype))

    with es:
        X = sb("X", [128, NCH, T])
        O = sb("O", [128, NCH, T])
        Ob = O.ap().bitcast(BF16)
        WA = sb("WA", [128, 18464], BF16)
        SC = sb("SC", [128, 8192], BF16)
        ST = sb("ST", [128, 2048])

        STT = sb("STT", [128, 160])
        STTB = sb("STTB", [128, 64], BF16)
        ROWB = sb("ROWB", [2, 2, 2048], BF16)
        STG = ST.ap()[0:1, 0:1024]
        CROWB = ROWB.ap()[0:1, 0, 0:1024]
        ONESC = sb("ONESC", [128, 2], BF16)
        ONESR = sb("ONESR", [2, 128], BF16)
        IDENT = sb("IDENT", [128, 128])
        BIAS = sb("BIAS", [128, 88])
        SMALL = sb("SMALL", [1, 64])
        GCH = sb("GCH", [128, 16], BF16)
        HALFT = sb("HALFT", [128, 512])
        TRIL = sb("TRIL", [128, 128], BF16)
        WMASK = sb("WMASK", [128, 3, 256], BF16)
        SEL = sb("SEL", [16, 2, NQUAD], BF16)
        KFBD = sb("KFBD", [128, 32], BF16)
        WQF = sb("WQF", [128, 16], BF16)
        WKF = sb("WKF", [128, 16], BF16)
        PATZ = sb("PATZ", [32, NCH, 128], BF16)
        ID2 = sb("ID2", [2, 2], BF16)
        ONESH = sb("ONESH", [128, 2], BF16)
        ONESBF = sb("ONESBF", [65, 512], BF16)
        ZT = sb("ZT", [128, 2, 256])
        ZTB = ZT.ap()[:, 0, :].bitcast(BF16).rearrange(
            "p (q c) -> p q c", q=2)
        ZRH = SC.ap()[0:32, 0:2048]
        BNDS = sb("BNDS", [128, 4])
        BSC = sb("BSC", [2, 16])
        BSCB = sb("BSCB", [2, 4], BF16)
        BROW = sb("BROW", [1, 8], BF16)

        PS = es.enter_context(nc.psum_tensor("PS", [128, 4096], F32))

        def bank(i, o0=0, n=512):
            return PS[:, 512 * i + o0: 512 * i + o0 + n]

        s_pe = p.sem("s_pe")
        s_act = p.sem("s_act")
        s_dve = p.sem("s_dve")
        s_gp = p.sem("s_gp")

        _dma_slots = {}

        def _slot_sem(slot):
            if slot not in _dma_slots:
                _dma_slots[slot] = p.sem("dq_" + slot)
            return _dma_slots[slot]

        def dma(dst, src, slot="init", queue="sync", deps=()):
            semname = _slot_sem(slot)
            eng = "sync" if queue == "sync" else "gpsimd"
            return p.emit(eng, lambda e: e.dma_start(out=dst, in_=src),
                          deps=deps, inc=semname, inc_n=16)

        def store(dst, src, slot="st0", deps=()):
            semname = _slot_sem(slot)
            return p.emit("sync", lambda e: e.dma_start(out=dst, in_=src),
                          deps=deps, inc=semname, inc_n=16)

        def mm(out, lhsT, rhs, start, stop, deps=(), inc=None):
            return p.emit(
                "tensor",
                lambda e: e.matmul(out, lhsT, rhs, start=start, stop=stop),
                deps=deps, inc=inc)

        def mmt(out, in_, deps=(), inc=None):
            return p.emit("tensor", lambda e: e.transpose(out, in_, IDENT[:]),
                          deps=deps, inc=inc)

        def act(out, in_, func, bias=0.0, scale=1.0, accum=None, deps=(), inc="s_act"):
            return p.emit(
                "scalar",
                lambda e: e.activation(out, in_, func, bias=bias, scale=scale,
                                       accum_out=accum),
                deps=deps, inc=inc)

        def acopy(out, in_, deps=(), inc="s_act"):
            return p.emit("scalar", lambda e: e.copy(out, in_), deps=deps, inc=inc)

        def dve(fn, deps=(), inc="s_dve"):
            return p.emit("vector", fn, deps=deps, inc=inc)

        def gp(fn, deps=(), inc="s_gp"):
            return p.emit("gpsimd", fn, deps=deps, inc=inc)

        def vdrain():
            p.emit("vector", lambda e: e.drain())

        t_id = dma(IDENT[:], dram["ident"][:])
        dve(lambda e: e.memset(ONESC[:], 1.0))
        dve(lambda e: e.memset(ONESR[:], 1.0))
        t_tril = dma(TRIL[:], dram["tril"][:], slot="initg", queue="gp")
        t_wmask = dma(WMASK[:], dram["winmask"][:], slot="initg", queue="gp")
        t_sel1 = dma(SEL[:, 0], dram["sel1"][:], slot="initg", queue="gp")
        t_sel2 = dma(SEL[:, 1], dram["sel2"][:], slot="initg", queue="gp")
        t_kfbd = dma(KFBD[:], dram["kf_blockdiag"][:], slot="initg", queue="gp")
        t_wqf = dma(WQF[:], dram["l1_wqf"][:], slot="initg", queue="gp")
        dma(WKF[:], dram["l1_wkf"][:], slot="initg", queue="gp")
        dma(PATZ[:], dram["patz"][:], slot="initg", queue="gp")
        dma(ID2[:], dram["id2"][:], slot="initg", queue="gp")
        dve(lambda e: e.memset(ONESH[:, :], 0.0))
        vdrain()
        dve(lambda e: e.memset(ONESH[0:64, 0:1], 1.0))
        dve(lambda e: e.memset(ONESH[64:128, 1:2], 1.0))
        dve(lambda e: e.memset(ONESBF[:, :], 1.0))
        gp(lambda e: e.memset(HALFT[:], 0.5))

        BIAS_MAP = {}
        _bias_col = [0]

        def load_bias(name, ncols):
            c0 = _bias_col[0]
            _bias_col[0] += ncols
            assert _bias_col[0] <= 76
            BIAS_MAP[name] = c0
            return dma(BIAS[:, c0:c0 + ncols], dram[name][:])

        for nm, wd in [("l0_cn_b1", 8), ("l0_cn_b2", 4), ("l0_cn_b3", 2),
                       ("l0_gc_b1", 4), ("l0_gc_b2", 2), ("l0_bo", 8),
                       ("l1_normw", 8), ("l1_b1", 32), ("l1_b2", 8)]:
            load_bias(nm, wd)

        def bias_ap(name, j):
            c = BIAS_MAP[name] + j
            return BIAS[:, c:c + 1]

        t_hb2 = None
        dma(SMALL[:, 0:1], dram["l0_cn_b4"][:])
        dma(SMALL[:, 1:2], dram["l0_gc_b3"][:])
        dma(SMALL[:, 2:3], dram["l0_ar"][:])
        t_params = p.tok(_slot_sem("init"))
        t_consts = p.tok(_slot_sem("initg"))
        t_id = t_params
        b2c = BIAS_MAP["l1_b2"]
        t_hb2 = dve(lambda e: e.tensor_scalar(
            BIAS[:, 76:84], BIAS[:, b2c:b2c + 8], 0.5, None, ALU.mult),
            deps=[t_params])
        vdrain()

        def rsqrt_dve(out, in_, tmp, deps=()):
            oi = out.bitcast(I32)
            vi = in_.bitcast(I32)
            dve(lambda e: e.tensor_scalar(oi, vi, 1, None,
                                          ALU.logical_shift_right), deps=deps)
            vdrain()
            dve(lambda e: e.tensor_scalar(oi, oi, 0x5F3759DF, None, ALU.subtract))
            vdrain()
            dve(lambda e: e.tensor_scalar(oi, oi, -1, None, ALU.mult))
            t0 = None
            for _ in range(3):
                vdrain()
                dve(lambda e: e.tensor_tensor(tmp, out, out, ALU.mult))
                vdrain()
                dve(lambda e: e.tensor_tensor(tmp, tmp, in_, ALU.mult))
                vdrain()
                dve(lambda e: e.tensor_scalar(tmp, tmp, -0.5, 1.5, ALU.mult, ALU.add))
                vdrain()
                t0 = dve(lambda e: e.tensor_tensor(out, out, tmp, ALU.mult))
            return t0

        def hilo(dst_hi, dst_lo, src, deps=()):
            dve(lambda e: e.tensor_copy(dst_hi, src), deps=deps)
            vdrain()
            t2 = dve(lambda e: e.tensor_tensor(dst_lo, src, dst_hi, ALU.subtract))
            return t2

        _ln_state = {"bk": None}

        def ln_stats_seq(src, b_stat, dep_src=(), rms_fuse=False):
            TT = 512
            NT = T // TT
            t_dma = None
            for it in range(NT):
                sl = slice(it * TT, (it + 1) * TT)
                xin = src[:, :, sl]
                xbf = SC[:, 0:NCH * TT].rearrange("p (c t) -> p c t", c=NCH)
                sq = SC[:, NCH * TT:2 * NCH * TT].rearrange(
                    "p (c t) -> p c t", c=NCH)
                dsrc = list(dep_src) if it == 0 else [p.tok("s_pe")]
                t_cast = dve(lambda e, xbf=xbf, xin=xin: e.tensor_copy(xbf, xin),
                             deps=dsrc)
                t_sq = act(sq, xbf, AF.Square, deps=[t_cast])
                tsum = None
                for c in range(NCH):
                    tsum = mm(bank(b_stat, 0, TT)[0:1, :], ONESC[:, 0:1],
                              xbf[:, c, :],
                              start=(c == 0), stop=(c == NCH - 1),
                              deps=[t_cast, p.tok("s_act")] if c == 0 else (),
                              inc="s_pe" if c == NCH - 1 else None)
                tsq = None
                for c in range(NCH):
                    tsq = mm(bank(b_stat + 1, 0, TT)[0:1, :], ONESC[:, 0:1],
                             sq[:, c, :],
                             start=(c == 0), stop=(c == NCH - 1),
                             deps=[t_sq] if c == 0 else (),
                             inc="s_pe" if c == NCH - 1 else None)
                ev1 = acopy(STG[0:1, 0:TT], bank(b_stat, 0, TT)[0:1, :],
                            deps=[tsum, t_dma])
                ev2 = acopy(STG[0:1, TT:2 * TT], bank(b_stat + 1, 0, TT)[0:1, :],
                            deps=[tsq])
                dma(STT[:, 4 * it:4 * it + 4],
                    STG[0:1, 0:TT].rearrange("o (p c) -> o p c", p=128),
                    slot="lnst", deps=[ev1])
                t_dma = dma(STT[:, 16 + 4 * it:16 + 4 * it + 4],
                            STG[0:1, TT:2 * TT].rearrange("o (p c) -> o p c", p=128),
                            slot="lnst", deps=[ev2])
            C = lambda j: STT[:, j:j + 16]
            m, v, r, tmp, mr, tmp2, r2 = (C(32), C(48), C(64), C(80), C(96),
                                          C(112), C(128))
            dep0 = [t_dma, _ln_state["bk"]]
            dve(lambda e: e.tensor_scalar(m, C(0), 1.0 / D, None, ALU.mult),
                deps=dep0)
            dve(lambda e: e.tensor_scalar(v, C(16), 1.0 / D, None, ALU.mult))
            vdrain()
            dve(lambda e: e.tensor_tensor(tmp, m, m, ALU.mult))
            vdrain()
            dve(lambda e: e.tensor_tensor(v, v, tmp, ALU.subtract))
            vdrain()
            dve(lambda e: e.tensor_scalar(v, v, LN_EPS, None, ALU.add))
            vdrain()
            rsqrt_dve(r, v, tmp)
            if rms_fuse:
                vdrain()
                dve(lambda e: e.tensor_tensor(tmp2, r, r, ALU.mult))
                vdrain()
                dve(lambda e: e.tensor_scalar(v, v, -LN_EPS, None, ALU.add))
                vdrain()
                dve(lambda e: e.tensor_tensor(tmp2, tmp2, v, ALU.mult))
                vdrain()
                dve(lambda e: e.tensor_scalar(tmp2, tmp2, RMS_EPS, None, ALU.add))
                vdrain()
                rsqrt_dve(r2, tmp2, tmp)
                vdrain()
                dve(lambda e: e.tensor_tensor(r, r, r2, ALU.mult))
            vdrain()
            dve(lambda e: e.tensor_tensor(mr, m, r, ALU.mult))
            vdrain()
            hilo(STTB[:, 0:16], STTB[:, 16:32], r)
            t_hl = hilo(STTB[:, 32:48], STTB[:, 48:64], mr)
            vdrain()
            tbk = None
            for (q, hl, rm) in ((0, 0, 0), (1, 1, 0), (2, 0, 1), (3, 1, 1)):
                for it in range(NT):
                    tbk = dma(ROWB[hl:hl + 1, rm, it * TT:(it + 1) * TT]
                              .rearrange("o (p c) -> o p c", p=128),
                              STTB[:, 16 * q + 4 * it:16 * q + 4 * it + 4],
                              slot="lnbk", deps=[t_hl])
            _ln_state["bk"] = tbk
            return tbk

        def ln_bcast(it, b_bc, dep_rows, dep_banks=()):
            TT = 512
            sl = slice(it * TT, (it + 1) * TT)
            t_rb = mm(bank(b_bc, 0, TT), ONESR[:], ROWB[:, 0, sl],
                      start=True, stop=True,
                      deps=[dep_rows] + list(dep_banks), inc="s_pe")
            t_mrb = mm(bank(b_bc + 1, 0, TT), ONESR[:], ROWB[:, 1, sl],
                       start=True, stop=True, inc="s_pe")
            return t_rb, t_mrb

        def ln_apply(src, it, b_bc, dst, t_rb, t_mrb, split_gp=True, dep_dst=(),
                     dep_war=(), scale_vec=None):
            TT = 512
            sl = slice(it * TT, (it + 1) * TT)
            rb = bank(b_bc, 0, TT)
            mrb = bank(b_bc + 1, 0, TT)
            toks = []
            st_r = ST[:, 0:TT]
            st_mr = ST[:, TT:2 * TT]
            ta = acopy(st_r, rb, deps=[t_rb] + list(dep_war))
            tb = acopy(st_mr, mrb, deps=[t_mrb])
            t_xread = None
            for c in range(NCH):
                eng_gp = split_gp and (c >= NCH // 2)
                xin = src[:, c, sl]
                zout = dst[:, c, sl]
                tmp = ST[:, (2 + (0 if eng_gp else 1)) * TT:
                          (3 + (0 if eng_gp else 1)) * TT]
                if eng_gp:
                    gp(lambda e, tmp=tmp, xin=xin: e.tensor_tensor(
                        tmp, xin, st_r, ALU.mult),
                       deps=[ta, tb, t_xread] + list(dep_dst))
                    p.emit("gpsimd", lambda e: e.drain())
                    if scale_vec is None:
                        t2 = gp(lambda e, zout=zout, tmp=tmp: e.tensor_tensor(
                            zout, tmp, st_mr, ALU.subtract))
                    else:
                        tg = gp(lambda e, tmp=tmp: e.tensor_tensor(
                            tmp, tmp, st_mr, ALU.subtract))
                        t2 = dve(lambda e, zout=zout, tmp=tmp, c=c:
                                 e.tensor_scalar(zout, tmp,
                                                 bias_ap(scale_vec, c), None,
                                                 ALU.mult), deps=[tg])
                        vdrain()
                        t_xread = t2
                    p.emit("gpsimd", lambda e: e.drain())
                else:
                    dve(lambda e, tmp=tmp, xin=xin: e.tensor_tensor(
                        tmp, xin, st_r, ALU.mult), deps=[ta, tb] + list(dep_dst))
                    vdrain()
                    if scale_vec is None:
                        t2 = dve(lambda e, zout=zout, tmp=tmp: e.tensor_tensor(
                            zout, tmp, st_mr, ALU.subtract))
                    else:
                        dve(lambda e, tmp=tmp: e.tensor_tensor(
                            tmp, tmp, st_mr, ALU.subtract))
                        vdrain()
                        t2 = dve(lambda e, zout=zout, tmp=tmp, c=c:
                                 e.tensor_scalar(zout, tmp,
                                                 bias_ap(scale_vec, c), None,
                                                 ALU.mult))
                    vdrain()
                toks.append(t2)
            if split_gp:
                return [toks[NCH // 2 - 1], toks[-1]]
            return [toks[-1]]

        def load_stream(seq, dep_st=()):
            evs = []
            for tt in range(T // 128):
                half = tt % 2
                tile = ST[:, half * 1024:(half + 1) * 1024]
                dep_war = [("s_pe", pe_after_tile[tt - 2])] if tt >= 2 else []
                if tt < 2:
                    dep_war = dep_war + [("s_pe", p.counts["s_pe"])] + list(dep_st)
                t_ld = dma(tile, x_hbm[seq, tt * 128:(tt + 1) * 128, :],
                           slot="xin%d" % half, deps=dep_war)
                for c in range(NCH):
                    dst = PS[:, 1024 * half + 128 * c: 1024 * half + 128 * c + 128]
                    deps = [t_ld, t_id] if c == 0 else ()
                    if tt >= 2 and c == 0:
                        deps = list(deps) + [evs[tt - 2]]
                    mmt(dst, tile[:, c * 128:(c + 1) * 128], deps=deps,
                        inc="s_pe" if c == NCH - 1 else None)
                pe_after_tile.append(p.counts["s_pe"])
                t_pe = p.tok("s_pe")
                dst = X[:, :, tt * 128:(tt + 1) * 128]
                src = PS[:, half * 1024: half * 1024 + 1024].rearrange(
                    "p (c t) -> p c t", c=NCH)
                evs.append(dve(lambda e, dst=dst, src=src: e.tensor_copy(dst, src),
                               deps=[t_pe]))
            return evs

        def store_stream(seq, deps):
            sts = []
            evs = []
            for tt in range(T // 128):
                half = tt % 2
                for c in range(NCH):
                    dst = PS[:, 2048 + 1024 * half + 128 * c:
                             2048 + 1024 * half + 128 * c + 128]
                    dps = list(deps) if c == 0 else []
                    if tt >= 2 and c == 0:
                        dps += [evs[tt - 2]]
                    mmt(dst, X[:, c, tt * 128:(tt + 1) * 128], deps=dps,
                        inc="s_pe" if c == NCH - 1 else None)
                t_pe = p.tok("s_pe")
                tile = ST[:, half * 1024:(half + 1) * 1024]
                src = PS[:, 2048 + half * 1024: 2048 + half * 1024 + 1024]
                dep_war = [sts[tt - 2]] if tt >= 2 else []
                evs.append(acopy(tile, src, deps=[t_pe] + dep_war))
                sts.append(store(y_hbm[seq, tt * 128:(tt + 1) * 128, :], tile,
                                 slot="st%d" % half, deps=[evs[-1]]))
            return sts

        def ffn0(seq, dep_in, ncycles=3, lnonly=False):
            TT = 512
            NT = T // TT
            phase0 = [("s_act", p.counts["s_act"]), ("s_pe", p.counts["s_pe"]),
                      ("s_dve", p.counts["s_dve"]), ("s_gp", p.counts["s_gp"])]
            wtoks = load_ffn0_weights(deps=phase0)
            vdrain()
            t_rows = ln_stats_seq(X, 4, dep_src=list(dep_in) + phase0)
            prev_apply = []
            for it in range(NT):
                t_rb, t_mrb = ln_bcast(it, 6, t_rows,
                                       dep_banks=prev_apply[-2:])
                prev_apply = ln_apply(X, it, 6, O, t_rb, t_mrb,
                                      dep_war=prev_apply[-2:])
            t_ln = (p.tok("s_dve"), p.tok("s_gp"))
            if lnonly:
                for it in range(NT):
                    sl = slice(it * TT, (it + 1) * TT)
                    dve(lambda e, sl=sl: e.tensor_copy(X[:, :, sl], O[:, :, sl]),
                        deps=[t_ln[0], t_ln[1]])
                    vdrain()
                return (p.tok("s_dve"), p.tok("s_gp"))
            g32 = ST[:, 0:NCH]
            t_g = dve(lambda e: e.tensor_reduce(
                g32, O[:, :, :], mybir.AxisListType.X, ALU.add),
                deps=[t_ln[0], t_ln[1]])
            vdrain()
            g16 = GCH[:, 0:NCH]
            t_g16 = dve(lambda e: e.tensor_scalar(g16, g32, 1.0 / T, None, ALU.mult))
            for j in range(4):
                for c in range(NCH):
                    mm(PS[:, 512 * 7 + j: 512 * 7 + j + 1],
                       wa_tile("gc_w1", c, j * 128, 128),
                       g16[:, c:c + 1],
                       start=(c == 0), stop=(c == NCH - 1),
                       deps=[t_g16, wtoks] if c == 0 else (),
                       inc="s_pe" if (j == 3 and c == NCH - 1) else None)
            t1 = p.tok("s_pe")
            for j in range(4):
                act(GCH[:, 8 + j:9 + j], PS[:, 512 * 7 + j:512 * 7 + j + 1],
                    AF.Gelu, bias=bias_ap("l0_gc_b1", j), deps=[t1])
            t_h1g = p.tok("s_act")
            for j2 in range(2):
                for c in range(4):
                    mm(PS[:, 512 * 7 + 16 + j2: 512 * 7 + 17 + j2],
                       wa_tile("gc_w2", c, j2 * 128, 128),
                       GCH[:, 8 + c:9 + c],
                       start=(c == 0), stop=(c == 3),
                       deps=[t_h1g] if c == 0 else (),
                       inc="s_pe" if (j2 == 1 and c == 3) else None)
            t2 = p.tok("s_pe")
            for j2 in range(2):
                act(GCH[:, 12 + j2:13 + j2],
                    PS[:, 512 * 7 + 16 + j2:512 * 7 + 17 + j2],
                    AF.Gelu, bias=bias_ap("l0_gc_b2", j2), deps=[t2])
            t_h2g = p.tok("s_act")
            for c in range(2):
                mm(PS[0:1, 512 * 7 + 24:512 * 7 + 25],
                   wa_tile("gc_w3", c, 0, 1), GCH[:, 12 + c:13 + c],
                   start=(c == 0), stop=(c == 1),
                   deps=[t_h2g] if c == 0 else (),
                   inc="s_pe" if c == 1 else None)
            t3 = p.tok("s_pe")
            tsm = dve(lambda e: e.tensor_scalar(SMALL[:, 8:9], SMALL[:, 1:2],
                                                0.5, None, ALU.mult),
                      deps=[t_params])
            t_tg = act(SMALL[:, 9:10], PS[0:1, 512 * 7 + 24:512 * 7 + 25],
                       AF.Tanh, bias=SMALL[:, 8:9], scale=0.5, deps=[t3, tsm])
            dve(lambda e: e.tensor_tensor(SMALL[:, 10:11], SMALL[:, 2:3],
                                          SMALL[:, 9:10], ALU.mult), deps=[t_tg])
            vdrain()
            dve(lambda e: e.tensor_scalar(SMALL[:, 10:11], SMALL[:, 10:11],
                                          0.7, 0.1, ALU.mult, ALU.add))
            t_c0 = dve(lambda e: e.tensor_scalar(SMALL[:, 11:12], SMALL[:, 2:3],
                                                 0.3, None, ALU.mult))
            vdrain()
            t_b4 = dve(lambda e: e.tensor_scalar(SMALL[:, 12:13], SMALL[:, 0:1],
                                                 0.5, None, ALU.mult))

            ev_prev = {}
            for cyc in range(ncycles):
                for it in range(NT):
                    sl = slice(it * TT, (it + 1) * TT)
                    obf = SC[:, 0:NCH * TT].rearrange("p (c t) -> p c t", c=NCH)
                    dep_o = [("s_dve", p.counts["s_dve"]),
                             ("s_gp", p.counts["s_gp"]),
                             ("s_pe", p.counts["s_pe"])]
                    t_obf = dve(lambda e, obf=obf, sl=sl: e.tensor_copy(
                        obf, O[:, :, sl]), deps=dep_o)
                    h1 = SC[:, NCH * TT:2 * NCH * TT].rearrange(
                        "p (c t) -> p c t", c=NCH)
                    for j in range(NCH):
                        bi = j % 2
                        dep0 = [t_obf, wtoks] if j == 0 else []
                        if ev_prev.get(("cn1", bi)) is not None:
                            dep0.append(ev_prev[("cn1", bi)])
                        for c in range(NCH):
                            mm(bank(bi, 0, TT), wa_tile("cn_w1", c, j * 128, 128),
                               obf[:, c, :], start=(c == 0), stop=(c == NCH - 1),
                               deps=dep0 if c == 0 else (),
                               inc="s_pe" if c == NCH - 1 else None)
                        ev = act(h1[:, j, :], bank(bi, 0, TT), AF.Gelu,
                                 bias=bias_ap("l0_cn_b1", j),
                                 deps=[p.tok("s_pe")])
                        ev_prev[("cn1", bi)] = ev
                    h2 = SC[:, 0:4 * TT].rearrange("p (c t) -> p c t", c=4)
                    for j in range(4):
                        bi = 2 + (j % 2)
                        dep0 = []
                        if j == 0:
                            dep0 += [ev_prev[("cn1", 0)], ev_prev[("cn1", 1)]]
                        if ev_prev.get(("b", bi)) is not None:
                            dep0.append(ev_prev[("b", bi)])
                        for c in range(NCH):
                            mm(bank(bi, 0, TT), wa_tile("cn_w2", c, j * 128, 128),
                               h1[:, c, :], start=(c == 0), stop=(c == NCH - 1),
                               deps=dep0 if c == 0 else (),
                               inc="s_pe" if c == NCH - 1 else None)
                        ev = act(h2[:, j, :], bank(bi, 0, TT), AF.Gelu,
                                 bias=bias_ap("l0_cn_b2", j),
                                 deps=[p.tok("s_pe")])
                        ev_prev[("b", bi)] = ev
                    h3 = SC[:, 4 * TT:6 * TT].rearrange("p (c t) -> p c t", c=2)
                    for j in range(2):
                        bi = 2 + (j % 2)
                        dep0 = []
                        if j == 0:
                            dep0 += [ev_prev[("b", 2)], ev_prev[("b", 3)]]
                        if ev_prev.get(("b", bi)) is not None:
                            dep0.append(ev_prev[("b", bi)])
                        for c in range(4):
                            mm(bank(bi, 0, TT), wa_tile("cn_w3", c, j * 128, 128),
                               h2[:, c, :], start=(c == 0), stop=(c == 3),
                               deps=dep0 if c == 0 else (),
                               inc="s_pe" if c == 3 else None)
                        ev = act(h3[:, j, :], bank(bi, 0, TT), AF.Gelu,
                                 bias=bias_ap("l0_cn_b3", j),
                                 deps=[p.tok("s_pe")])
                        ev_prev[("b", bi)] = ev
                    dep0 = [ev_prev[("b", 2)], ev_prev[("b", 3)]]
                    for c in range(2):
                        mm(bank(2, 0, TT)[0:1, :],
                           wa_tile("cn_w4", c, 0, 1), h3[:, c, :],
                           start=(c == 0), stop=(c == 1),
                           deps=dep0 if c == 0 else (),
                           inc="s_pe" if c == 1 else None)
                    t_cn4 = p.tok("s_pe")
                    crow = STG[0:1, 0:TT]
                    t_tf = act(crow, bank(2, 0, TT)[0:1, :], AF.Tanh,
                               bias=SMALL[:, 12:13], scale=0.5,
                               deps=[t_cn4, t_b4, t_c0])
                    ev_prev[("b", 2)] = t_tf
                    t_c = dve(lambda e: e.tensor_scalar(
                        crow, crow, SMALL[:, 11:12], SMALL[:, 10:11],
                        ALU.mult, ALU.add), deps=[t_tf])
                    vdrain()
                    t_hilo = hilo(CROWB[0:1, 0:TT], CROWB[0:1, TT:2 * TT], crow)
                    dep_cb = [t_hilo]
                    if ev_prev.get(("cb",)) is not None:
                        dep_cb.append(ev_prev[("cb",)])
                    mm(bank(4, 0, TT), ONESR[0:1, :], CROWB[0:1, 0:TT],
                       start=True, stop=False, deps=dep_cb)
                    t_cb = mm(bank(4, 0, TT), ONESR[0:1, :], CROWB[0:1, TT:2 * TT],
                              start=False, stop=True, inc="s_pe")
                    cb_st = ST[:, 0:TT]
                    t_cbst = acopy(cb_st, bank(4, 0, TT),
                                   deps=[t_cb, t_ln[0], t_ln[1],
                                         ev_prev.get(("tw_dve",)),
                                         ev_prev.get(("tw_gp",))])
                    ev_prev[("cb",)] = t_cbst
                    tmp_a = ST[:, 1 * TT:2 * TT]
                    tmp_b = ST[:, 2 * TT:3 * TT]
                    tmp_g = ST[:, 3 * TT:4 * TT]
                    tmp_g2 = SC[:, 6 * TT:8 * TT].bitcast(F32)
                    def _sgn(j):
                        return ALU.add if j in (0, 1, 6, 7) else ALU.subtract
                    for g in (0, 1, 3):
                        j1, j2 = g, g + 4
                        dve(lambda e, j2=j2, sl=sl: e.tensor_tensor(
                            tmp_a, O[:, j2, sl], cb_st, ALU.mult),
                            deps=[t_cbst, t_obf])
                        dve(lambda e, j1=j1, sl=sl: e.tensor_tensor(
                            tmp_b, O[:, j1, sl], cb_st, ALU.mult))
                        vdrain()
                        dve(lambda e, j1=j1, sl=sl: e.tensor_tensor(
                            O[:, j1, sl], O[:, j1, sl], tmp_a, _sgn(j1)))
                        dve(lambda e, j2=j2, sl=sl: e.tensor_tensor(
                            O[:, j2, sl], O[:, j2, sl], tmp_b, _sgn(j2)))
                        vdrain()
                    j1, j2 = 2, 6
                    gp(lambda e, j2=j2, sl=sl: e.tensor_tensor(
                        tmp_g, O[:, j2, sl], cb_st, ALU.mult),
                        deps=[t_cbst, t_obf, ev_prev.get(("b", 2)),
                              ev_prev.get(("b", 3))])
                    gp(lambda e, j1=j1, sl=sl: e.tensor_tensor(
                        tmp_g2, O[:, j1, sl], cb_st, ALU.mult))
                    p.emit("gpsimd", lambda e: e.drain())
                    gp(lambda e, j1=j1, sl=sl: e.tensor_tensor(
                        O[:, j1, sl], O[:, j1, sl], tmp_g, _sgn(j1)))
                    gp(lambda e, j2=j2, sl=sl: e.tensor_tensor(
                        O[:, j2, sl], O[:, j2, sl], tmp_g2, _sgn(j2)))
                    p.emit("gpsimd", lambda e: e.drain())
                    ev_prev[("tw_dve",)] = p.tok("s_dve")
                    ev_prev[("tw_gp",)] = p.tok("s_gp")
            for it in range(NT):
                sl = slice(it * TT, (it + 1) * TT)
                dep_o = [("s_dve", p.counts["s_dve"]), ("s_gp", p.counts["s_gp"])]
                for c in range(NCH):
                    if c % 2 == 0:
                        dve(lambda e, c=c, sl=sl: e.scalar_tensor_tensor(
                            X[:, c, sl], O[:, c, sl], 0.5, X[:, c, sl],
                            ALU.mult, ALU.add), deps=dep_o if c == 0 else ())
                    else:
                        gtmp = ST[:, 0:TT]
                        gp(lambda e, c=c, sl=sl, gtmp=gtmp: e.tensor_tensor(
                            gtmp, O[:, c, sl], HALFT[:], ALU.mult),
                           deps=dep_o if c == 1 else ())
                        p.emit("gpsimd", lambda e: e.drain())
                        gp(lambda e, c=c, sl=sl, gtmp=gtmp: e.tensor_tensor(
                            X[:, c, sl], X[:, c, sl], gtmp, ALU.add))
                        p.emit("gpsimd", lambda e: e.drain())
            return (p.tok("s_dve"), p.tok("s_gp"))

        W_A0 = {"v": 0, "wq": 16640, "wo": 0}

        def attn0(seq, dep_in):
            TT = 512
            NT = T // TT
            phase0 = [("s_act", p.counts["s_act"]), ("s_pe", p.counts["s_pe"]),
                      ("s_dve", p.counts["s_dve"]), ("s_gp", p.counts["s_gp"])]
            zb = Ob[:, :, 0:T]
            ab = Ob[:, :, T:2 * T]
            vdrain()
            t_rows = ln_stats_seq(X, 4, dep_src=list(dep_in) + phase0)
            prev_apply = []
            for it in range(NT):
                t_rb, t_mrb = ln_bcast(it, 6, t_rows, dep_banks=prev_apply[-2:])
                prev_apply = ln_apply(X, it, 6, zb, t_rb, t_mrb,
                                      dep_war=prev_apply[-2:])
            t_ln = (p.tok("s_dve"), p.tok("s_gp"))

            ev_v = {}
            for half in range(2):
                wvh = SC[:, 0:NCH * TT].rearrange("p (c w) -> p c w", c=NCH)
                t_wv = dma(wvh, dram["l0_wv"][:, 512 * half:512 * (half + 1)]
                           .rearrange("(c p) w -> p c w", p=128),
                           slot="wv0", queue="gp",
                           deps=[t_ln[0], t_ln[1], ("s_pe", p.counts["s_pe"])])
                for tt in range(16):
                    bk = (tt % 4)
                    dep0 = [t_wv]
                    if ev_v.get(bk) is not None:
                        dep0.append(ev_v[bk])
                    for c in range(NCH):
                        mm(bank(bk, 0, TT),
                           zb[:, c, tt * 128:(tt + 1) * 128],
                           wvh[:, c, :],
                           start=(c == 0), stop=(c == NCH - 1),
                           deps=dep0 if c == 0 else (),
                           inc="s_pe" if c == NCH - 1 else None)
                    dst = WA[:, 1040 * tt + 65 * 8 * half:
                             1040 * tt + 65 * 8 * half + 65 * 8]
                    dst = dst.rearrange("p (h d) -> p h d", h=8)[:, :, 0:64]
                    src = bank(bk, 0, TT).rearrange("p (h d) -> p h d", h=8)
                    ev_v[bk] = acopy(dst, src, deps=[p.tok("s_pe")])
            vap = WA[:, 64:64 + 256 * 65].rearrange(
                "p (n o) -> p n o", n=256, o=65)[:, :, 0:1]
            t_vo = dve(lambda e: e.memset(vap, 1.0),
                       deps=[ev_v[0], ev_v[1], ev_v[2], ev_v[3]])
            t_vo2 = t_vo
            vdrain()

            t_zevac = {}
            t_zdma = None
            t_yevac = None
            t_zev32 = None
            t_tydma = None
            for pj in range(8):
                dep_pair = [("s_act", p.counts["s_act"]),
                            ("s_pe", p.counts["s_pe"])]
                wqp = WA[:, W_A0["wq"]:W_A0["wq"] + 1024].rearrange(
                    "p (c w) -> p c w", c=NCH)
                q2 = SC[:, 0:2048]
                k2 = SC[:, 2048:4096]
                t_wq = dma(wqp, dram["l0_wq"][:, 128 * pj:128 * (pj + 1)]
                           .rearrange("(c p) w -> p c w", p=128),
                           slot="wqp", queue="gp", deps=dep_pair)
                for qt in range(NT):
                    bk = qt % 2
                    dep0 = [t_wq] + dep_pair
                    if t_zevac.get(("q", bk)) is not None:
                        dep0.append(t_zevac[("q", bk)])
                    for c in range(NCH):
                        mm(bank(bk, 0, TT), wqp[:, c, :],
                           zb[:, c, qt * TT:(qt + 1) * TT],
                           start=(c == 0), stop=(c == NCH - 1),
                           deps=dep0 if c == 0 else (),
                           inc="s_pe" if c == NCH - 1 else None)
                    t_zevac[("q", bk)] = acopy(
                        q2[:, qt * TT:(qt + 1) * TT], bank(bk, 0, TT),
                        deps=[p.tok("s_pe")])
                t_wk = dma(wqp, dram["l0_wk"][:, 128 * pj:128 * (pj + 1)]
                           .rearrange("(c p) w -> p c w", p=128),
                           slot="wqp", queue="gp", deps=[p.tok("s_pe")])
                for qt in range(NT):
                    bk = 2 + qt % 2
                    dep0 = [t_wk]
                    if t_zevac.get(("k", bk)) is not None:
                        dep0.append(t_zevac[("k", bk)])
                    for c in range(NCH):
                        mm(bank(bk, 0, TT), wqp[:, c, :],
                           zb[:, c, qt * TT:(qt + 1) * TT],
                           start=(c == 0), stop=(c == NCH - 1),
                           deps=dep0 if c == 0 else (),
                           inc="s_pe" if c == NCH - 1 else None)
                    t_zevac[("k", bk)] = acopy(
                        k2[:, qt * TT:(qt + 1) * TT], bank(bk, 0, TT),
                        deps=[p.tok("s_pe")])
                t_qk = p.tok("s_act")

                psq = SC[:, 4096:6144]
                t_red = None
                for wi, buf in ((0, q2), (1, k2)):
                    t_sq = act(psq, buf, AF.Square,
                               deps=[t_qk, ("s_pe", p.counts["s_pe"])])
                    for qt in range(NT):
                        mm(bank(0, 0, TT)[0:2, :], ONESH[:],
                           psq[:, qt * TT:(qt + 1) * TT],
                           start=True, stop=True,
                           deps=[t_sq] + ([t_red] if t_red else []),
                           inc="s_pe")
                        t_red = dve(lambda e, qt=qt: e.reduce_max(
                            BSC[0:2, 0:1] if qt == 0 else BSC[0:2, 1:2],
                            bank(0, 0, TT)[0:2, :],
                            mybir.AxisListType.X), deps=[p.tok("s_pe")])
                        vdrain()
                        if qt > 0:
                            t_red = dve(lambda e: e.tensor_tensor(
                                BSC[0:2, 0:1], BSC[0:2, 0:1], BSC[0:2, 1:2],
                                ALU.max))
                            vdrain()
                    dve(lambda e, wi=wi: e.tensor_copy(
                        BSC[0:2, 2 + wi:3 + wi], BSC[0:2, 0:1]))
                    vdrain()
                dve(lambda e: e.tensor_tensor(BSC[0:2, 4:5], BSC[0:2, 2:3],
                                              BSC[0:2, 3:4], ALU.mult))
                vdrain()
                rsqrt_dve(BSC[0:2, 5:6], BSC[0:2, 4:5], BSC[0:2, 6:7])
                vdrain()
                dve(lambda e: e.tensor_tensor(BSC[0:2, 5:6], BSC[0:2, 5:6],
                                              BSC[0:2, 4:5], ALU.mult))
                vdrain()
                t_bnd = dve(lambda e: e.tensor_scalar(
                    BSCB[0:2, 0:1], BSC[0:2, 5:6], -1.0 / 32.0, None, ALU.mult))
                mm(bank(0, 0, 2)[0:1, :], BSCB[0:2, 0:1], ID2[:],
                   start=True, stop=True, deps=[t_bnd, t_consts], inc="s_pe")
                t_br = acopy(BROW[0:1, 0:2], bank(0, 0, 2)[0:1, :],
                             deps=[p.tok("s_pe")])
                mm(bank(0, 0, 2), ONESR[0:1, :], BROW[0:1, 0:2],
                   start=True, stop=True, deps=[t_br], inc="s_pe")
                t_bb = acopy(BNDS[:, 0:2], bank(0, 0, 2), deps=[p.tok("s_pe")])

                pb = SC[:, 4096:6144]
                for hh in range(2):
                    h = 2 * pj + hh
                    yps = PS[:, 2048:4096]
                    t_exp = {}
                    t_pv = None
                    t_scA = t_scB = None
                    for sc in range(16):
                        for qt in range(NT):
                            dep0 = []
                            if qt == 0:
                                dep0 = [t_bb, t_qk]
                                if t_exp.get(0) is not None:
                                    dep0.append(t_exp[0])
                            if qt == 2 and t_exp.get(1) is not None:
                                dep0 = [t_exp[1]]
                            mm(bank(qt, 0, TT),
                               k2[64 * hh:64 * (hh + 1),
                                  sc * 128:(sc + 1) * 128],
                               q2[64 * hh:64 * (hh + 1),
                                  qt * TT:(qt + 1) * TT],
                               start=True, stop=True, deps=dep0,
                               inc="s_pe" if qt in (1, 3) else None)
                            if qt == 1:
                                t_scA = p.tok("s_pe")
                            if qt == 3:
                                t_scB = p.tok("s_pe")
                        dep_pv = [t_pv] if t_pv is not None else []
                        t_exp[0] = act(pb[:, 0:1024], PS[:, 0:1024], AF.Exp,
                                       bias=BNDS[:, hh:hh + 1], scale=1 / 32.0,
                                       deps=[t_scA] + dep_pv)
                        t_exp[1] = act(pb[:, 1024:2048], PS[:, 1024:2048],
                                       AF.Exp, bias=BNDS[:, hh:hh + 1],
                                       scale=1 / 32.0, deps=[t_scB])
                        for qt in range(NT):
                            dep0 = ()
                            if qt == 0:
                                dep0 = [t_exp[0], t_vo, t_vo2]
                                if sc == 0 and t_yevac is not None:
                                    dep0 = dep0 + [t_yevac, t_zev32]
                                if sc == 0 and t_zdma is not None:
                                    dep0 = dep0 + [t_zdma]
                            elif qt == 2:
                                dep0 = [t_exp[1]]
                            mm(yps[0:65, qt * TT:(qt + 1) * TT],
                               WA[:, 1040 * sc + 65 * h:
                                  1040 * sc + 65 * h + 65],
                               pb[:, qt * TT:(qt + 1) * TT],
                               start=(sc == 0), stop=(sc == 15),
                               deps=dep0,
                               inc="s_pe" if qt == 3 else None)
                        t_pv = p.tok("s_pe")
                    ych = h // 2
                    ypart = 64 * (h % 2)
                    if ypart == 0:
                        t_yevac = acopy(ab[:, ych, :][0:64, :], yps[0:64, :],
                                        deps=[t_pv])
                    else:
                        ty = SC[:, 6144:8192][0:64, :]
                        t_ty = acopy(ty, yps[0:64, :], deps=[t_pv, t_tydma])
                        t_tydma = dma(ab[:, ych, :][64:128, :], ty,
                                      slot="tydma", deps=[t_ty])
                        t_yevac = t_ty
                    t_zev32 = acopy(ST[64:65, :], yps[64:65, :],
                                    deps=[t_pv, t_zdma])
                    t_zdma = dma(ZT[:, 0, 16 * h:16 * (h + 1)],
                                 ST[64:65, :].rearrange("o (q c) -> o q c",
                                                        q=128),
                                 slot="zdma", deps=[t_zev32])

            t_zr = dve(lambda e: e.reciprocal(ZT[:, 1, :], ZT[:, 0, :]),
                       deps=[t_zdma, t_tydma, p.tok(_slot_sem("zbk"))])
            vdrain()
            t_hl = hilo(ZTB[:, 0, :], ZTB[:, 1, :], ZT[:, 1, :])
            vdrain()
            t_zb = None
            for h in range(HEADS):
                for hl in range(2):
                    t_zb = dma(ZRH[h + 16 * hl:h + 16 * hl + 1, :]
                               .rearrange("o (q c) -> o q c", q=128),
                               ZTB[:, hl, 16 * h:16 * (h + 1)],
                               slot="zbk", deps=[t_hl])
            t_wo = dma(WA[:, 0:8192].rearrange("p (c w) -> p c w", c=NCH),
                       dram["l0_wo"][:].rearrange("(c p) w -> p c w", p=128),
                       slot="wo0", queue="gp", deps=[("s_pe", p.counts["s_pe"])])
            t_norm = {}
            for qt in range(NT):
                for och in range(NCH):
                    bk = och % 2
                    dep0 = [t_zb, t_consts]
                    if t_norm.get(bk) is not None:
                        dep0.append(t_norm[bk])
                    mm(bank(bk, 0, TT), PATZ[:, och, :],
                       ZRH[:, qt * TT:(qt + 1) * TT],
                       start=True, stop=True, deps=dep0, inc="s_pe")
                    t_norm[bk] = dve(
                        lambda e, och=och, qt=qt, bk=bk: e.tensor_tensor(
                            ab[:, och, qt * TT:(qt + 1) * TT],
                            ab[:, och, qt * TT:(qt + 1) * TT],
                            bank(bk, 0, TT), ALU.mult),
                        deps=[p.tok("s_pe")])
                    vdrain()
            t_abn = (p.tok("s_dve"),)

            wo = WA[:, 0:8192].rearrange("p (c w) -> p c w", c=NCH)
            ev_ep = {}
            for qt in range(NT):
                dep_b = [t_rows, ("s_act", p.counts["s_act"])]
                mm(bank(6, 0, TT), ONESR[:], ROWB[:, 0, qt * TT:(qt + 1) * TT],
                   start=True, stop=False, deps=dep_b)
                t_b6 = mm(bank(6, 0, TT), ONESR[0:1, :], ONESBF[0:1, :],
                          start=False, stop=True, inc="s_pe")
                t_b7 = mm(bank(7, 0, TT), ONESR[:],
                          ROWB[:, 1, qt * TT:(qt + 1) * TT],
                          start=True, stop=True, inc="s_pe")
                t_c6 = acopy(ST[:, 0:TT], bank(6, 0, TT), deps=[t_b6])
                t_c7 = acopy(ST[:, TT:2 * TT], bank(7, 0, TT), deps=[t_b7])
                for och in range(NCH):
                    bk = 2 + och % 2
                    dep0 = [t_abn[0], t_wo]
                    if ev_ep.get(bk) is not None:
                        dep0.append(ev_ep[bk])
                    for c in range(NCH):
                        mm(bank(bk, 0, TT), wo[:, c, 128 * och:128 * (och + 1)],
                           ab[:, c, qt * TT:(qt + 1) * TT],
                           start=(c == 0), stop=(c == NCH - 1),
                           deps=dep0 if c == 0 else (),
                           inc="s_pe" if c == NCH - 1 else None)
                    eng_gp = och >= 4
                    tpr = ST[:, (2 + (1 if eng_gp else 0)) * TT:
                             (3 + (1 if eng_gp else 0)) * TT]
                    t_t = act(tpr, bank(bk, 0, TT), AF.Identity,
                              bias=bias_ap("l0_bo", och),
                              deps=[p.tok("s_pe"),
                                    ev_ep.get("tg" if eng_gp else "td")])
                    xsl = X[:, och, qt * TT:(qt + 1) * TT]
                    if eng_gp:
                        gp(lambda e, tpr=tpr: e.tensor_tensor(
                            tpr, tpr, ST[:, TT:2 * TT], ALU.subtract),
                           deps=[t_t, t_c7, t_c6])
                        p.emit("gpsimd", lambda e: e.drain())
                        gp(lambda e, xsl=xsl: e.tensor_tensor(
                            xsl, xsl, ST[:, 0:TT], ALU.mult))
                        p.emit("gpsimd", lambda e: e.drain())
                        ev_ep[bk] = gp(lambda e, xsl=xsl, tpr=tpr: e.tensor_tensor(
                            xsl, xsl, tpr, ALU.add))
                        ev_ep["tg"] = ev_ep[bk]
                        p.emit("gpsimd", lambda e: e.drain())
                    else:
                        dve(lambda e, tpr=tpr, bk=bk: e.tensor_tensor(
                            tpr, tpr, bank(7, 0, TT), ALU.subtract),
                            deps=[t_t, t_b7])
                        vdrain()
                        dve(lambda e, xsl=xsl: e.tensor_tensor(
                            xsl, xsl, bank(6, 0, TT), ALU.mult), deps=[t_b6])
                        vdrain()
                        ev_ep[bk] = dve(lambda e, xsl=xsl, tpr=tpr: e.tensor_tensor(
                            xsl, xsl, tpr, ALU.add))
                        ev_ep["td"] = ev_ep[bk]
                        vdrain()
                p.wait("tensor", [ev_ep.get(2), ev_ep.get(3),
                                  ("s_gp", p.counts["s_gp"]),
                                  ("s_dve", p.counts["s_dve"])])
            return (p.tok("s_dve"), p.tok("s_gp"))

        def ffn1(seq, dep_in):
            TT = 512
            NT = T // TT
            phase0 = [("s_act", p.counts["s_act"]), ("s_pe", p.counts["s_pe"]),
                      ("s_dve", p.counts["s_dve"]), ("s_gp", p.counts["s_gp"])]
            zbw = WA[:, 0:NCH * T].rearrange("p (c t) -> p c t", c=NCH)
            w1h = Ob[:, :, 0:T]
            w2h = Ob[:, :, T:2 * T].rearrange(
                "p c (k w) -> p c k w", k=2)

            def w2t(kc, j):
                return w2h[:, kc // 2, kc % 2, 128 * j:128 * (j + 1)]
            vdrain()
            t_rows = ln_stats_seq(X, 4, dep_src=list(dep_in) + phase0)
            prev_apply = []
            for it in range(NT):
                t_rb, t_mrb = ln_bcast(it, 6, t_rows, dep_banks=prev_apply[-2:])
                prev_apply = ln_apply(X, it, 6, zbw, t_rb, t_mrb,
                                      dep_war=prev_apply[-2:])
            t_ln = (p.tok("s_dve"), p.tok("s_gp"))
            h1sb = SC[:, 0:8192].rearrange("p (c t) -> p c t", c=16)
            ev_h1 = {}
            ev_out = {}
            for half in range(2):
                dep_w = [t_ln[0], t_ln[1], ("s_pe", p.counts["s_pe"])]
                t_w1 = dma(w1h, dram["l1_w1"][:, 2048 * half:2048 * (half + 1)]
                           .rearrange("(c p) w -> p c w", p=128),
                           slot="w1h", queue="gp", deps=dep_w)
                src2 = dram["l1_w2"][2048 * half:2048 * (half + 1), :].rearrange(
                    "(c k p) w -> p c k w", p=128, k=2)
                dma(w2h[:, :, 0, :], src2[:, :, 0, :],
                    slot="w2h", queue="gp", deps=dep_w)
                t_w2 = dma(w2h[:, :, 1, :], src2[:, :, 1, :],
                           slot="w2h", queue="gp", deps=dep_w)
                for it in range(NT):
                    sl = slice(it * TT, (it + 1) * TT)
                    for j in range(16):
                        bk = j % 2
                        dep0 = [t_w1]
                        if ev_h1.get(bk) is not None:
                            dep0.append(ev_h1[bk])
                        for c in range(NCH):
                            mm(bank(bk, 0, TT),
                               w1h[:, c, 128 * j:128 * (j + 1)],
                               zbw[:, c, sl],
                               start=(c == 0), stop=(c == NCH - 1),
                               deps=dep0 if c == 0 else (),
                               inc="s_pe" if c == NCH - 1 else None)
                        ev_h1[bk] = act(h1sb[:, j, :], bank(bk, 0, TT),
                                        AF.Relu,
                                        bias=bias_ap("l1_b1", 16 * half + j),
                                        deps=[p.tok("s_pe")])
                    t_h1 = p.tok("s_act")
                    for j in range(NCH):
                        bk = 2 + j % 2
                        dep0 = [t_w2, t_h1]
                        if ev_out.get(bk) is not None:
                            dep0.append(ev_out[bk])
                        for c in range(16):
                            mm(bank(bk, 0, TT),
                               w2t(c, j),
                               h1sb[:, c, :],
                               start=(c == 0), stop=(c == 15),
                               deps=dep0 if c == 0 else (),
                               inc="s_pe" if c == 15 else None)
                        eng_gp = j % 2 == 1
                        tpr = ST[:, (0 + (1 if eng_gp else 0)) * TT:
                                 (1 + (1 if eng_gp else 0)) * TT]
                        if half == 0:
                            t_t = act(tpr, bank(bk, 0, TT), AF.Identity,
                                      bias=BIAS[:, 76 + j:77 + j], scale=0.5,
                                      deps=[p.tok("s_pe"),
                                            ev_out.get("g" if eng_gp else "d"),
                                            t_hb2])
                        else:
                            t_t = act(tpr, bank(bk, 0, TT), AF.Identity,
                                      scale=0.5,
                                      deps=[p.tok("s_pe"),
                                            ev_out.get("g" if eng_gp else "d")])
                        xsl = X[:, j, sl]
                        if eng_gp:
                            tok = gp(lambda e, xsl=xsl, tpr=tpr: e.tensor_tensor(
                                xsl, xsl, tpr, ALU.add), deps=[t_t])
                            p.emit("gpsimd", lambda e: e.drain())
                            ev_out["g"] = tok
                        else:
                            tok = dve(lambda e, xsl=xsl, tpr=tpr: e.tensor_tensor(
                                xsl, xsl, tpr, ALU.add), deps=[t_t])
                            vdrain()
                            ev_out["d"] = tok
                        ev_out[bk] = t_t
            return (p.tok("s_dve"), p.tok("s_gp"))

        def attn1(seq, dep_in, phases=("w", "l")):
            TT = 512
            NT = T // TT
            phase0 = [("s_act", p.counts["s_act"]), ("s_pe", p.counts["s_pe"]),
                      ("s_dve", p.counts["s_dve"]), ("s_gp", p.counts["s_gp"])]
            hb = Ob[:, :, 0:T]
            ab = Ob[:, :, T:2 * T]
            vdrain()
            t_rows = ln_stats_seq(X, 4, dep_src=list(dep_in) + phase0,
                                  rms_fuse=True)
            prev_apply = []
            for it in range(NT):
                t_rb, t_mrb = ln_bcast(it, 6, t_rows, dep_banks=prev_apply[-2:])
                prev_apply = ln_apply(X, it, 6, hb, t_rb, t_mrb,
                                      dep_war=prev_apply[-2:],
                                      scale_vec="l1_normw")
            t_ln = (p.tok("s_dve"), p.tok("s_gp"))

            ev_v = {}
            for half in range(2):
                wvh = SC[:, 0:NCH * TT].rearrange("p (c w) -> p c w", c=NCH)
                t_wv = dma(wvh, dram["l1_wv"][:, 512 * half:512 * (half + 1)]
                           .rearrange("(c p) w -> p c w", p=128),
                           slot="wv0", queue="gp",
                           deps=[t_ln[0], t_ln[1], ("s_pe", p.counts["s_pe"])])
                for tt in range(16):
                    bk = (tt % 4)
                    dep0 = [t_wv]
                    if ev_v.get(bk) is not None:
                        dep0.append(ev_v[bk])
                    for c in range(NCH):
                        mm(bank(bk, 0, TT),
                           hb[:, c, tt * 128:(tt + 1) * 128],
                           wvh[:, c, :],
                           start=(c == 0), stop=(c == NCH - 1),
                           deps=dep0 if c == 0 else (),
                           inc="s_pe" if c == NCH - 1 else None)
                    dst = WA[:, 1040 * tt + 65 * 8 * half:
                             1040 * tt + 65 * 8 * half + 65 * 8]
                    dst = dst.rearrange("p (h d) -> p h d", h=8)[:, :, 0:64]
                    src = bank(bk, 0, TT).rearrange("p (h d) -> p h d", h=8)
                    ev_v[bk] = acopy(dst, src, deps=[p.tok("s_pe")])
            vap = WA[:, 64:64 + 256 * 65].rearrange(
                "p (n o) -> p n o", n=256, o=65)[:, :, 0:1]
            t_vo = dve(lambda e: e.memset(vap, 1.0),
                       deps=[ev_v[0], ev_v[1], ev_v[2], ev_v[3]])
            vdrain()
            if "v" in phases:
                return (p.tok("s_dve"), p.tok("s_gp"))

            def vplus(sc, h):
                return WA[:, 1040 * sc + 65 * h:1040 * sc + 65 * h + 65]

            skip_w = "w" not in phases
            ZRHW = SC[0:32, 4096:6144]
            t_zevac = {}
            t_zdma = None
            t_tydma = None
            t_yevac = None
            t_zev32 = None
            for pj in (range(8) if not skip_w else ()):
                dep_pair = [("s_act", p.counts["s_act"]),
                            ("s_pe", p.counts["s_pe"])]
                wqp = WA[:, W_A0["wq"]:W_A0["wq"] + 1024].rearrange(
                    "p (c w) -> p c w", c=NCH)
                q2 = SC[:, 0:2048]
                k2 = SC[:, 2048:4096]
                t_wq = dma(wqp, dram["l1_wq"][:, 128 * pj:128 * (pj + 1)]
                           .rearrange("(c p) w -> p c w", p=128),
                           slot="wqp", queue="gp", deps=dep_pair)
                for qt in range(NT):
                    bk = qt % 2
                    dep0 = [t_wq] + dep_pair
                    if t_zevac.get(("q", bk)) is not None:
                        dep0.append(t_zevac[("q", bk)])
                    for c in range(NCH):
                        mm(bank(bk, 0, TT), wqp[:, c, :],
                           hb[:, c, qt * TT:(qt + 1) * TT],
                           start=(c == 0), stop=(c == NCH - 1),
                           deps=dep0 if c == 0 else (),
                           inc="s_pe" if c == NCH - 1 else None)
                    t_zevac[("q", bk)] = acopy(
                        q2[:, qt * TT:(qt + 1) * TT], bank(bk, 0, TT),
                        deps=[p.tok("s_pe")])
                t_wk = dma(wqp, dram["l1_wk"][:, 128 * pj:128 * (pj + 1)]
                           .rearrange("(c p) w -> p c w", p=128),
                           slot="wqp", queue="gp", deps=[p.tok("s_pe")])
                for qt in range(NT):
                    bk = 2 + qt % 2
                    dep0 = [t_wk]
                    if t_zevac.get(("k", bk)) is not None:
                        dep0.append(t_zevac[("k", bk)])
                    for c in range(NCH):
                        mm(bank(bk, 0, TT), wqp[:, c, :],
                           hb[:, c, qt * TT:(qt + 1) * TT],
                           start=(c == 0), stop=(c == NCH - 1),
                           deps=dep0 if c == 0 else (),
                           inc="s_pe" if c == NCH - 1 else None)
                    t_zevac[("k", bk)] = acopy(
                        k2[:, qt * TT:(qt + 1) * TT], bank(bk, 0, TT),
                        deps=[p.tok("s_pe")])
                t_qk = p.tok("s_act")

                psq = SC[:, 4096:6144]
                t_red = None
                for wi, buf in ((0, q2), (1, k2)):
                    t_sq = act(psq, buf, AF.Square,
                               deps=[t_qk, ("s_pe", p.counts["s_pe"])])
                    for qt in range(NT):
                        mm(bank(0, 0, TT)[0:2, :], ONESH[:],
                           psq[:, qt * TT:(qt + 1) * TT],
                           start=True, stop=True,
                           deps=[t_sq] + ([t_red] if t_red else []),
                           inc="s_pe")
                        t_red = dve(lambda e, qt=qt: e.reduce_max(
                            BSC[0:2, 0:1] if qt == 0 else BSC[0:2, 1:2],
                            bank(0, 0, TT)[0:2, :],
                            mybir.AxisListType.X), deps=[p.tok("s_pe")])
                        vdrain()
                        if qt > 0:
                            t_red = dve(lambda e: e.tensor_tensor(
                                BSC[0:2, 0:1], BSC[0:2, 0:1], BSC[0:2, 1:2],
                                ALU.max))
                            vdrain()
                    dve(lambda e, wi=wi: e.tensor_copy(
                        BSC[0:2, 2 + wi:3 + wi], BSC[0:2, 0:1]))
                    vdrain()
                dve(lambda e: e.tensor_tensor(BSC[0:2, 4:5], BSC[0:2, 2:3],
                                              BSC[0:2, 3:4], ALU.mult))
                vdrain()
                rsqrt_dve(BSC[0:2, 5:6], BSC[0:2, 4:5], BSC[0:2, 6:7])
                vdrain()
                dve(lambda e: e.tensor_tensor(BSC[0:2, 5:6], BSC[0:2, 5:6],
                                              BSC[0:2, 4:5], ALU.mult))
                vdrain()
                t_bnd = dve(lambda e: e.tensor_scalar(
                    BSCB[0:2, 0:1], BSC[0:2, 5:6], -1.0 / 8.0, None, ALU.mult))
                mm(bank(0, 0, 2)[0:1, :], BSCB[0:2, 0:1], ID2[:],
                   start=True, stop=True, deps=[t_bnd, t_consts], inc="s_pe")
                t_br = acopy(BROW[0:1, 0:2], bank(0, 0, 2)[0:1, :],
                             deps=[p.tok("s_pe")])
                mm(bank(0, 0, 2), ONESR[0:1, :], BROW[0:1, 0:2],
                   start=True, stop=True, deps=[t_br], inc="s_pe")
                t_bb = acopy(BNDS[:, 0:2], bank(0, 0, 2), deps=[p.tok("s_pe")])

                pw = SC[:, 4096:4864].rearrange("p (i t) -> p i t", i=3)
                for hh in range(2):
                    h = 2 * pj + hh
                    t_mask = None
                    for qt in range(8):
                        kcs = [(i, qt * 2 - 1 + i) for i in range(3)
                               if 0 <= qt * 2 - 1 + i < 16]
                        sdep = [t_bb, t_qk]
                        if t_mask is not None:
                            sdep.append(t_mask)
                        t_sc = None
                        for i, kc in kcs:
                            dst = (bank(0, 0, TT)[:, 256 * i:256 * (i + 1)]
                                   if i < 2 else bank(1, 0, 256))
                            t_sc = mm(dst,
                                      k2[64 * hh:64 * (hh + 1),
                                         kc * 128:(kc + 1) * 128],
                                      q2[64 * hh:64 * (hh + 1),
                                         qt * 256:(qt + 1) * 256],
                                      start=True, stop=True,
                                      deps=sdep, inc="s_pe")
                            sdep = []
                        t_exps = []
                        first_i = kcs[0][0]
                        for i, kc in kcs:
                            src = (bank(0, 0, TT)[:, 256 * i:256 * (i + 1)]
                                   if i < 2 else bank(1, 0, 256))
                            t_exps.append(act(pw[:, i, :], src, AF.Exp,
                                              bias=BNDS[:, hh:hh + 1],
                                              scale=1 / 8.0,
                                              deps=[t_sc] if i == first_i
                                              else ()))
                        ni = len(kcs)
                        t_mask = dve(
                            lambda e, first_i=first_i, ni=ni: e.tensor_tensor(
                                pw[:, first_i:first_i + ni, :],
                                pw[:, first_i:first_i + ni, :],
                                WMASK[:, first_i:first_i + ni, :], ALU.mult),
                            deps=t_exps + [t_consts])
                        vdrain()
                        ydst = PS[0:65, 2048 + 512 * (qt // 2) + 256 * (qt % 2):
                                  2048 + 512 * (qt // 2) + 256 * (qt % 2) + 256]
                        for ji, (i, kc) in enumerate(kcs):
                            dep0 = []
                            if ji == 0:
                                dep0 = [t_mask, t_vo]
                                if qt == 0 and t_yevac is not None:
                                    dep0 += [t_yevac, t_zev32]
                            mm(ydst, vplus(kc, h), pw[:, i, :],
                               start=(ji == 0), stop=(ji == len(kcs) - 1),
                               deps=dep0,
                               inc="s_pe" if ji == len(kcs) - 1 else None)
                        t_pv = p.tok("s_pe")
                    ych = h // 2
                    ypart = 64 * (h % 2)
                    yps = PS[:, 2048:4096]
                    if ypart == 0:
                        t_yevac = acopy(ab[:, ych, :][0:64, :], yps[0:64, :],
                                        deps=[t_pv])
                    else:
                        ty = SC[:, 6144:8192][0:64, :]
                        t_ty = acopy(ty, yps[0:64, :], deps=[t_pv, t_tydma])
                        t_tydma = dma(ab[:, ych, :][64:128, :], ty,
                                      slot="tydma", deps=[t_ty])
                        t_yevac = t_ty
                    t_zev32 = acopy(ST[64:65, :], yps[64:65, :],
                                    deps=[t_pv, t_zdma])
                    t_zdma = dma(ZT[:, 0, 16 * h:16 * (h + 1)],
                                 ST[64:65, :].rearrange("o (q c) -> o q c",
                                                        q=128),
                                 slot="zdma", deps=[t_zev32])

            if not skip_w:
                t_zr = dve(lambda e: e.reciprocal(ZT[:, 1, :], ZT[:, 0, :]),
                           deps=[t_zdma, t_tydma, p.tok(_slot_sem("zbk"))])
                vdrain()
                t_hl = hilo(ZTB[:, 0, :], ZTB[:, 1, :], ZT[:, 1, :])
                vdrain()
                t_zb = None
                for h in range(HEADS):
                    for hl in range(2):
                        t_zb = dma(ZRHW[h + 16 * hl:h + 16 * hl + 1, :]
                                   .rearrange("o (q c) -> o q c", q=128),
                                   ZTB[:, hl, 16 * h:16 * (h + 1)],
                                   slot="zbk", deps=[t_hl])
                t_norm = {}
                for qt in range(NT):
                    for och in range(NCH):
                        bk = och % 2
                        dep0 = [t_zb, t_consts, ("s_act", p.counts["s_act"])]
                        if t_norm.get(bk) is not None:
                            dep0.append(t_norm[bk])
                        mm(bank(bk, 0, TT), PATZ[:, och, :],
                           ZRHW[:, qt * TT:(qt + 1) * TT],
                           start=True, stop=True, deps=dep0, inc="s_pe")
                        t_norm[bk] = dve(
                            lambda e, och=och, qt=qt, bk=bk: e.tensor_tensor(
                                ab[:, och, qt * TT:(qt + 1) * TT],
                                ab[:, och, qt * TT:(qt + 1) * TT],
                                bank(bk, 0, TT), ALU.mult),
                            deps=[p.tok("s_pe")])
                        vdrain()
                t_abn = p.tok("s_dve")
                if "stop_norm" in phases:
                    p.wait("sync", [t_abn, ("s_pe", p.counts["s_pe"])])
                    return (p.tok("s_dve"), p.tok("s_gp"))
                Cc = lambda j: STT[:, j:j + 16]
                m1, v1, r1, tmp1, mr1 = (Cc(32), Cc(48), Cc(144), Cc(80),
                                         Cc(112))
                dve(lambda e: e.tensor_scalar(v1, v1, LN_EPS, None, ALU.add),
                    deps=[t_abn, ("s_gp", p.counts["s_gp"]),
                          ("s_act", p.counts["s_act"])])
                vdrain()
                rsqrt_dve(r1, v1, tmp1)
                vdrain()
                dve(lambda e: e.tensor_tensor(mr1, m1, r1, ALU.mult))
                vdrain()
                hilo(STTB[:, 0:16], STTB[:, 16:32], r1)
                t_hl1 = hilo(STTB[:, 32:48], STTB[:, 48:64], mr1)
                vdrain()
                tbk = None
                for (q, hl, rm) in ((0, 0, 0), (1, 1, 0), (2, 0, 1),
                                    (3, 1, 1)):
                    for it in range(NT):
                        tbk = dma(ROWB[hl:hl + 1, rm, it * TT:(it + 1) * TT]
                                  .rearrange("o (pp c) -> o pp c", pp=128),
                                  STTB[:, 16 * q + 4 * it:
                                       16 * q + 4 * it + 4],
                                  slot="lnbk", deps=[t_hl1])
                _ln_state["bk"] = tbk
                if "stop_rows" in phases:
                    p.wait("sync", [tbk, ("s_pe", p.counts["s_pe"]),
                                    ("s_dve", p.counts["s_dve"])])
                    return (p.tok("s_dve"), p.tok("s_gp"))
                ev_pr = {}
                for whalf in range(2):
                    w2v = SC[:, 0:4096].rearrange("p (c w) -> p c w", c=NCH)
                    t_w2 = dma(w2v, dram["l1_wout"]
                               [1024:2048, 512 * whalf:512 * (whalf + 1)]
                               .rearrange("(c p) w -> p c w", p=128),
                               slot="wqp", queue="gp",
                               deps=[t_abn, ("s_act", p.counts["s_act"]),
                                     ("s_pe", p.counts["s_pe"])])
                    for qt in range(NT):
                        dep_b = [tbk, ("s_act", p.counts["s_act"]),
                                 ("s_dve", p.counts["s_dve"]),
                                 ("s_gp", p.counts["s_gp"])]
                        mm(bank(6, 0, TT), ONESR[:],
                           ROWB[:, 0, qt * TT:(qt + 1) * TT],
                           start=True, stop=False, deps=dep_b)
                        t_b6 = mm(bank(6, 0, TT), ONESR[0:1, :],
                                  ONESBF[0:1, :],
                                  start=False, stop=True, inc="s_pe")
                        t_b7 = mm(bank(7, 0, TT), ONESR[:],
                                  ROWB[:, 1, qt * TT:(qt + 1) * TT],
                                  start=True, stop=True, inc="s_pe")
                        t_c6 = acopy(ST[:, 0:TT], bank(6, 0, TT),
                                     deps=[t_b6])
                        t_c7 = acopy(ST[:, TT:2 * TT], bank(7, 0, TT),
                                     deps=[t_b7])
                        for oj in range(4):
                            och = 4 * whalf + oj
                            bk = 2 + oj % 2
                            dep0 = [t_w2, t_abn]
                            if ev_pr.get(bk) is not None:
                                dep0.append(ev_pr[bk])
                            for c in range(NCH):
                                mm(bank(bk, 0, TT),
                                   w2v[:, c, 128 * oj:128 * (oj + 1)],
                                   ab[:, c, qt * TT:(qt + 1) * TT],
                                   start=(c == 0), stop=(c == NCH - 1),
                                   deps=dep0 if c == 0 else (),
                                   inc="s_pe" if c == NCH - 1 else None)
                            eng_gp = oj % 2 == 1
                            tpr = ST[:, (2 + (1 if eng_gp else 0)) * TT:
                                     (3 + (1 if eng_gp else 0)) * TT]
                            t_t = act(tpr, bank(bk, 0, TT), AF.Identity,
                                      deps=[p.tok("s_pe"),
                                            ev_pr.get("g" if eng_gp
                                                      else "d")])
                            ev_pr[bk] = t_t
                            if "wnoep" in phases:
                                ev_pr["g" if eng_gp else "d"] = t_t
                                continue
                            xsl = X[:, och, qt * TT:(qt + 1) * TT]
                            if eng_gp:
                                gp(lambda e, tpr=tpr: e.tensor_tensor(
                                    tpr, tpr, ST[:, TT:2 * TT],
                                    ALU.subtract),
                                   deps=[t_t, t_c7, t_c6])
                                p.emit("gpsimd", lambda e: e.drain())
                                gp(lambda e, xsl=xsl: e.tensor_tensor(
                                    xsl, xsl, ST[:, 0:TT], ALU.mult))
                                p.emit("gpsimd", lambda e: e.drain())
                                tok = gp(lambda e, xsl=xsl, tpr=tpr:
                                         e.tensor_tensor(xsl, xsl, tpr,
                                                         ALU.add))
                                ev_pr["g"] = tok
                                p.emit("gpsimd", lambda e: e.drain())
                            else:
                                dve(lambda e, tpr=tpr: e.tensor_tensor(
                                    tpr, tpr, bank(7, 0, TT), ALU.subtract),
                                    deps=[t_t, t_b7])
                                vdrain()
                                dve(lambda e, xsl=xsl: e.tensor_tensor(
                                    xsl, xsl, bank(6, 0, TT), ALU.mult),
                                    deps=[t_b6])
                                vdrain()
                                tok = dve(lambda e, xsl=xsl, tpr=tpr:
                                          e.tensor_tensor(xsl, xsl, tpr,
                                                          ALU.add))
                                ev_pr["d"] = tok
                                vdrain()
                        p.wait("tensor", [ev_pr.get("d"), ev_pr.get("g"),
                                          ("s_dve", p.counts["s_dve"]),
                                          ("s_gp", p.counts["s_gp"])])
            t_wend = (p.tok("s_dve"), p.tok("s_gp"))
            if "l" not in phases:
                return t_wend

            A_sb = SC[:, 0:512]
            kfp = SC[:, 4416:4672].rearrange("p (c f) -> p c f", c=16)
            kfv = SC[0:16, 512:2560]
            kqh = SC[:, 2560:3648].rearrange("p (c f) -> p c f", c=8)
            DSB = [SC[0:97, 3648 + 256 * g:3648 + 256 * g + 241]
                   for g in range(3)]
            QF0 = SC[0:17, 4736:6784]
            QF16 = SC[0:16, 4736:6784]
            qB = SC[0:96, 6784:7296]
            qC = SC[0:40, 7296:7808]
            kf_tm = SC[:, 7808:8080].rearrange("p (c f) -> p c f", c=16)
            tyL = WA[:, W_A0["wq"]:W_A0["wq"] + 512][0:64, :]
            pbuf = SC[:, 512:2560]

            def dsb_piece(g, piece):
                base = SC[0:97, 3648 + 256 * g:3648 + 256 * g + 241]
                if piece == "A":
                    return base[0:17, 0:65]
                if piece == "B":
                    return base[0:96, 80:145]
                return base[0:40, 176:241]

            def kvps(g, piece):
                col = 512 * 6 + 256 * g if g < 2 else 512 * 7
                if piece == "A":
                    return PS[0:17, col:col + 65]
                if piece == "B":
                    return PS[0:96, col + 80:col + 145]
                return PS[0:40, col + 176:col + 241]

            ZRH2 = SC[0:32, 6144:8192][:, 0:2048]
            t_lin = {}
            t_zdma = None
            t_tydma = None
            kfv = pbuf[0:16, 0:2048]
            dep_all = [("s_act", p.counts["s_act"]),
                       ("s_pe", p.counts["s_pe"]),
                       ("s_dve", p.counts["s_dve"]),
                       ("s_gp", p.counts["s_gp"])]
            t_onq = dve(lambda e: e.memset(QF0[:], 1.0), deps=dep_all)
            vdrain()
            t_ones = dve(lambda e: e.memset(kf_tm[:, :, 16:17], 1.0))
            vdrain()
            t_lin[("qf", 0)] = t_onq
            t_lin[("qf", 1)] = t_onq
            for pj in range(8):
                for hh in range(2):
                    h = 2 * pj + hh
                    dep_pair = [("s_act", p.counts["s_act"]),
                                ("s_pe", p.counts["s_pe"]),
                                ("s_dve", p.counts["s_dve"]),
                                ("s_gp", p.counts["s_gp"])]
                    wqp = WA[:, W_A0["wq"] + 512:W_A0["wq"] + 1536].rearrange(
                        "p (c w) -> p c w", c=NCH)
                    t_w = dma(wqp, dram["l1_wq"][:, 128 * pj:128 * (pj + 1)]
                              .rearrange("(c p) w -> p c w", p=128),
                              slot="wqp", queue="gp", deps=dep_pair)
                    for qt in range(NT):
                        bk = qt % 2
                        dep0 = [t_w] + dep_pair
                        if t_lin.get(("pb", bk)) is not None:
                            dep0.append(t_lin[("pb", bk)])
                        for c in range(NCH):
                            mm(bank(bk, 0, TT), wqp[:, c, :],
                               hb[:, c, qt * TT:(qt + 1) * TT],
                               start=(c == 0), stop=(c == NCH - 1),
                               deps=dep0 if c == 0 else (),
                               inc="s_pe" if c == NCH - 1 else None)
                        t_lin[("pb", bk)] = acopy(
                            pbuf[:, qt * TT:(qt + 1) * TT], bank(bk, 0, TT),
                            deps=[p.tok("s_pe")])
                    t_pb = p.tok("s_act")
                    for qt in range(NT):
                        mm(bank(4 + qt % 2, 0, TT)[0:16, :],
                           WQF[64 * hh:64 * hh + 64, :],
                           pbuf[64 * hh:64 * (hh + 1),
                                qt * TT:(qt + 1) * TT],
                           start=True, stop=True,
                           deps=[t_pb, t_consts, t_lin.get(("qf", qt % 2))],
                           inc="s_pe")
                        t_lin[("qf", qt % 2)] = acopy(
                            QF16[:, qt * TT:(qt + 1) * TT],
                            bank(4 + qt % 2, 0, TT)[0:16, :],
                            deps=[p.tok("s_pe")])
                    t_qf = p.tok("s_act")
                    t_w = dma(wqp, dram["l1_wk"][:, 128 * pj:128 * (pj + 1)]
                              .rearrange("(c p) w -> p c w", p=128),
                              slot="wqp", queue="gp",
                              deps=[p.tok("s_pe"), p.tok("s_act")])
                    for qt in range(NT):
                        bk = qt % 2
                        dep0 = [t_w, t_lin[("pb", bk)], t_qf]
                        for c in range(NCH):
                            mm(bank(bk, 0, TT), wqp[:, c, :],
                               hb[:, c, qt * TT:(qt + 1) * TT],
                               start=(c == 0), stop=(c == NCH - 1),
                               deps=dep0 if c == 0 else (),
                               inc="s_pe" if c == NCH - 1 else None)
                        t_lin[("pb", bk)] = acopy(
                            pbuf[:, qt * TT:(qt + 1) * TT], bank(bk, 0, TT),
                            deps=[p.tok("s_pe")])
                    t_pb = p.tok("s_act")
                    for tc in range(16):
                        mm(bank(4 + tc % 2, 0, 32),
                           pbuf[:, tc * 128:(tc + 1) * 128],
                           KFBD[:], start=True, stop=True,
                           deps=[t_pb, t_consts, t_lin.get(("kt", tc % 2))],
                           inc="s_pe")
                        acopy(kf_tm[:, tc, 0:16],
                              bank(4 + tc % 2, 0, 32)
                              [:, 16 * hh:16 * (hh + 1)],
                              deps=[p.tok("s_pe")])
                        a2 = p.emit("scalar", lambda e, tc=tc, hh=hh:
                                    e.activation(
                                        kfp[:, tc, :],
                                        bank(4 + tc % 2, 0, 32)
                                        [:, 16 * hh:16 * (hh + 1)],
                                        AF.Identity,
                                        scale=float(np.sqrt(0.5))),
                                    inc="s_act")
                        t_lin[("kt", tc % 2)] = a2
                    t_ktm = p.tok("s_act")

                    def build_kq(khalf, t_ktm=t_ktm):
                        sl8 = slice(8 * khalf, 8 * khalf + 8)
                        t_q1 = dve(lambda e: e.tensor_tensor(
                            kqh[:, :, 0:16], kfp[:, sl8, :], kfp[:, sl8, :],
                            ALU.mult),
                            deps=[t_ktm, t_lin.get(("kq",)),
                                  ("s_pe", p.counts["s_pe"])])
                        vdrain()
                        dve(lambda e: e.tensor_scalar(
                            kqh[:, :, 0:16], kqh[:, :, 0:16], 0.5, None,
                            ALU.mult))
                        vdrain()
                        for i in range(15):
                            off = 16 + 15 * i - (i * (i - 1)) // 2
                            nw = 15 - i
                            dve(lambda e, i=i, off=off, nw=nw: e.tensor_tensor(
                                kqh[:, :, off:off + nw],
                                kfp[:, sl8, i:i + 1].broadcast_to(
                                    (128, 8, nw)),
                                kfp[:, sl8, i + 1:16], ALU.mult))
                        vdrain()
                        tk = dve(lambda e: e.engine_nop())
                        t_lin[("kq",)] = tk
                        return tk

                    t_kq = build_kq(0)
                    t_lin[("kq",)] = t_kq
                    for qt in range(NT):
                        mm(bank(4 + qt % 2, 0, TT)[0:16, :],
                           WKF[64 * hh:64 * hh + 64, :],
                           pbuf[64 * hh:64 * (hh + 1),
                                qt * TT:(qt + 1) * TT],
                           start=True, stop=True,
                           deps=[t_ktm, t_kq, t_consts,
                                 t_lin.get(("qf", qt % 2))],
                           inc="s_pe")
                        t_lin[("qf", qt % 2)] = acopy(
                            kfv[:, qt * TT:(qt + 1) * TT],
                            bank(4 + qt % 2, 0, TT)[0:16, :],
                            deps=[p.tok("s_pe")])
                    t_kf = p.tok("s_act")
                    if "nostate" in phases:
                        continue
                    t_yev = None
                    t_qbc = None
                    t_zev32 = None
                    t_dev = None
                    qfh = QF0
                    for qt in range(4):
                        sl5 = slice(qt * TT, (qt + 1) * TT)
                        ybank = 2 + (qt % 2)
                        ydst = PS[0:65, 512 * ybank:512 * ybank + 512]
                        if qt == 2:
                            t_kq = build_kq(1)
                        dep_q = [t_kf, t_consts,
                                 ("s_act", p.counts["s_act"])]
                        if t_qbc is not None:
                            dep_q.append(t_qbc)
                        mm(PS[0:96, 512 * 4:512 * 4 + TT], SEL[:, 0, 0:96],
                           QF16[:, sl5], start=True, stop=True,
                           deps=dep_q, inc="s_pe")
                        tb1 = acopy(qB[:], PS[0:96, 512 * 4:512 * 4 + TT],
                                    deps=[p.tok("s_pe")])
                        mm(PS[0:96, 512 * 5:512 * 5 + TT], SEL[:, 1, 0:96],
                           QF16[:, sl5], start=True, stop=True,
                           deps=[tb1], inc="s_pe")
                        tq1 = dve(lambda e: e.tensor_tensor(
                            qB[:], qB[:], PS[0:96, 512 * 5:512 * 5 + TT],
                            ALU.mult), deps=[p.tok("s_pe"), tb1])
                        vdrain()
                        mm(PS[0:40, 512 * 4:512 * 4 + TT], SEL[:, 0, 96:136],
                           QF16[:, sl5], start=True, stop=True,
                           deps=[tq1], inc="s_pe")
                        tb2 = acopy(qC[:], PS[0:40, 512 * 4:512 * 4 + TT],
                                    deps=[p.tok("s_pe")])
                        mm(PS[0:40, 512 * 5:512 * 5 + TT], SEL[:, 1, 96:136],
                           QF16[:, sl5], start=True, stop=True,
                           deps=[tb2], inc="s_pe")
                        t_qbc = dve(lambda e: e.tensor_tensor(
                            qC[:], qC[:], PS[0:40, 512 * 5:512 * 5 + TT],
                            ALU.mult), deps=[p.tok("s_pe"), tb2])
                        vdrain()
                        if "st_qphi" in phases:
                            t_yev = t_qbc
                            t_zev32 = t_qbc
                            continue
                        started = False
                        dep_y0 = [t_qbc, t_vo, t_ones]
                        if t_yev is not None:
                            dep_y0.append(t_yev)
                        if t_dev is not None:
                            dep_y0.append(t_dev)
                        for g in (range(qt) if "st_nointer" not in phases
                                  else ()):
                            mm(ydst, dsb_piece(g, "A"), qfh[:, sl5],
                               start=(not started), stop=False, deps=dep_y0)
                            dep_y0 = []
                            started = True
                            if "st_ia" in phases:
                                continue
                            mm(ydst, dsb_piece(g, "B"), qB[:],
                               start=False, stop=False)
                            if "st_iab" in phases:
                                continue
                            mm(ydst, dsb_piece(g, "C"), qC[:],
                               start=False, stop=False)

                        for sj in range(4):
                            sa = 4 * qt + sj
                            t_u = mm(bank(0, 0, TT),
                                     kfv[:, sa * 128:(sa + 1) * 128],
                                     QF16[:, sl5],
                                     start=True, stop=True,
                                     deps=[t_lin.get(("A",)), t_kf] + dep_y0,
                                     inc="s_pe")
                            dep_y0 = []
                            t_A = act(A_sb, bank(0, 0, TT), AF.Square,
                                      bias=1.0, scale=0.5, deps=[t_u])
                            if sj > 0:
                                t_z0 = dve(lambda e, sj=sj: e.memset(
                                    A_sb[:, 0:128 * sj], 0.0), deps=[t_A])
                            else:
                                t_z0 = t_A
                            t_Am = dve(lambda e, sj=sj: e.tensor_tensor(
                                A_sb[:, 128 * sj:128 * (sj + 1)],
                                A_sb[:, 128 * sj:128 * (sj + 1)],
                                TRIL[:], ALU.mult),
                                deps=[t_z0, t_consts, t_A])
                            vdrain()
                            t_lin[("A",)] = t_Am
                            if "st_noy" not in phases:
                                mm(ydst, vplus(sa, h), A_sb,
                                   start=(not started), stop=(sj == 3),
                                   deps=[t_Am],
                                   inc="s_pe" if sj == 3 else None)
                                started = True
                        t_ymm = p.tok("s_pe")
                        if "st_noy" in phases:
                            t_yev = t_Am
                            t_zev32 = t_Am
                            continue
                        if qt < 3 and "st_nokv" not in phases:
                            dep_kv = [t_ktm, t_kq, t_ones]
                            if t_dev is not None:
                                dep_kv.append(t_dev)
                            for sj in range(4):
                                sa = 4 * qt + sj
                                kqc = sa % 8
                                if sj < 3:
                                    mm(kvps(qt, "B"), kqh[:, kqc, 0:96],
                                       vplus(sa, h),
                                       start=(sj == 0), stop=False,
                                       deps=dep_kv)
                                    dep_kv = []
                                    mm(kvps(qt, "A"), kf_tm[:, sa, :],
                                       vplus(sa, h), start=False, stop=False)
                                    mm(kvps(qt, "C"), kqh[:, kqc, 96:136],
                                       vplus(sa, h), start=False, stop=False)
                                else:
                                    mm(kvps(qt, "A"), kf_tm[:, sa, :],
                                       vplus(sa, h), start=False, stop=False)
                                    mm(kvps(qt, "C"), kqh[:, kqc, 96:136],
                                       vplus(sa, h), start=False, stop=False)
                                    mm(kvps(qt, "B"), kqh[:, kqc, 0:96],
                                       vplus(sa, h), start=False, stop=True,
                                       inc="s_pe")
                            col = 512 * 6 + 256 * qt if qt < 2 else 512 * 7
                            t_dev = acopy(SC[0:96, 3648 + 256 * qt:
                                             3648 + 256 * qt + 241],
                                          PS[0:96, col:col + 241],
                                          deps=[p.tok("s_pe")])
                        ych = h // 2
                        ypart = 64 * (h % 2)
                        ysrc = ydst
                        if ypart == 0:
                            t_yev = acopy(
                                ab[:, ych, qt * TT:(qt + 1) * TT][0:64, :],
                                ysrc[0:64, :], deps=[t_ymm])
                        else:
                            tt1 = acopy(tyL[:, 0:TT], ysrc[0:64, :],
                                        deps=[t_ymm, t_tydma])
                            t_tydma = dma(
                                ab[:, ych, qt * TT:(qt + 1) * TT][64:128, :],
                                tyL[:, 0:TT], slot="tydma", deps=[tt1])
                            t_yev = tt1
                        t_zev32 = acopy(
                            ST[64:65, qt * TT:(qt + 1) * TT],
                            ysrc[64:65, :],
                            deps=[t_ymm] + ([t_zdma] if qt == 0 else []))
                    if "nostate" not in phases:
                        t_zdma = dma(ZT[:, 0, 16 * h:16 * (h + 1)],
                                     ST[64:65, :].rearrange(
                                         "o (q c) -> o q c", q=128),
                                     slot="zdma",
                                     deps=[t_zev32, p.tok(_slot_sem("zbk"))])

            if "stop_lpairs" in phases:
                p.wait("sync", [t_zdma if "nostate" not in phases else None,
                                ("s_pe", p.counts["s_pe"]),
                                ("s_act", p.counts["s_act"]),
                                ("s_dve", p.counts["s_dve"])])
                return (p.tok("s_dve"), p.tok("s_gp"))
            t_zr = dve(lambda e: e.reciprocal(ZT[:, 1, :], ZT[:, 0, :]),
                       deps=[t_zdma, t_tydma, p.tok(_slot_sem("zbk")),
                             ("s_pe", p.counts["s_pe"])])
            vdrain()
            t_hl = hilo(ZTB[:, 0, :], ZTB[:, 1, :], ZT[:, 1, :])
            vdrain()
            t_zb = None
            for h in range(HEADS):
                for hl in range(2):
                    t_zb = dma(ZRH2[h + 16 * hl:h + 16 * hl + 1, :]
                               .rearrange("o (q c) -> o q c", q=128),
                               ZTB[:, hl, 16 * h:16 * (h + 1)],
                               slot="zbk", deps=[t_hl])
            t_norm = {}
            for qt in range(NT):
                for och in range(NCH):
                    bk = och % 2
                    dep0 = [t_zb, t_consts, ("s_act", p.counts["s_act"])]
                    if t_norm.get(bk) is not None:
                        dep0.append(t_norm[bk])
                    mm(bank(bk, 0, TT), PATZ[:, och, :],
                       ZRH2[:, qt * TT:(qt + 1) * TT],
                       start=True, stop=True, deps=dep0, inc="s_pe")
                    t_norm[bk] = dve(
                        lambda e, och=och, qt=qt, bk=bk: e.tensor_tensor(
                            ab[:, och, qt * TT:(qt + 1) * TT],
                            ab[:, och, qt * TT:(qt + 1) * TT],
                            bank(bk, 0, TT), ALU.mult),
                        deps=[p.tok("s_pe")])
                    vdrain()
            t_abn = p.tok("s_dve")
            ev_pr = {}
            for whalf in range(2):
                w2v = SC[:, 0:4096].rearrange("p (c w) -> p c w", c=NCH)
                t_w2 = dma(w2v, dram["l1_wout"]
                           [0:1024, 512 * whalf:512 * (whalf + 1)]
                           .rearrange("(c p) w -> p c w", p=128),
                           slot="wqp", queue="gp",
                           deps=[t_abn, ("s_act", p.counts["s_act"]),
                                 ("s_pe", p.counts["s_pe"])])
                for qt in range(NT):
                    for oj in range(4):
                        och = 4 * whalf + oj
                        bk = 2 + oj % 2
                        dep0 = [t_w2, t_abn]
                        if ev_pr.get(bk) is not None:
                            dep0.append(ev_pr[bk])
                        for c in range(NCH):
                            mm(bank(bk, 0, TT),
                               w2v[:, c, 128 * oj:128 * (oj + 1)],
                               ab[:, c, qt * TT:(qt + 1) * TT],
                               start=(c == 0), stop=(c == NCH - 1),
                               deps=dep0 if c == 0 else (),
                               inc="s_pe" if c == NCH - 1 else None)
                        eng_gp = oj % 2 == 1
                        tpr = ST[:, (0 + (1 if eng_gp else 0)) * TT:
                                 (1 + (1 if eng_gp else 0)) * TT]
                        t_t = act(tpr, bank(bk, 0, TT), AF.Identity,
                                  deps=[p.tok("s_pe"),
                                        ev_pr.get("g" if eng_gp else "d")])
                        ev_pr[bk] = t_t
                        xsl = X[:, och, qt * TT:(qt + 1) * TT]
                        if eng_gp:
                            tok = gp(lambda e, xsl=xsl, tpr=tpr:
                                     e.tensor_tensor(xsl, xsl, tpr, ALU.add),
                                     deps=[t_t])
                            p.emit("gpsimd", lambda e: e.drain())
                            ev_pr["g"] = tok
                        else:
                            tok = dve(lambda e, xsl=xsl, tpr=tpr:
                                      e.tensor_tensor(xsl, xsl, tpr, ALU.add),
                                      deps=[t_t])
                            vdrain()
                            ev_pr["d"] = tok
            return (p.tok("s_dve"), p.tok("s_gp"))

        W_OFF = {"cn_w1": (0, 1024), "cn_w2": (8192, 512), "cn_w3": (12288, 256),
                 "cn_w4": (13312, 1), "gc_w1": (13328, 512), "gc_w2": (17424, 256),
                 "gc_w3": (18448, 1)}

        def wa_tile(name, c, m0, mlen):
            off, wid = W_OFF[name]
            return WA[:, off + c * wid + m0: off + c * wid + m0 + mlen]

        def load_ffn0_weights(deps=()):
            toks = []
            for name, hbm in [("cn_w1", "l0_cn_w1"), ("cn_w2", "l0_cn_w2"),
                              ("cn_w3", "l0_cn_w3"), ("cn_w4", "l0_cn_w4"),
                              ("gc_w1", "l0_gc_w1"), ("gc_w2", "l0_gc_w2"),
                              ("gc_w3", "l0_gc_w3")]:
                off, wid = W_OFF[name]
                src = dram[hbm]
                if wid == 1:
                    nch = src.shape[1]
                    toks.append(dma(WA[:, off: off + nch], src, slot="wff0",
                                    queue="gp", deps=deps))
                    continue
                nch = src.shape[0] // 128
                dst = WA[:, off: off + nch * wid].rearrange(
                    "p (c w) -> p c w", c=nch)
                toks.append(dma(dst, src.rearrange("(c p) w -> p c w", p=128),
                                slot="wff0", queue="gp", deps=deps))
            return toks[-1]

        pe_after_tile = []
        prev_sts = []
        for seq in range(B_PER_CORE):
            pe_after_tile = []
            evs = load_stream(seq, dep_st=prev_sts[-2:])
            last = [evs[-1]]
            for sl_name in sublayers:
                if sl_name in ("ffn0a", "ffn0b"):
                    last = list(ffn0(seq, dep_in=last))
                elif sl_name == "attn0":
                    last = list(attn0(seq, dep_in=last))
                elif sl_name == "attn1":
                    last = list(attn1(seq, dep_in=last))
                elif sl_name == "attn1w":
                    last = list(attn1(seq, dep_in=last, phases=("w",)))
                elif sl_name == "attn1v":
                    last = list(attn1(seq, dep_in=last, phases=("v",)))
                elif sl_name == "attn1ln":
                    last = list(attn1(seq, dep_in=last,
                                      phases=("l", "stop_lpairs",
                                              "st_nointer")))
                elif sl_name == "attn1we":
                    last = list(attn1(seq, dep_in=last, phases=("w", "wnoep")))
                elif sl_name == "attn1lq":
                    last = list(attn1(seq, dep_in=last,
                                      phases=("l", "stop_lpairs", "st_qphi")))
                elif sl_name == "attn1la":
                    last = list(attn1(seq, dep_in=last,
                                      phases=("l", "stop_lpairs", "st_noy")))
                elif sl_name == "attn1lk":
                    last = list(attn1(seq, dep_in=last,
                                      phases=("l", "stop_lpairs", "st_nokv")))
                elif sl_name == "attn1lb":
                    last = list(attn1(seq, dep_in=last,
                                      phases=("l", "stop_lpairs", "nostate")))
                elif sl_name == "attn1lp":
                    last = list(attn1(seq, dep_in=last,
                                      phases=("l", "stop_lpairs")))
                elif sl_name == "attn1wr":
                    last = list(attn1(seq, dep_in=last,
                                      phases=("w", "stop_rows")))
                elif sl_name == "attn1wn":
                    last = list(attn1(seq, dep_in=last,
                                      phases=("w", "stop_norm")))
                elif sl_name == "attn1wp":
                    last = list(attn1(seq, dep_in=last,
                                      phases=("w", "stop_pairs")))
                elif sl_name == "attn1li1":
                    last = list(attn1(seq, dep_in=last,
                                      phases=("l", "stop_lpairs", "st_ia")))
                elif sl_name == "attn1li2":
                    last = list(attn1(seq, dep_in=last,
                                      phases=("l", "stop_lpairs", "st_iab")))
                elif sl_name == "attn1li3":
                    last = list(attn1(seq, dep_in=last,
                                      phases=("l", "stop_lpairs", "st_iabc")))
                elif sl_name == "attn1l":
                    last = list(attn1(seq, dep_in=last, phases=("l",)))
                elif sl_name in ("ffn1a", "ffn1b"):
                    last = list(ffn1(seq, dep_in=last))
                elif sl_name == "lnonly":
                    last = list(ffn0(seq, dep_in=last, lnonly=True))
                elif sl_name.startswith("ffn0nc"):
                    last = list(ffn0(seq, dep_in=last,
                                     ncycles=int(sl_name[-1])))
                else:
                    raise NotImplementedError(sl_name)
            prev_sts = store_stream(seq, last)
        p.wait("sync", [p.tok(_slot_sem("st0")), p.tok(_slot_sem("st1"))])

        block = es.enter_context(nc.Block())

        @block.sync
        def _(e):
            p.replay("sync", e)

        @block.tensor
        def _(e):
            p.replay("tensor", e)

        @block.scalar
        def _(e):
            p.replay("scalar", e)

        @block.vector
        def _(e):
            p.replay("vector", e)

        @block.gpsimd
        def _(e):
            p.replay("gpsimd", e)

    return nc



def _prep_consts():
    ident = np.eye(128, dtype=np.float32)
    tril = np.triu(np.ones((128, 128), dtype=np.float32))
    win = np.zeros((128, 3, 256), dtype=np.float32)
    for di, delta in enumerate((-128, 0, 128)):
        pp = np.arange(128)[:, None]
        qq = np.arange(256)[None, :]
        k = pp + delta
        win[:, di, :] = ((k <= qq) & (k >= qq - WINDOW)).astype(np.float32)
    iu, ju = np.triu_indices(FEAT)
    sel1 = np.zeros((FEAT, NQUAD), dtype=np.float32)
    sel2 = np.zeros((FEAT, NQUAD), dtype=np.float32)
    for r in range(NQUAD):
        sel1[iu[r], r] = 1.0
        sel2[ju[r], r] = 1.0
    return ident, tril, win, sel1, sel2


def _prep_weights(inp):
    w = {}
    R = np.ascontiguousarray(inp["L0_attn_W_qkv"], dtype=np.float32).reshape(
        HEADS, 3, HD, D)
    w["l0_wq"] = R[:, 0].transpose(2, 0, 1).reshape(D, D)
    w["l0_wk"] = R[:, 1].transpose(2, 0, 1).reshape(D, D)
    w["l0_wv"] = R[:, 2].transpose(2, 0, 1).reshape(D, D)
    w["l0_wo"] = np.asarray(inp["L0_attn_W_out"]).T
    w["l0_bo"] = np.asarray(inp["L0_attn_b_out"]).reshape(8, 128).T
    w["l0_cn_w1"] = inp["L0_ffn_cn_w1"]
    w["l0_cn_w2"] = inp["L0_ffn_cn_w2"]
    w["l0_cn_w3"] = inp["L0_ffn_cn_w3"]
    w["l0_cn_w4"] = np.asarray(inp["L0_ffn_cn_w4"]).reshape(2, 128).T
    w["l0_cn_b1"] = np.asarray(inp["L0_ffn_cn_b1"]).reshape(8, 128).T
    w["l0_cn_b2"] = np.asarray(inp["L0_ffn_cn_b2"]).reshape(4, 128).T
    w["l0_cn_b3"] = np.asarray(inp["L0_ffn_cn_b3"]).reshape(2, 128).T
    w["l0_cn_b4"] = np.asarray(inp["L0_ffn_cn_b4"]).reshape(1, 1)
    w["l0_gc_w1"] = inp["L0_ffn_gc_w1"]
    w["l0_gc_w2"] = inp["L0_ffn_gc_w2"]
    w["l0_gc_w3"] = np.asarray(inp["L0_ffn_gc_w3"]).reshape(2, 128).T
    w["l0_gc_b1"] = np.asarray(inp["L0_ffn_gc_b1"]).reshape(4, 128).T
    w["l0_gc_b2"] = np.asarray(inp["L0_ffn_gc_b2"]).reshape(2, 128).T
    w["l0_gc_b3"] = np.asarray(inp["L0_ffn_gc_b3"]).reshape(1, 1)
    w["l0_ar"] = np.asarray(inp["L0_ffn_adaptive_range"]).reshape(1, 1)
    w["l1_wq"] = inp["L1_attn_Wq"]
    w["l1_wk"] = inp["L1_attn_Wk"]
    w["l1_wv"] = inp["L1_attn_Wv"]
    w["l1_wout"] = inp["L1_attn_Wout"]
    w["l1_wqf"] = np.vstack([np.asarray(inp["L1_attn_Wqf"])] * 2)
    w["l1_wkf"] = np.vstack([np.asarray(inp["L1_attn_Wkf"])] * 2)
    w["l1_normw"] = np.asarray(inp["L1_attn_norm_w"]).reshape(8, 128).T
    w["l1_w1"] = inp["L1_ffn_w1"]
    w["l1_b1"] = np.asarray(inp["L1_ffn_b1"]).reshape(32, 128).T
    w["l1_w2"] = inp["L1_ffn_w2"]
    w["l1_b2"] = np.asarray(inp["L1_ffn_b2"]).reshape(8, 128).T
    kfbd = np.zeros((128, 32), dtype=np.float32)
    kfbd[0:64, 0:16] = np.asarray(inp["L1_attn_Wkf"])
    kfbd[64:128, 16:32] = np.asarray(inp["L1_attn_Wkf"])
    w["kf_blockdiag"] = kfbd
    ident, tril, win, sel1, sel2 = _prep_consts()
    patz = np.zeros((32, NCH, 128), dtype=np.float32)
    for och in range(NCH):
        for mcol in range(128):
            hd_ = 2 * och + (mcol // 64)
            patz[hd_, och, mcol] = 1.0
            patz[16 + hd_, och, mcol] = 1.0
    w["patz"] = patz
    w["id2"] = np.eye(2, dtype=np.float32)
    w["ident"] = ident
    w["tril"] = tril
    w["winmask"] = win
    w["sel1"] = sel1
    w["sel2"] = sel2
    return {k: np.ascontiguousarray(v, dtype=np.float32) for k, v in w.items()}


def kernel(**inputs):
    x = np.ascontiguousarray(inputs["L0_attn_x"], dtype=np.float32)
    w = _prep_weights(inputs)
    nc = build_program()
    in_maps = []
    for c in range(N_CORES):
        m = {"x": x[c * B_PER_CORE:(c + 1) * B_PER_CORE]}
        m.update(w)
        in_maps.append(m)
    res = run_bass_kernel_spmd(nc, in_maps, core_ids=list(range(N_CORES)))
    out = np.concatenate([r["y"] for r in res.results], axis=0)
    return out.astype(np.float32)


# version 07
# speedup vs baseline: 1.0053x; 1.0053x over previous
import contextlib
import numpy as np

import concourse.bass as bass
import concourse.mybir as mybir
from concourse.bass_utils import run_bass_kernel_spmd

F32 = mybir.dt.float32
BF16 = mybir.dt.bfloat16
I32 = mybir.dt.int32
AF = mybir.ActivationFunctionType
ALU = mybir.AluOpType

D = 1024
NCH = 8
T = 2048
B_PER_CORE = 2
N_CORES = 8
HEADS = 16
HD = 64
LN_EPS = 1e-5
RMS_EPS = 1e-6
LIN_EPS = 1e-6
WINDOW = 64
CHUNK = 128
FEAT = 16
NQUAD = 136
NPHI = 153

ENGINES = ("sync", "tensor", "scalar", "vector", "gpsimd")

SUBLAYERS = ["ffn0a", "attn0", "ffn0b", "ffn1a", "attn1", "ffn1b"]


class Prog:

    def __init__(self, nc):
        self.nc = nc
        self.ops = []
        self.sems = {}
        self.counts = {}
        self.last_wait = {e: {} for e in ENGINES}
        self.es = contextlib.ExitStack()

    def sem(self, name):
        h = self.es.enter_context(self.nc.semaphore(name))
        self.sems[name] = h
        self.counts[name] = 0
        return name

    def tok(self, semname):
        return (semname, self.counts[semname])

    def emit(self, eng, fn, deps=(), inc=None, inc_n=1):
        deps = [d for d in deps if d is not None]
        needed = []
        for s, v in deps:
            if v <= 0:
                continue
            if self.last_wait[eng].get(s, 0) >= v:
                continue
            self.last_wait[eng][s] = v
            needed.append((s, v))
        if inc is not None:
            self.counts[inc] += inc_n
        tok = (inc, self.counts[inc]) if inc is not None else None
        sems = self.sems

        def run(e):
            for s, v in needed:
                e.wait_ge(sems[s], v)
            ins = fn(e)
            if inc is not None:
                if ins is None:
                    ins = e.nop()
                ins.then_inc(sems[inc], inc_n)

        self.ops.append((eng, run))
        return tok

    def wait(self, eng, deps):
        self.emit(eng, lambda e: None, deps=deps)

    def replay(self, eng_name, e):
        for name, fn in self.ops:
            if name == eng_name:
                fn(e)



def build_program(sublayers=None):
    if sublayers is None:
        sublayers = SUBLAYERS
    nc = bass.Bass("TRN2", target_bir_lowering=False)

    dram = {}

    def din(name, shape):
        dram[name] = nc.dram_tensor(name, shape, F32, kind="ExternalInput").ap()
        return dram[name]

    x_hbm = din("x", [B_PER_CORE, T, D])
    y_hbm = nc.dram_tensor("y", [B_PER_CORE, T, D], F32, kind="ExternalOutput").ap()

    din("l0_cn_w1", [D, D]);     din("l0_cn_b1", [128, 8])
    din("l0_cn_w2", [D, 512]);   din("l0_cn_b2", [128, 4])
    din("l0_cn_w3", [512, 256]); din("l0_cn_b3", [128, 2])
    din("l0_cn_w4", [128, 2]);   din("l0_cn_b4", [1, 1])
    din("l0_gc_w1", [D, 512]);   din("l0_gc_b1", [128, 4])
    din("l0_gc_w2", [512, 256]); din("l0_gc_b2", [128, 2])
    din("l0_gc_w3", [128, 2]);   din("l0_gc_b3", [1, 1])
    din("l0_ar", [1, 1])
    din("l0_wq", [D, D]); din("l0_wk", [D, D]); din("l0_wv", [D, D])
    din("l0_wo", [D, D]); din("l0_bo", [128, 8])
    din("l1_wq", [D, D]); din("l1_wk", [D, D]); din("l1_wv", [D, D])
    din("l1_wout", [2 * D, D])
    din("l1_wqf", [2 * HD, FEAT]); din("l1_wkf", [2 * HD, FEAT])
    din("l1_normw", [128, 8])
    din("l1_w1", [D, 4 * D]); din("l1_b1", [128, 32])
    din("l1_w2", [4 * D, D]); din("l1_b2", [128, 8])
    din("ident", [128, 128])
    din("kf_blockdiag", [128, 32])
    din("tril", [128, 128])
    din("winmask", [128, 3, 256])
    din("sel1", [FEAT, NQUAD])
    din("sel2", [FEAT, NQUAD])
    din("patz", [32, NCH, 128])
    din("id2", [2, 2])

    p = Prog(nc)
    es = p.es

    def sb(name, shape, dtype=F32):
        return es.enter_context(nc.sbuf_tensor(name, shape, dtype))

    with es:
        X = sb("X", [128, NCH, T])
        O = sb("O", [128, NCH, T])
        Ob = O.ap().bitcast(BF16)
        WA = sb("WA", [128, 18464], BF16)
        SC = sb("SC", [128, 8192], BF16)
        ST = sb("ST", [128, 2048])

        STT = sb("STT", [128, 160])
        STTB = sb("STTB", [128, 64], BF16)
        ROWB = sb("ROWB", [2, 2, 2048], BF16)
        STG = ST.ap()[0:1, 0:1024]
        CROWB = ROWB.ap()[0:1, 0, 0:1024]
        ONESC = sb("ONESC", [128, 2], BF16)
        ONESR = sb("ONESR", [2, 128], BF16)
        IDENT = sb("IDENT", [128, 128])
        BIAS = sb("BIAS", [128, 88])
        SMALL = sb("SMALL", [1, 64])
        GCH = sb("GCH", [128, 16], BF16)
        HALFT = sb("HALFT", [128, 512])
        TRIL = sb("TRIL", [128, 128], BF16)
        WMASK = sb("WMASK", [128, 3, 256], BF16)
        SEL = sb("SEL", [16, 2, NQUAD], BF16)
        KFBD = sb("KFBD", [128, 32], BF16)
        WQF = sb("WQF", [128, 16], BF16)
        WKF = sb("WKF", [128, 16], BF16)
        PATZ = sb("PATZ", [32, NCH, 128], BF16)
        ID2 = sb("ID2", [2, 2], BF16)
        ONESH = sb("ONESH", [128, 2], BF16)
        ONESBF = sb("ONESBF", [65, 512], BF16)
        ZT = sb("ZT", [128, 2, 256])
        ZTB = ZT.ap()[:, 0, :].bitcast(BF16).rearrange(
            "p (q c) -> p q c", q=2)
        ZRH = SC.ap()[0:32, 0:2048]
        BNDS = sb("BNDS", [128, 4])
        BSC = sb("BSC", [2, 16])
        BSCB = sb("BSCB", [2, 4], BF16)
        BROW = sb("BROW", [1, 8], BF16)

        PS = es.enter_context(nc.psum_tensor("PS", [128, 4096], F32))

        def bank(i, o0=0, n=512):
            return PS[:, 512 * i + o0: 512 * i + o0 + n]

        s_pe = p.sem("s_pe")
        s_act = p.sem("s_act")
        s_dve = p.sem("s_dve")
        s_gp = p.sem("s_gp")

        _dma_slots = {}

        def _slot_sem(slot):
            if slot not in _dma_slots:
                _dma_slots[slot] = p.sem("dq_" + slot)
            return _dma_slots[slot]

        def dma(dst, src, slot="init", queue="sync", deps=()):
            semname = _slot_sem(slot)
            eng = "sync" if queue == "sync" else "gpsimd"
            return p.emit(eng, lambda e: e.dma_start(out=dst, in_=src),
                          deps=deps, inc=semname, inc_n=16)

        def store(dst, src, slot="st0", deps=()):
            semname = _slot_sem(slot)
            return p.emit("sync", lambda e: e.dma_start(out=dst, in_=src),
                          deps=deps, inc=semname, inc_n=16)

        def mm(out, lhsT, rhs, start, stop, deps=(), inc=None):
            return p.emit(
                "tensor",
                lambda e: e.matmul(out, lhsT, rhs, start=start, stop=stop),
                deps=deps, inc=inc)

        def mmt(out, in_, deps=(), inc=None):
            return p.emit("tensor", lambda e: e.transpose(out, in_, IDENT[:]),
                          deps=deps, inc=inc)

        def act(out, in_, func, bias=0.0, scale=1.0, accum=None, deps=(), inc="s_act"):
            return p.emit(
                "scalar",
                lambda e: e.activation(out, in_, func, bias=bias, scale=scale,
                                       accum_out=accum),
                deps=deps, inc=inc)

        def acopy(out, in_, deps=(), inc="s_act"):
            return p.emit("scalar", lambda e: e.copy(out, in_), deps=deps, inc=inc)

        def dve(fn, deps=(), inc="s_dve"):
            return p.emit("vector", fn, deps=deps, inc=inc)

        def gp(fn, deps=(), inc="s_gp"):
            return p.emit("gpsimd", fn, deps=deps, inc=inc)

        def vdrain():
            p.emit("vector", lambda e: e.drain())

        t_id = dma(IDENT[:], dram["ident"][:])
        dve(lambda e: e.memset(ONESC[:], 1.0))
        dve(lambda e: e.memset(ONESR[:], 1.0))
        t_tril = dma(TRIL[:], dram["tril"][:], slot="initg", queue="gp")
        t_wmask = dma(WMASK[:], dram["winmask"][:], slot="initg", queue="gp")
        t_sel1 = dma(SEL[:, 0], dram["sel1"][:], slot="initg", queue="gp")
        t_sel2 = dma(SEL[:, 1], dram["sel2"][:], slot="initg", queue="gp")
        t_kfbd = dma(KFBD[:], dram["kf_blockdiag"][:], slot="initg", queue="gp")
        t_wqf = dma(WQF[:], dram["l1_wqf"][:], slot="initg", queue="gp")
        dma(WKF[:], dram["l1_wkf"][:], slot="initg", queue="gp")
        dma(PATZ[:], dram["patz"][:], slot="initg", queue="gp")
        dma(ID2[:], dram["id2"][:], slot="initg", queue="gp")
        dve(lambda e: e.memset(ONESH[:, :], 0.0))
        vdrain()
        dve(lambda e: e.memset(ONESH[0:64, 0:1], 1.0))
        dve(lambda e: e.memset(ONESH[64:128, 1:2], 1.0))
        dve(lambda e: e.memset(ONESBF[:, :], 1.0))
        gp(lambda e: e.memset(HALFT[:], 0.5))

        BIAS_MAP = {}
        _bias_col = [0]

        def load_bias(name, ncols):
            c0 = _bias_col[0]
            _bias_col[0] += ncols
            assert _bias_col[0] <= 76
            BIAS_MAP[name] = c0
            return dma(BIAS[:, c0:c0 + ncols], dram[name][:])

        for nm, wd in [("l0_cn_b1", 8), ("l0_cn_b2", 4), ("l0_cn_b3", 2),
                       ("l0_gc_b1", 4), ("l0_gc_b2", 2), ("l0_bo", 8),
                       ("l1_normw", 8), ("l1_b1", 32), ("l1_b2", 8)]:
            load_bias(nm, wd)

        def bias_ap(name, j):
            c = BIAS_MAP[name] + j
            return BIAS[:, c:c + 1]

        t_hb2 = None
        dma(SMALL[:, 0:1], dram["l0_cn_b4"][:])
        dma(SMALL[:, 1:2], dram["l0_gc_b3"][:])
        dma(SMALL[:, 2:3], dram["l0_ar"][:])
        t_params = p.tok(_slot_sem("init"))
        t_consts = p.tok(_slot_sem("initg"))
        t_id = t_params
        b2c = BIAS_MAP["l1_b2"]
        t_hb2 = dve(lambda e: e.tensor_scalar(
            BIAS[:, 76:84], BIAS[:, b2c:b2c + 8], 0.5, None, ALU.mult),
            deps=[t_params])
        vdrain()

        def rsqrt_dve(out, in_, tmp, deps=()):
            oi = out.bitcast(I32)
            vi = in_.bitcast(I32)
            dve(lambda e: e.tensor_scalar(oi, vi, 1, None,
                                          ALU.logical_shift_right), deps=deps)
            vdrain()
            dve(lambda e: e.tensor_scalar(oi, oi, 0x5F3759DF, None, ALU.subtract))
            vdrain()
            dve(lambda e: e.tensor_scalar(oi, oi, -1, None, ALU.mult))
            t0 = None
            for _ in range(3):
                vdrain()
                dve(lambda e: e.tensor_tensor(tmp, out, out, ALU.mult))
                vdrain()
                dve(lambda e: e.tensor_tensor(tmp, tmp, in_, ALU.mult))
                vdrain()
                dve(lambda e: e.tensor_scalar(tmp, tmp, -0.5, 1.5, ALU.mult, ALU.add))
                vdrain()
                t0 = dve(lambda e: e.tensor_tensor(out, out, tmp, ALU.mult))
            return t0

        def hilo(dst_hi, dst_lo, src, deps=()):
            dve(lambda e: e.tensor_copy(dst_hi, src), deps=deps)
            vdrain()
            t2 = dve(lambda e: e.tensor_tensor(dst_lo, src, dst_hi, ALU.subtract))
            return t2

        _ln_state = {"bk": None}

        def ln_stats_seq(src, b_stat, dep_src=(), rms_fuse=False):
            TT = 512
            NT = T // TT
            t_dma = None
            for it in range(NT):
                sl = slice(it * TT, (it + 1) * TT)
                xin = src[:, :, sl]
                xbf = SC[:, 0:NCH * TT].rearrange("p (c t) -> p c t", c=NCH)
                sq = SC[:, NCH * TT:2 * NCH * TT].rearrange(
                    "p (c t) -> p c t", c=NCH)
                dsrc = list(dep_src) if it == 0 else [p.tok("s_pe")]
                t_cast = dve(lambda e, xbf=xbf, xin=xin: e.tensor_copy(xbf, xin),
                             deps=dsrc)
                t_sq = act(sq, xbf, AF.Square, deps=[t_cast])
                tsum = None
                for c in range(NCH):
                    tsum = mm(bank(b_stat, 0, TT)[0:1, :], ONESC[:, 0:1],
                              xbf[:, c, :],
                              start=(c == 0), stop=(c == NCH - 1),
                              deps=[t_cast, p.tok("s_act")] if c == 0 else (),
                              inc="s_pe" if c == NCH - 1 else None)
                tsq = None
                for c in range(NCH):
                    tsq = mm(bank(b_stat + 1, 0, TT)[0:1, :], ONESC[:, 0:1],
                             sq[:, c, :],
                             start=(c == 0), stop=(c == NCH - 1),
                             deps=[t_sq] if c == 0 else (),
                             inc="s_pe" if c == NCH - 1 else None)
                ev1 = acopy(STG[0:1, 0:TT], bank(b_stat, 0, TT)[0:1, :],
                            deps=[tsum, t_dma])
                ev2 = acopy(STG[0:1, TT:2 * TT], bank(b_stat + 1, 0, TT)[0:1, :],
                            deps=[tsq])
                dma(STT[:, 4 * it:4 * it + 4],
                    STG[0:1, 0:TT].rearrange("o (p c) -> o p c", p=128),
                    slot="lnst", deps=[ev1])
                t_dma = dma(STT[:, 16 + 4 * it:16 + 4 * it + 4],
                            STG[0:1, TT:2 * TT].rearrange("o (p c) -> o p c", p=128),
                            slot="lnst", deps=[ev2])
            C = lambda j: STT[:, j:j + 16]
            m, v, r, tmp, mr, tmp2, r2 = (C(32), C(48), C(64), C(80), C(96),
                                          C(112), C(128))
            dep0 = [t_dma, _ln_state["bk"]]
            dve(lambda e: e.tensor_scalar(m, C(0), 1.0 / D, None, ALU.mult),
                deps=dep0)
            dve(lambda e: e.tensor_scalar(v, C(16), 1.0 / D, None, ALU.mult))
            vdrain()
            dve(lambda e: e.tensor_tensor(tmp, m, m, ALU.mult))
            vdrain()
            dve(lambda e: e.tensor_tensor(v, v, tmp, ALU.subtract))
            vdrain()
            dve(lambda e: e.tensor_scalar(v, v, LN_EPS, None, ALU.add))
            vdrain()
            rsqrt_dve(r, v, tmp)
            if rms_fuse:
                vdrain()
                dve(lambda e: e.tensor_tensor(tmp2, r, r, ALU.mult))
                vdrain()
                dve(lambda e: e.tensor_scalar(v, v, -LN_EPS, None, ALU.add))
                vdrain()
                dve(lambda e: e.tensor_tensor(tmp2, tmp2, v, ALU.mult))
                vdrain()
                dve(lambda e: e.tensor_scalar(tmp2, tmp2, RMS_EPS, None, ALU.add))
                vdrain()
                rsqrt_dve(r2, tmp2, tmp)
                vdrain()
                dve(lambda e: e.tensor_tensor(r, r, r2, ALU.mult))
            vdrain()
            dve(lambda e: e.tensor_tensor(mr, m, r, ALU.mult))
            vdrain()
            hilo(STTB[:, 0:16], STTB[:, 16:32], r)
            t_hl = hilo(STTB[:, 32:48], STTB[:, 48:64], mr)
            vdrain()
            tbk = None
            for (q, hl, rm) in ((0, 0, 0), (1, 1, 0), (2, 0, 1), (3, 1, 1)):
                for it in range(NT):
                    tbk = dma(ROWB[hl:hl + 1, rm, it * TT:(it + 1) * TT]
                              .rearrange("o (p c) -> o p c", p=128),
                              STTB[:, 16 * q + 4 * it:16 * q + 4 * it + 4],
                              slot="lnbk", deps=[t_hl])
            _ln_state["bk"] = tbk
            return tbk

        def ln_bcast(it, b_bc, dep_rows, dep_banks=()):
            TT = 512
            sl = slice(it * TT, (it + 1) * TT)
            t_rb = mm(bank(b_bc, 0, TT), ONESR[:], ROWB[:, 0, sl],
                      start=True, stop=True,
                      deps=[dep_rows] + list(dep_banks), inc="s_pe")
            t_mrb = mm(bank(b_bc + 1, 0, TT), ONESR[:], ROWB[:, 1, sl],
                       start=True, stop=True, inc="s_pe")
            return t_rb, t_mrb

        def ln_apply(src, it, b_bc, dst, t_rb, t_mrb, split_gp=True, dep_dst=(),
                     dep_war=(), scale_vec=None):
            TT = 512
            sl = slice(it * TT, (it + 1) * TT)
            rb = bank(b_bc, 0, TT)
            mrb = bank(b_bc + 1, 0, TT)
            toks = []
            st_r = ST[:, 0:TT]
            st_mr = ST[:, TT:2 * TT]
            ta = acopy(st_r, rb, deps=[t_rb] + list(dep_war))
            tb = acopy(st_mr, mrb, deps=[t_mrb])
            t_xread = None
            for c in range(NCH):
                eng_gp = split_gp and (c >= NCH // 2)
                xin = src[:, c, sl]
                zout = dst[:, c, sl]
                tmp = ST[:, (2 + (0 if eng_gp else 1)) * TT:
                          (3 + (0 if eng_gp else 1)) * TT]
                if eng_gp:
                    gp(lambda e, tmp=tmp, xin=xin: e.tensor_tensor(
                        tmp, xin, st_r, ALU.mult),
                       deps=[ta, tb, t_xread] + list(dep_dst))
                    p.emit("gpsimd", lambda e: e.drain())
                    if scale_vec is None:
                        t2 = gp(lambda e, zout=zout, tmp=tmp: e.tensor_tensor(
                            zout, tmp, st_mr, ALU.subtract))
                    else:
                        tg = gp(lambda e, tmp=tmp: e.tensor_tensor(
                            tmp, tmp, st_mr, ALU.subtract))
                        t2 = dve(lambda e, zout=zout, tmp=tmp, c=c:
                                 e.tensor_scalar(zout, tmp,
                                                 bias_ap(scale_vec, c), None,
                                                 ALU.mult), deps=[tg])
                        vdrain()
                        t_xread = t2
                    p.emit("gpsimd", lambda e: e.drain())
                else:
                    dve(lambda e, tmp=tmp, xin=xin: e.tensor_tensor(
                        tmp, xin, st_r, ALU.mult), deps=[ta, tb] + list(dep_dst))
                    vdrain()
                    if scale_vec is None:
                        t2 = dve(lambda e, zout=zout, tmp=tmp: e.tensor_tensor(
                            zout, tmp, st_mr, ALU.subtract))
                    else:
                        dve(lambda e, tmp=tmp: e.tensor_tensor(
                            tmp, tmp, st_mr, ALU.subtract))
                        vdrain()
                        t2 = dve(lambda e, zout=zout, tmp=tmp, c=c:
                                 e.tensor_scalar(zout, tmp,
                                                 bias_ap(scale_vec, c), None,
                                                 ALU.mult))
                    vdrain()
                toks.append(t2)
            if split_gp:
                return [toks[NCH // 2 - 1], toks[-1]]
            return [toks[-1]]

        def load_stream(seq, dep_st=()):
            evs = []
            for tt in range(T // 128):
                half = tt % 2
                tile = ST[:, half * 1024:(half + 1) * 1024]
                dep_war = [("s_pe", pe_after_tile[tt - 2])] if tt >= 2 else []
                if tt < 2:
                    dep_war = dep_war + [("s_pe", p.counts["s_pe"])] + list(dep_st)
                t_ld = dma(tile, x_hbm[seq, tt * 128:(tt + 1) * 128, :],
                           slot="xin%d" % half, deps=dep_war)
                for c in range(NCH):
                    dst = PS[:, 1024 * half + 128 * c: 1024 * half + 128 * c + 128]
                    deps = [t_ld, t_id] if c == 0 else ()
                    if tt >= 2 and c == 0:
                        deps = list(deps) + [evs[tt - 2]]
                    mmt(dst, tile[:, c * 128:(c + 1) * 128], deps=deps,
                        inc="s_pe" if c == NCH - 1 else None)
                pe_after_tile.append(p.counts["s_pe"])
                t_pe = p.tok("s_pe")
                dst = X[:, :, tt * 128:(tt + 1) * 128]
                src = PS[:, half * 1024: half * 1024 + 1024].rearrange(
                    "p (c t) -> p c t", c=NCH)
                evs.append(dve(lambda e, dst=dst, src=src: e.tensor_copy(dst, src),
                               deps=[t_pe]))
            return evs

        def store_stream(seq, deps):
            sts = []
            evs = []
            for tt in range(T // 128):
                half = tt % 2
                for c in range(NCH):
                    dst = PS[:, 2048 + 1024 * half + 128 * c:
                             2048 + 1024 * half + 128 * c + 128]
                    dps = list(deps) if c == 0 else []
                    if tt >= 2 and c == 0:
                        dps += [evs[tt - 2]]
                    mmt(dst, X[:, c, tt * 128:(tt + 1) * 128], deps=dps,
                        inc="s_pe" if c == NCH - 1 else None)
                t_pe = p.tok("s_pe")
                tile = ST[:, half * 1024:(half + 1) * 1024]
                src = PS[:, 2048 + half * 1024: 2048 + half * 1024 + 1024]
                dep_war = [sts[tt - 2]] if tt >= 2 else []
                evs.append(acopy(tile, src, deps=[t_pe] + dep_war))
                sts.append(store(y_hbm[seq, tt * 128:(tt + 1) * 128, :], tile,
                                 slot="st%d" % half, deps=[evs[-1]]))
            return sts

        def ffn0(seq, dep_in, ncycles=3, lnonly=False):
            TT = 512
            NT = T // TT
            phase0 = [("s_act", p.counts["s_act"]), ("s_pe", p.counts["s_pe"]),
                      ("s_dve", p.counts["s_dve"]), ("s_gp", p.counts["s_gp"])]
            wtoks = load_ffn0_weights(deps=phase0)
            vdrain()
            t_rows = ln_stats_seq(X, 4, dep_src=list(dep_in) + phase0)
            prev_apply = []
            for it in range(NT):
                t_rb, t_mrb = ln_bcast(it, 6, t_rows,
                                       dep_banks=prev_apply[-2:])
                prev_apply = ln_apply(X, it, 6, O, t_rb, t_mrb,
                                      dep_war=prev_apply[-2:])
            t_ln = (p.tok("s_dve"), p.tok("s_gp"))
            if lnonly:
                for it in range(NT):
                    sl = slice(it * TT, (it + 1) * TT)
                    dve(lambda e, sl=sl: e.tensor_copy(X[:, :, sl], O[:, :, sl]),
                        deps=[t_ln[0], t_ln[1]])
                    vdrain()
                return (p.tok("s_dve"), p.tok("s_gp"))
            g32 = ST[:, 0:NCH]
            t_g = dve(lambda e: e.tensor_reduce(
                g32, O[:, :, :], mybir.AxisListType.X, ALU.add),
                deps=[t_ln[0], t_ln[1]])
            vdrain()
            g16 = GCH[:, 0:NCH]
            t_g16 = dve(lambda e: e.tensor_scalar(g16, g32, 1.0 / T, None, ALU.mult))
            for j in range(4):
                for c in range(NCH):
                    mm(PS[:, 512 * 7 + j: 512 * 7 + j + 1],
                       wa_tile("gc_w1", c, j * 128, 128),
                       g16[:, c:c + 1],
                       start=(c == 0), stop=(c == NCH - 1),
                       deps=[t_g16, wtoks] if c == 0 else (),
                       inc="s_pe" if (j == 3 and c == NCH - 1) else None)
            t1 = p.tok("s_pe")
            for j in range(4):
                act(GCH[:, 8 + j:9 + j], PS[:, 512 * 7 + j:512 * 7 + j + 1],
                    AF.Gelu, bias=bias_ap("l0_gc_b1", j), deps=[t1])
            t_h1g = p.tok("s_act")
            for j2 in range(2):
                for c in range(4):
                    mm(PS[:, 512 * 7 + 16 + j2: 512 * 7 + 17 + j2],
                       wa_tile("gc_w2", c, j2 * 128, 128),
                       GCH[:, 8 + c:9 + c],
                       start=(c == 0), stop=(c == 3),
                       deps=[t_h1g] if c == 0 else (),
                       inc="s_pe" if (j2 == 1 and c == 3) else None)
            t2 = p.tok("s_pe")
            for j2 in range(2):
                act(GCH[:, 12 + j2:13 + j2],
                    PS[:, 512 * 7 + 16 + j2:512 * 7 + 17 + j2],
                    AF.Gelu, bias=bias_ap("l0_gc_b2", j2), deps=[t2])
            t_h2g = p.tok("s_act")
            for c in range(2):
                mm(PS[0:1, 512 * 7 + 24:512 * 7 + 25],
                   wa_tile("gc_w3", c, 0, 1), GCH[:, 12 + c:13 + c],
                   start=(c == 0), stop=(c == 1),
                   deps=[t_h2g] if c == 0 else (),
                   inc="s_pe" if c == 1 else None)
            t3 = p.tok("s_pe")
            tsm = dve(lambda e: e.tensor_scalar(SMALL[:, 8:9], SMALL[:, 1:2],
                                                0.5, None, ALU.mult),
                      deps=[t_params])
            t_tg = act(SMALL[:, 9:10], PS[0:1, 512 * 7 + 24:512 * 7 + 25],
                       AF.Tanh, bias=SMALL[:, 8:9], scale=0.5, deps=[t3, tsm])
            dve(lambda e: e.tensor_tensor(SMALL[:, 10:11], SMALL[:, 2:3],
                                          SMALL[:, 9:10], ALU.mult), deps=[t_tg])
            vdrain()
            dve(lambda e: e.tensor_scalar(SMALL[:, 10:11], SMALL[:, 10:11],
                                          0.7, 0.1, ALU.mult, ALU.add))
            t_c0 = dve(lambda e: e.tensor_scalar(SMALL[:, 11:12], SMALL[:, 2:3],
                                                 0.3, None, ALU.mult))
            vdrain()
            t_b4 = dve(lambda e: e.tensor_scalar(SMALL[:, 12:13], SMALL[:, 0:1],
                                                 0.5, None, ALU.mult))

            ev_prev = {}
            for cyc in range(ncycles):
                for it in range(NT):
                    sl = slice(it * TT, (it + 1) * TT)
                    obf = SC[:, 0:NCH * TT].rearrange("p (c t) -> p c t", c=NCH)
                    dep_o = [("s_dve", p.counts["s_dve"]),
                             ("s_gp", p.counts["s_gp"]),
                             ("s_pe", p.counts["s_pe"])]
                    t_obf = dve(lambda e, obf=obf, sl=sl: e.tensor_copy(
                        obf, O[:, :, sl]), deps=dep_o)
                    h1 = SC[:, NCH * TT:2 * NCH * TT].rearrange(
                        "p (c t) -> p c t", c=NCH)
                    for j in range(NCH):
                        bi = j % 2
                        dep0 = [t_obf, wtoks] if j == 0 else []
                        if ev_prev.get(("cn1", bi)) is not None:
                            dep0.append(ev_prev[("cn1", bi)])
                        for c in range(NCH):
                            mm(bank(bi, 0, TT), wa_tile("cn_w1", c, j * 128, 128),
                               obf[:, c, :], start=(c == 0), stop=(c == NCH - 1),
                               deps=dep0 if c == 0 else (),
                               inc="s_pe" if c == NCH - 1 else None)
                        ev = act(h1[:, j, :], bank(bi, 0, TT), AF.Gelu,
                                 bias=bias_ap("l0_cn_b1", j),
                                 deps=[p.tok("s_pe")])
                        ev_prev[("cn1", bi)] = ev
                    h2 = SC[:, 0:4 * TT].rearrange("p (c t) -> p c t", c=4)
                    for j in range(4):
                        bi = 2 + (j % 2)
                        dep0 = []
                        if j == 0:
                            dep0 += [ev_prev[("cn1", 0)], ev_prev[("cn1", 1)]]
                        if ev_prev.get(("b", bi)) is not None:
                            dep0.append(ev_prev[("b", bi)])
                        for c in range(NCH):
                            mm(bank(bi, 0, TT), wa_tile("cn_w2", c, j * 128, 128),
                               h1[:, c, :], start=(c == 0), stop=(c == NCH - 1),
                               deps=dep0 if c == 0 else (),
                               inc="s_pe" if c == NCH - 1 else None)
                        ev = act(h2[:, j, :], bank(bi, 0, TT), AF.Gelu,
                                 bias=bias_ap("l0_cn_b2", j),
                                 deps=[p.tok("s_pe")])
                        ev_prev[("b", bi)] = ev
                    h3 = SC[:, 4 * TT:6 * TT].rearrange("p (c t) -> p c t", c=2)
                    for j in range(2):
                        bi = 2 + (j % 2)
                        dep0 = []
                        if j == 0:
                            dep0 += [ev_prev[("b", 2)], ev_prev[("b", 3)]]
                        if ev_prev.get(("b", bi)) is not None:
                            dep0.append(ev_prev[("b", bi)])
                        for c in range(4):
                            mm(bank(bi, 0, TT), wa_tile("cn_w3", c, j * 128, 128),
                               h2[:, c, :], start=(c == 0), stop=(c == 3),
                               deps=dep0 if c == 0 else (),
                               inc="s_pe" if c == 3 else None)
                        ev = act(h3[:, j, :], bank(bi, 0, TT), AF.Gelu,
                                 bias=bias_ap("l0_cn_b3", j),
                                 deps=[p.tok("s_pe")])
                        ev_prev[("b", bi)] = ev
                    dep0 = [ev_prev[("b", 2)], ev_prev[("b", 3)]]
                    for c in range(2):
                        mm(bank(2, 0, TT)[0:1, :],
                           wa_tile("cn_w4", c, 0, 1), h3[:, c, :],
                           start=(c == 0), stop=(c == 1),
                           deps=dep0 if c == 0 else (),
                           inc="s_pe" if c == 1 else None)
                    t_cn4 = p.tok("s_pe")
                    crow = STG[0:1, 0:TT]
                    t_tf = act(crow, bank(2, 0, TT)[0:1, :], AF.Tanh,
                               bias=SMALL[:, 12:13], scale=0.5,
                               deps=[t_cn4, t_b4, t_c0])
                    ev_prev[("b", 2)] = t_tf
                    t_c = dve(lambda e: e.tensor_scalar(
                        crow, crow, SMALL[:, 11:12], SMALL[:, 10:11],
                        ALU.mult, ALU.add), deps=[t_tf])
                    vdrain()
                    t_hilo = hilo(CROWB[0:1, 0:TT], CROWB[0:1, TT:2 * TT], crow)
                    dep_cb = [t_hilo]
                    if ev_prev.get(("cb",)) is not None:
                        dep_cb.append(ev_prev[("cb",)])
                    mm(bank(4, 0, TT), ONESR[0:1, :], CROWB[0:1, 0:TT],
                       start=True, stop=False, deps=dep_cb)
                    t_cb = mm(bank(4, 0, TT), ONESR[0:1, :], CROWB[0:1, TT:2 * TT],
                              start=False, stop=True, inc="s_pe")
                    cb_st = ST[:, 0:TT]
                    t_cbst = acopy(cb_st, bank(4, 0, TT),
                                   deps=[t_cb, t_ln[0], t_ln[1],
                                         ev_prev.get(("tw_dve",)),
                                         ev_prev.get(("tw_gp",))])
                    ev_prev[("cb",)] = t_cbst
                    tmp_a = ST[:, 1 * TT:2 * TT]
                    tmp_b = ST[:, 2 * TT:3 * TT]
                    tmp_g = ST[:, 3 * TT:4 * TT]
                    tmp_g2 = SC[:, 6 * TT:8 * TT].bitcast(F32)
                    def _sgn(j):
                        return ALU.add if j in (0, 1, 6, 7) else ALU.subtract
                    for g in (0, 1, 3):
                        j1, j2 = g, g + 4
                        dve(lambda e, j2=j2, sl=sl: e.tensor_tensor(
                            tmp_a, O[:, j2, sl], cb_st, ALU.mult),
                            deps=[t_cbst, t_obf])
                        dve(lambda e, j1=j1, sl=sl: e.tensor_tensor(
                            tmp_b, O[:, j1, sl], cb_st, ALU.mult))
                        vdrain()
                        dve(lambda e, j1=j1, sl=sl: e.tensor_tensor(
                            O[:, j1, sl], O[:, j1, sl], tmp_a, _sgn(j1)))
                        dve(lambda e, j2=j2, sl=sl: e.tensor_tensor(
                            O[:, j2, sl], O[:, j2, sl], tmp_b, _sgn(j2)))
                        vdrain()
                    j1, j2 = 2, 6
                    gp(lambda e, j2=j2, sl=sl: e.tensor_tensor(
                        tmp_g, O[:, j2, sl], cb_st, ALU.mult),
                        deps=[t_cbst, t_obf, ev_prev.get(("b", 2)),
                              ev_prev.get(("b", 3))])
                    gp(lambda e, j1=j1, sl=sl: e.tensor_tensor(
                        tmp_g2, O[:, j1, sl], cb_st, ALU.mult))
                    p.emit("gpsimd", lambda e: e.drain())
                    gp(lambda e, j1=j1, sl=sl: e.tensor_tensor(
                        O[:, j1, sl], O[:, j1, sl], tmp_g, _sgn(j1)))
                    gp(lambda e, j2=j2, sl=sl: e.tensor_tensor(
                        O[:, j2, sl], O[:, j2, sl], tmp_g2, _sgn(j2)))
                    p.emit("gpsimd", lambda e: e.drain())
                    ev_prev[("tw_dve",)] = p.tok("s_dve")
                    ev_prev[("tw_gp",)] = p.tok("s_gp")
            for it in range(NT):
                sl = slice(it * TT, (it + 1) * TT)
                dep_o = [("s_dve", p.counts["s_dve"]), ("s_gp", p.counts["s_gp"])]
                for c in range(NCH):
                    if c % 2 == 0:
                        dve(lambda e, c=c, sl=sl: e.scalar_tensor_tensor(
                            X[:, c, sl], O[:, c, sl], 0.5, X[:, c, sl],
                            ALU.mult, ALU.add), deps=dep_o if c == 0 else ())
                    else:
                        gtmp = ST[:, 0:TT]
                        gp(lambda e, c=c, sl=sl, gtmp=gtmp: e.tensor_tensor(
                            gtmp, O[:, c, sl], HALFT[:], ALU.mult),
                           deps=dep_o if c == 1 else ())
                        p.emit("gpsimd", lambda e: e.drain())
                        gp(lambda e, c=c, sl=sl, gtmp=gtmp: e.tensor_tensor(
                            X[:, c, sl], X[:, c, sl], gtmp, ALU.add))
                        p.emit("gpsimd", lambda e: e.drain())
            return (p.tok("s_dve"), p.tok("s_gp"))

        W_A0 = {"v": 0, "wq": 16640, "wo": 0}

        def attn0(seq, dep_in):
            TT = 512
            NT = T // TT
            phase0 = [("s_act", p.counts["s_act"]), ("s_pe", p.counts["s_pe"]),
                      ("s_dve", p.counts["s_dve"]), ("s_gp", p.counts["s_gp"])]
            zb = Ob[:, :, 0:T]
            ab = Ob[:, :, T:2 * T]
            vdrain()
            t_rows = ln_stats_seq(X, 4, dep_src=list(dep_in) + phase0)
            prev_apply = []
            for it in range(NT):
                t_rb, t_mrb = ln_bcast(it, 6, t_rows, dep_banks=prev_apply[-2:])
                prev_apply = ln_apply(X, it, 6, zb, t_rb, t_mrb,
                                      dep_war=prev_apply[-2:])
            t_ln = (p.tok("s_dve"), p.tok("s_gp"))

            ev_v = {}
            for half in range(2):
                wvh = SC[:, 0:NCH * TT].rearrange("p (c w) -> p c w", c=NCH)
                t_wv = dma(wvh, dram["l0_wv"][:, 512 * half:512 * (half + 1)]
                           .rearrange("(c p) w -> p c w", p=128),
                           slot="wv0", queue="gp",
                           deps=[t_ln[0], t_ln[1], ("s_pe", p.counts["s_pe"])])
                for tt in range(16):
                    bk = (tt % 4)
                    dep0 = [t_wv]
                    if ev_v.get(bk) is not None:
                        dep0.append(ev_v[bk])
                    for c in range(NCH):
                        mm(bank(bk, 0, TT),
                           zb[:, c, tt * 128:(tt + 1) * 128],
                           wvh[:, c, :],
                           start=(c == 0), stop=(c == NCH - 1),
                           deps=dep0 if c == 0 else (),
                           inc="s_pe" if c == NCH - 1 else None)
                    dst = WA[:, 1040 * tt + 65 * 8 * half:
                             1040 * tt + 65 * 8 * half + 65 * 8]
                    dst = dst.rearrange("p (h d) -> p h d", h=8)[:, :, 0:64]
                    src = bank(bk, 0, TT).rearrange("p (h d) -> p h d", h=8)
                    ev_v[bk] = acopy(dst, src, deps=[p.tok("s_pe")])
            vap = WA[:, 64:64 + 256 * 65].rearrange(
                "p (n o) -> p n o", n=256, o=65)[:, :, 0:1]
            t_vo = dve(lambda e: e.memset(vap, 1.0),
                       deps=[ev_v[0], ev_v[1], ev_v[2], ev_v[3]])
            t_vo2 = t_vo
            vdrain()

            t_zevac = {}
            t_zdma = None
            t_yevac = None
            t_zev32 = None
            t_tydma = None
            for pj in range(8):
                dep_pair = [("s_act", p.counts["s_act"]),
                            ("s_pe", p.counts["s_pe"])]
                wqp = WA[:, W_A0["wq"]:W_A0["wq"] + 1024].rearrange(
                    "p (c w) -> p c w", c=NCH)
                q2 = SC[:, 0:2048]
                k2 = SC[:, 2048:4096]
                t_wq = dma(wqp, dram["l0_wq"][:, 128 * pj:128 * (pj + 1)]
                           .rearrange("(c p) w -> p c w", p=128),
                           slot="wqp", queue="gp", deps=dep_pair)
                for qt in range(NT):
                    bk = qt % 2
                    dep0 = [t_wq] + dep_pair
                    if t_zevac.get(("q", bk)) is not None:
                        dep0.append(t_zevac[("q", bk)])
                    for c in range(NCH):
                        mm(bank(bk, 0, TT), wqp[:, c, :],
                           zb[:, c, qt * TT:(qt + 1) * TT],
                           start=(c == 0), stop=(c == NCH - 1),
                           deps=dep0 if c == 0 else (),
                           inc="s_pe" if c == NCH - 1 else None)
                    t_zevac[("q", bk)] = acopy(
                        q2[:, qt * TT:(qt + 1) * TT], bank(bk, 0, TT),
                        deps=[p.tok("s_pe")])
                t_wk = dma(wqp, dram["l0_wk"][:, 128 * pj:128 * (pj + 1)]
                           .rearrange("(c p) w -> p c w", p=128),
                           slot="wqp", queue="gp", deps=[p.tok("s_pe")])
                for qt in range(NT):
                    bk = 2 + qt % 2
                    dep0 = [t_wk]
                    if t_zevac.get(("k", bk)) is not None:
                        dep0.append(t_zevac[("k", bk)])
                    for c in range(NCH):
                        mm(bank(bk, 0, TT), wqp[:, c, :],
                           zb[:, c, qt * TT:(qt + 1) * TT],
                           start=(c == 0), stop=(c == NCH - 1),
                           deps=dep0 if c == 0 else (),
                           inc="s_pe" if c == NCH - 1 else None)
                    t_zevac[("k", bk)] = acopy(
                        k2[:, qt * TT:(qt + 1) * TT], bank(bk, 0, TT),
                        deps=[p.tok("s_pe")])
                t_qk = p.tok("s_act")

                psq = SC[:, 4096:6144]
                t_red = None
                for wi, buf in ((0, q2), (1, k2)):
                    t_sq = act(psq, buf, AF.Square,
                               deps=[t_qk, ("s_pe", p.counts["s_pe"])])
                    for qt in range(NT):
                        mm(bank(0, 0, TT)[0:2, :], ONESH[:],
                           psq[:, qt * TT:(qt + 1) * TT],
                           start=True, stop=True,
                           deps=[t_sq] + ([t_red] if t_red else []),
                           inc="s_pe")
                        t_red = dve(lambda e, qt=qt: e.reduce_max(
                            BSC[0:2, 0:1] if qt == 0 else BSC[0:2, 1:2],
                            bank(0, 0, TT)[0:2, :],
                            mybir.AxisListType.X), deps=[p.tok("s_pe")])
                        vdrain()
                        if qt > 0:
                            t_red = dve(lambda e: e.tensor_tensor(
                                BSC[0:2, 0:1], BSC[0:2, 0:1], BSC[0:2, 1:2],
                                ALU.max))
                            vdrain()
                    dve(lambda e, wi=wi: e.tensor_copy(
                        BSC[0:2, 2 + wi:3 + wi], BSC[0:2, 0:1]))
                    vdrain()
                dve(lambda e: e.tensor_tensor(BSC[0:2, 4:5], BSC[0:2, 2:3],
                                              BSC[0:2, 3:4], ALU.mult))
                vdrain()
                rsqrt_dve(BSC[0:2, 5:6], BSC[0:2, 4:5], BSC[0:2, 6:7])
                vdrain()
                dve(lambda e: e.tensor_tensor(BSC[0:2, 5:6], BSC[0:2, 5:6],
                                              BSC[0:2, 4:5], ALU.mult))
                vdrain()
                t_bnd = dve(lambda e: e.tensor_scalar(
                    BSCB[0:2, 0:1], BSC[0:2, 5:6], -1.0 / 32.0, None, ALU.mult))
                mm(bank(0, 0, 2)[0:1, :], BSCB[0:2, 0:1], ID2[:],
                   start=True, stop=True, deps=[t_bnd, t_consts], inc="s_pe")
                t_br = acopy(BROW[0:1, 0:2], bank(0, 0, 2)[0:1, :],
                             deps=[p.tok("s_pe")])
                mm(bank(0, 0, 2), ONESR[0:1, :], BROW[0:1, 0:2],
                   start=True, stop=True, deps=[t_br], inc="s_pe")
                t_bb = acopy(BNDS[:, 0:2], bank(0, 0, 2), deps=[p.tok("s_pe")])

                pb = SC[:, 4096:6144]
                for hh in range(2):
                    h = 2 * pj + hh
                    yps = PS[:, 2048:4096]
                    t_exp = {}
                    t_pv = None
                    t_scA = t_scB = None
                    for sc in range(16):
                        for qt in range(NT):
                            dep0 = []
                            if qt == 0:
                                dep0 = [t_bb, t_qk]
                                if t_exp.get(0) is not None:
                                    dep0.append(t_exp[0])
                            if qt == 2 and t_exp.get(1) is not None:
                                dep0 = [t_exp[1]]
                            mm(bank(qt, 0, TT),
                               k2[64 * hh:64 * (hh + 1),
                                  sc * 128:(sc + 1) * 128],
                               q2[64 * hh:64 * (hh + 1),
                                  qt * TT:(qt + 1) * TT],
                               start=True, stop=True, deps=dep0,
                               inc="s_pe" if qt in (1, 3) else None)
                            if qt == 1:
                                t_scA = p.tok("s_pe")
                            if qt == 3:
                                t_scB = p.tok("s_pe")
                        dep_pv = [t_pv] if t_pv is not None else []
                        t_exp[0] = act(pb[:, 0:1024], PS[:, 0:1024], AF.Exp,
                                       bias=BNDS[:, hh:hh + 1], scale=1 / 32.0,
                                       deps=[t_scA] + dep_pv)
                        t_exp[1] = act(pb[:, 1024:2048], PS[:, 1024:2048],
                                       AF.Exp, bias=BNDS[:, hh:hh + 1],
                                       scale=1 / 32.0, deps=[t_scB])
                        for qt in range(NT):
                            dep0 = ()
                            if qt == 0:
                                dep0 = [t_exp[0], t_vo, t_vo2]
                                if sc == 0 and t_yevac is not None:
                                    dep0 = dep0 + [t_yevac, t_zev32]
                                if sc == 0 and t_zdma is not None:
                                    dep0 = dep0 + [t_zdma]
                            elif qt == 2:
                                dep0 = [t_exp[1]]
                            mm(yps[0:65, qt * TT:(qt + 1) * TT],
                               WA[:, 1040 * sc + 65 * h:
                                  1040 * sc + 65 * h + 65],
                               pb[:, qt * TT:(qt + 1) * TT],
                               start=(sc == 0), stop=(sc == 15),
                               deps=dep0,
                               inc="s_pe" if qt == 3 else None)
                        t_pv = p.tok("s_pe")
                    ych = h // 2
                    ypart = 64 * (h % 2)
                    if ypart == 0:
                        t_yevac = acopy(ab[:, ych, :][0:64, :], yps[0:64, :],
                                        deps=[t_pv])
                    else:
                        ty = SC[:, 6144:8192][0:64, :]
                        t_ty = acopy(ty, yps[0:64, :], deps=[t_pv, t_tydma])
                        t_tydma = dma(ab[:, ych, :][64:128, :], ty,
                                      slot="tydma", deps=[t_ty])
                        t_yevac = t_ty
                    t_zev32 = acopy(ST[64:65, :], yps[64:65, :],
                                    deps=[t_pv, t_zdma])
                    t_zdma = dma(ZT[:, 0, 16 * h:16 * (h + 1)],
                                 ST[64:65, :].rearrange("o (q c) -> o q c",
                                                        q=128),
                                 slot="zdma", deps=[t_zev32])

            t_zr = dve(lambda e: e.reciprocal(ZT[:, 1, :], ZT[:, 0, :]),
                       deps=[t_zdma, t_tydma, p.tok(_slot_sem("zbk"))])
            vdrain()
            t_hl = hilo(ZTB[:, 0, :], ZTB[:, 1, :], ZT[:, 1, :])
            vdrain()
            t_zb = None
            for h in range(HEADS):
                for hl in range(2):
                    t_zb = dma(ZRH[h + 16 * hl:h + 16 * hl + 1, :]
                               .rearrange("o (q c) -> o q c", q=128),
                               ZTB[:, hl, 16 * h:16 * (h + 1)],
                               slot="zbk", deps=[t_hl])
            t_wo = dma(WA[:, 0:8192].rearrange("p (c w) -> p c w", c=NCH),
                       dram["l0_wo"][:].rearrange("(c p) w -> p c w", p=128),
                       slot="wo0", queue="gp", deps=[("s_pe", p.counts["s_pe"])])
            t_norm = {}
            for qt in range(NT):
                for och in range(NCH):
                    bk = och % 2
                    dep0 = [t_zb, t_consts]
                    if t_norm.get(bk) is not None:
                        dep0.append(t_norm[bk])
                    mm(bank(bk, 0, TT), PATZ[:, och, :],
                       ZRH[:, qt * TT:(qt + 1) * TT],
                       start=True, stop=True, deps=dep0, inc="s_pe")
                    t_norm[bk] = dve(
                        lambda e, och=och, qt=qt, bk=bk: e.tensor_tensor(
                            ab[:, och, qt * TT:(qt + 1) * TT],
                            ab[:, och, qt * TT:(qt + 1) * TT],
                            bank(bk, 0, TT), ALU.mult),
                        deps=[p.tok("s_pe")])
                    vdrain()
            t_abn = (p.tok("s_dve"),)

            wo = WA[:, 0:8192].rearrange("p (c w) -> p c w", c=NCH)
            ev_ep = {}
            for qt in range(NT):
                dep_b = [t_rows, ("s_act", p.counts["s_act"])]
                mm(bank(6, 0, TT), ONESR[:], ROWB[:, 0, qt * TT:(qt + 1) * TT],
                   start=True, stop=False, deps=dep_b)
                t_b6 = mm(bank(6, 0, TT), ONESR[0:1, :], ONESBF[0:1, :],
                          start=False, stop=True, inc="s_pe")
                t_b7 = mm(bank(7, 0, TT), ONESR[:],
                          ROWB[:, 1, qt * TT:(qt + 1) * TT],
                          start=True, stop=True, inc="s_pe")
                t_c6 = acopy(ST[:, 0:TT], bank(6, 0, TT), deps=[t_b6])
                t_c7 = acopy(ST[:, TT:2 * TT], bank(7, 0, TT), deps=[t_b7])
                for och in range(NCH):
                    bk = 2 + och % 2
                    dep0 = [t_abn[0], t_wo]
                    if ev_ep.get(bk) is not None:
                        dep0.append(ev_ep[bk])
                    for c in range(NCH):
                        mm(bank(bk, 0, TT), wo[:, c, 128 * och:128 * (och + 1)],
                           ab[:, c, qt * TT:(qt + 1) * TT],
                           start=(c == 0), stop=(c == NCH - 1),
                           deps=dep0 if c == 0 else (),
                           inc="s_pe" if c == NCH - 1 else None)
                    eng_gp = och >= 4
                    tpr = ST[:, (2 + (1 if eng_gp else 0)) * TT:
                             (3 + (1 if eng_gp else 0)) * TT]
                    t_t = act(tpr, bank(bk, 0, TT), AF.Identity,
                              bias=bias_ap("l0_bo", och),
                              deps=[p.tok("s_pe"),
                                    ev_ep.get("tg" if eng_gp else "td")])
                    xsl = X[:, och, qt * TT:(qt + 1) * TT]
                    if eng_gp:
                        gp(lambda e, tpr=tpr: e.tensor_tensor(
                            tpr, tpr, ST[:, TT:2 * TT], ALU.subtract),
                           deps=[t_t, t_c7, t_c6])
                        p.emit("gpsimd", lambda e: e.drain())
                        gp(lambda e, xsl=xsl: e.tensor_tensor(
                            xsl, xsl, ST[:, 0:TT], ALU.mult))
                        p.emit("gpsimd", lambda e: e.drain())
                        ev_ep[bk] = gp(lambda e, xsl=xsl, tpr=tpr: e.tensor_tensor(
                            xsl, xsl, tpr, ALU.add))
                        ev_ep["tg"] = ev_ep[bk]
                        p.emit("gpsimd", lambda e: e.drain())
                    else:
                        dve(lambda e, tpr=tpr, bk=bk: e.tensor_tensor(
                            tpr, tpr, bank(7, 0, TT), ALU.subtract),
                            deps=[t_t, t_b7])
                        vdrain()
                        dve(lambda e, xsl=xsl: e.tensor_tensor(
                            xsl, xsl, bank(6, 0, TT), ALU.mult), deps=[t_b6])
                        vdrain()
                        ev_ep[bk] = dve(lambda e, xsl=xsl, tpr=tpr: e.tensor_tensor(
                            xsl, xsl, tpr, ALU.add))
                        ev_ep["td"] = ev_ep[bk]
                        vdrain()
                p.wait("tensor", [ev_ep.get(2), ev_ep.get(3),
                                  ("s_gp", p.counts["s_gp"]),
                                  ("s_dve", p.counts["s_dve"])])
            return (p.tok("s_dve"), p.tok("s_gp"))

        def ffn1(seq, dep_in):
            TT = 512
            NT = T // TT
            phase0 = [("s_act", p.counts["s_act"]), ("s_pe", p.counts["s_pe"]),
                      ("s_dve", p.counts["s_dve"]), ("s_gp", p.counts["s_gp"])]
            zbw = WA[:, 0:NCH * T].rearrange("p (c t) -> p c t", c=NCH)
            w1h = Ob[:, :, 0:T]
            w2h = Ob[:, :, T:2 * T].rearrange(
                "p c (k w) -> p c k w", k=2)

            def w2t(kc, j):
                return w2h[:, kc // 2, kc % 2, 128 * j:128 * (j + 1)]
            vdrain()
            t_rows = ln_stats_seq(X, 4, dep_src=list(dep_in) + phase0)
            prev_apply = []
            for it in range(NT):
                t_rb, t_mrb = ln_bcast(it, 6, t_rows, dep_banks=prev_apply[-2:])
                prev_apply = ln_apply(X, it, 6, zbw, t_rb, t_mrb,
                                      dep_war=prev_apply[-2:])
            t_ln = (p.tok("s_dve"), p.tok("s_gp"))
            h1sb = SC[:, 0:8192].rearrange("p (c t) -> p c t", c=16)
            ev_h1 = {}
            ev_out = {}
            for half in range(2):
                dep_w = [t_ln[0], t_ln[1], ("s_pe", p.counts["s_pe"])]
                t_w1 = dma(w1h, dram["l1_w1"][:, 2048 * half:2048 * (half + 1)]
                           .rearrange("(c p) w -> p c w", p=128),
                           slot="w1h", queue="gp", deps=dep_w)
                src2 = dram["l1_w2"][2048 * half:2048 * (half + 1), :].rearrange(
                    "(c k p) w -> p c k w", p=128, k=2)
                dma(w2h[:, :, 0, :], src2[:, :, 0, :],
                    slot="w2h", queue="gp", deps=dep_w)
                t_w2 = dma(w2h[:, :, 1, :], src2[:, :, 1, :],
                           slot="w2h", queue="gp", deps=dep_w)
                for it in range(NT):
                    sl = slice(it * TT, (it + 1) * TT)
                    for j in range(16):
                        bk = j % 2
                        dep0 = [t_w1]
                        if ev_h1.get(bk) is not None:
                            dep0.append(ev_h1[bk])
                        for c in range(NCH):
                            mm(bank(bk, 0, TT),
                               w1h[:, c, 128 * j:128 * (j + 1)],
                               zbw[:, c, sl],
                               start=(c == 0), stop=(c == NCH - 1),
                               deps=dep0 if c == 0 else (),
                               inc="s_pe" if c == NCH - 1 else None)
                        ev_h1[bk] = act(h1sb[:, j, :], bank(bk, 0, TT),
                                        AF.Relu,
                                        bias=bias_ap("l1_b1", 16 * half + j),
                                        deps=[p.tok("s_pe")])
                    t_h1 = p.tok("s_act")
                    for j in range(NCH):
                        bk = 2 + j % 2
                        dep0 = [t_w2, t_h1]
                        if ev_out.get(bk) is not None:
                            dep0.append(ev_out[bk])
                        for c in range(16):
                            mm(bank(bk, 0, TT),
                               w2t(c, j),
                               h1sb[:, c, :],
                               start=(c == 0), stop=(c == 15),
                               deps=dep0 if c == 0 else (),
                               inc="s_pe" if c == 15 else None)
                        eng_gp = j % 2 == 1
                        tpr = ST[:, (0 + (1 if eng_gp else 0)) * TT:
                                 (1 + (1 if eng_gp else 0)) * TT]
                        if half == 0:
                            t_t = act(tpr, bank(bk, 0, TT), AF.Identity,
                                      bias=BIAS[:, 76 + j:77 + j], scale=0.5,
                                      deps=[p.tok("s_pe"),
                                            ev_out.get("g" if eng_gp else "d"),
                                            t_hb2])
                        else:
                            t_t = act(tpr, bank(bk, 0, TT), AF.Identity,
                                      scale=0.5,
                                      deps=[p.tok("s_pe"),
                                            ev_out.get("g" if eng_gp else "d")])
                        xsl = X[:, j, sl]
                        if eng_gp:
                            tok = gp(lambda e, xsl=xsl, tpr=tpr: e.tensor_tensor(
                                xsl, xsl, tpr, ALU.add), deps=[t_t])
                            p.emit("gpsimd", lambda e: e.drain())
                            ev_out["g"] = tok
                        else:
                            tok = dve(lambda e, xsl=xsl, tpr=tpr: e.tensor_tensor(
                                xsl, xsl, tpr, ALU.add), deps=[t_t])
                            vdrain()
                            ev_out["d"] = tok
                        ev_out[bk] = t_t
            return (p.tok("s_dve"), p.tok("s_gp"))

        def attn1(seq, dep_in, phases=("w", "l")):
            TT = 512
            NT = T // TT
            phase0 = [("s_act", p.counts["s_act"]), ("s_pe", p.counts["s_pe"]),
                      ("s_dve", p.counts["s_dve"]), ("s_gp", p.counts["s_gp"])]
            hb = Ob[:, :, 0:T]
            ab = Ob[:, :, T:2 * T]
            vdrain()
            t_rows = ln_stats_seq(X, 4, dep_src=list(dep_in) + phase0,
                                  rms_fuse=True)
            prev_apply = []
            for it in range(NT):
                t_rb, t_mrb = ln_bcast(it, 6, t_rows, dep_banks=prev_apply[-2:])
                prev_apply = ln_apply(X, it, 6, hb, t_rb, t_mrb,
                                      dep_war=prev_apply[-2:],
                                      scale_vec="l1_normw")
            t_ln = (p.tok("s_dve"), p.tok("s_gp"))

            ev_v = {}
            for half in range(2):
                wvh = SC[:, 0:NCH * TT].rearrange("p (c w) -> p c w", c=NCH)
                t_wv = dma(wvh, dram["l1_wv"][:, 512 * half:512 * (half + 1)]
                           .rearrange("(c p) w -> p c w", p=128),
                           slot="wv0", queue="gp",
                           deps=[t_ln[0], t_ln[1], ("s_pe", p.counts["s_pe"])])
                for tt in range(16):
                    bk = (tt % 4)
                    dep0 = [t_wv]
                    if ev_v.get(bk) is not None:
                        dep0.append(ev_v[bk])
                    for c in range(NCH):
                        mm(bank(bk, 0, TT),
                           hb[:, c, tt * 128:(tt + 1) * 128],
                           wvh[:, c, :],
                           start=(c == 0), stop=(c == NCH - 1),
                           deps=dep0 if c == 0 else (),
                           inc="s_pe" if c == NCH - 1 else None)
                    dst = WA[:, 1040 * tt + 65 * 8 * half:
                             1040 * tt + 65 * 8 * half + 65 * 8]
                    dst = dst.rearrange("p (h d) -> p h d", h=8)[:, :, 0:64]
                    src = bank(bk, 0, TT).rearrange("p (h d) -> p h d", h=8)
                    ev_v[bk] = acopy(dst, src, deps=[p.tok("s_pe")])
            vap = WA[:, 64:64 + 256 * 65].rearrange(
                "p (n o) -> p n o", n=256, o=65)[:, :, 0:1]
            t_vo = dve(lambda e: e.memset(vap, 1.0),
                       deps=[ev_v[0], ev_v[1], ev_v[2], ev_v[3]])
            vdrain()
            if "v" in phases:
                return (p.tok("s_dve"), p.tok("s_gp"))

            def vplus(sc, h):
                return WA[:, 1040 * sc + 65 * h:1040 * sc + 65 * h + 65]

            skip_w = "w" not in phases
            ZRHW = SC[0:32, 4096:6144]
            t_zevac = {}
            t_zdma = None
            t_tydma = None
            t_yevac = None
            t_zev32 = None
            for pj in (range(8) if not skip_w else ()):
                dep_pair = [("s_act", p.counts["s_act"]),
                            ("s_pe", p.counts["s_pe"])]
                wqp = WA[:, W_A0["wq"]:W_A0["wq"] + 1024].rearrange(
                    "p (c w) -> p c w", c=NCH)
                q2 = SC[:, 0:2048]
                k2 = SC[:, 2048:4096]
                t_wq = dma(wqp, dram["l1_wq"][:, 128 * pj:128 * (pj + 1)]
                           .rearrange("(c p) w -> p c w", p=128),
                           slot="wqp", queue="gp", deps=dep_pair)
                for qt in range(NT):
                    bk = qt % 2
                    dep0 = [t_wq] + dep_pair
                    if t_zevac.get(("q", bk)) is not None:
                        dep0.append(t_zevac[("q", bk)])
                    for c in range(NCH):
                        mm(bank(bk, 0, TT), wqp[:, c, :],
                           hb[:, c, qt * TT:(qt + 1) * TT],
                           start=(c == 0), stop=(c == NCH - 1),
                           deps=dep0 if c == 0 else (),
                           inc="s_pe" if c == NCH - 1 else None)
                    t_zevac[("q", bk)] = acopy(
                        q2[:, qt * TT:(qt + 1) * TT], bank(bk, 0, TT),
                        deps=[p.tok("s_pe")])
                t_wk = dma(wqp, dram["l1_wk"][:, 128 * pj:128 * (pj + 1)]
                           .rearrange("(c p) w -> p c w", p=128),
                           slot="wqp", queue="gp", deps=[p.tok("s_pe")])
                for qt in range(NT):
                    bk = 2 + qt % 2
                    dep0 = [t_wk]
                    if t_zevac.get(("k", bk)) is not None:
                        dep0.append(t_zevac[("k", bk)])
                    for c in range(NCH):
                        mm(bank(bk, 0, TT), wqp[:, c, :],
                           hb[:, c, qt * TT:(qt + 1) * TT],
                           start=(c == 0), stop=(c == NCH - 1),
                           deps=dep0 if c == 0 else (),
                           inc="s_pe" if c == NCH - 1 else None)
                    t_zevac[("k", bk)] = acopy(
                        k2[:, qt * TT:(qt + 1) * TT], bank(bk, 0, TT),
                        deps=[p.tok("s_pe")])
                t_qk = p.tok("s_act")

                psq = SC[:, 4096:6144]
                t_red = None
                for wi, buf in ((0, q2), (1, k2)):
                    t_sq = act(psq, buf, AF.Square,
                               deps=[t_qk, ("s_pe", p.counts["s_pe"])])
                    for qt in range(NT):
                        mm(bank(0, 0, TT)[0:2, :], ONESH[:],
                           psq[:, qt * TT:(qt + 1) * TT],
                           start=True, stop=True,
                           deps=[t_sq] + ([t_red] if t_red else []),
                           inc="s_pe")
                        t_red = dve(lambda e, qt=qt: e.reduce_max(
                            BSC[0:2, 0:1] if qt == 0 else BSC[0:2, 1:2],
                            bank(0, 0, TT)[0:2, :],
                            mybir.AxisListType.X), deps=[p.tok("s_pe")])
                        vdrain()
                        if qt > 0:
                            t_red = dve(lambda e: e.tensor_tensor(
                                BSC[0:2, 0:1], BSC[0:2, 0:1], BSC[0:2, 1:2],
                                ALU.max))
                            vdrain()
                    dve(lambda e, wi=wi: e.tensor_copy(
                        BSC[0:2, 2 + wi:3 + wi], BSC[0:2, 0:1]))
                    vdrain()
                dve(lambda e: e.tensor_tensor(BSC[0:2, 4:5], BSC[0:2, 2:3],
                                              BSC[0:2, 3:4], ALU.mult))
                vdrain()
                rsqrt_dve(BSC[0:2, 5:6], BSC[0:2, 4:5], BSC[0:2, 6:7])
                vdrain()
                dve(lambda e: e.tensor_tensor(BSC[0:2, 5:6], BSC[0:2, 5:6],
                                              BSC[0:2, 4:5], ALU.mult))
                vdrain()
                t_bnd = dve(lambda e: e.tensor_scalar(
                    BSCB[0:2, 0:1], BSC[0:2, 5:6], -1.0 / 8.0, None, ALU.mult))
                mm(bank(0, 0, 2)[0:1, :], BSCB[0:2, 0:1], ID2[:],
                   start=True, stop=True, deps=[t_bnd, t_consts], inc="s_pe")
                t_br = acopy(BROW[0:1, 0:2], bank(0, 0, 2)[0:1, :],
                             deps=[p.tok("s_pe")])
                mm(bank(0, 0, 2), ONESR[0:1, :], BROW[0:1, 0:2],
                   start=True, stop=True, deps=[t_br], inc="s_pe")
                t_bb = acopy(BNDS[:, 0:2], bank(0, 0, 2), deps=[p.tok("s_pe")])

                pw = SC[:, 4096:4864].rearrange("p (i t) -> p i t", i=3)
                for hh in range(2):
                    h = 2 * pj + hh
                    t_mask = None
                    for qt in range(8):
                        kcs = [(i, qt * 2 - 1 + i) for i in range(3)
                               if 0 <= qt * 2 - 1 + i < 16]
                        sdep = [t_bb, t_qk]
                        if t_mask is not None:
                            sdep.append(t_mask)
                        t_sc = None
                        for i, kc in kcs:
                            dst = (bank(0, 0, TT)[:, 256 * i:256 * (i + 1)]
                                   if i < 2 else bank(1, 0, 256))
                            t_sc = mm(dst,
                                      k2[64 * hh:64 * (hh + 1),
                                         kc * 128:(kc + 1) * 128],
                                      q2[64 * hh:64 * (hh + 1),
                                         qt * 256:(qt + 1) * 256],
                                      start=True, stop=True,
                                      deps=sdep, inc="s_pe")
                            sdep = []
                        t_exps = []
                        first_i = kcs[0][0]
                        for i, kc in kcs:
                            src = (bank(0, 0, TT)[:, 256 * i:256 * (i + 1)]
                                   if i < 2 else bank(1, 0, 256))
                            t_exps.append(act(pw[:, i, :], src, AF.Exp,
                                              bias=BNDS[:, hh:hh + 1],
                                              scale=1 / 8.0,
                                              deps=[t_sc] if i == first_i
                                              else ()))
                        ni = len(kcs)
                        t_mask = dve(
                            lambda e, first_i=first_i, ni=ni: e.tensor_tensor(
                                pw[:, first_i:first_i + ni, :],
                                pw[:, first_i:first_i + ni, :],
                                WMASK[:, first_i:first_i + ni, :], ALU.mult),
                            deps=t_exps + [t_consts])
                        vdrain()
                        ydst = PS[0:65, 2048 + 512 * (qt // 2) + 256 * (qt % 2):
                                  2048 + 512 * (qt // 2) + 256 * (qt % 2) + 256]
                        for ji, (i, kc) in enumerate(kcs):
                            dep0 = []
                            if ji == 0:
                                dep0 = [t_mask, t_vo]
                                if qt == 0 and t_yevac is not None:
                                    dep0 += [t_yevac, t_zev32]
                            mm(ydst, vplus(kc, h), pw[:, i, :],
                               start=(ji == 0), stop=(ji == len(kcs) - 1),
                               deps=dep0,
                               inc="s_pe" if ji == len(kcs) - 1 else None)
                        t_pv = p.tok("s_pe")
                    ych = h // 2
                    ypart = 64 * (h % 2)
                    yps = PS[:, 2048:4096]
                    if ypart == 0:
                        t_yevac = acopy(ab[:, ych, :][0:64, :], yps[0:64, :],
                                        deps=[t_pv])
                    else:
                        ty = SC[:, 6144:8192][0:64, :]
                        t_ty = acopy(ty, yps[0:64, :], deps=[t_pv, t_tydma])
                        t_tydma = dma(ab[:, ych, :][64:128, :], ty,
                                      slot="tydma", deps=[t_ty])
                        t_yevac = t_ty
                    t_zev32 = acopy(ST[64:65, :], yps[64:65, :],
                                    deps=[t_pv, t_zdma])
                    t_zdma = dma(ZT[:, 0, 16 * h:16 * (h + 1)],
                                 ST[64:65, :].rearrange("o (q c) -> o q c",
                                                        q=128),
                                 slot="zdma", deps=[t_zev32])

            if not skip_w:
                t_zr = dve(lambda e: e.reciprocal(ZT[:, 1, :], ZT[:, 0, :]),
                           deps=[t_zdma, t_tydma, p.tok(_slot_sem("zbk"))])
                vdrain()
                t_hl = hilo(ZTB[:, 0, :], ZTB[:, 1, :], ZT[:, 1, :])
                vdrain()
                t_zb = None
                for h in range(HEADS):
                    for hl in range(2):
                        t_zb = dma(ZRHW[h + 16 * hl:h + 16 * hl + 1, :]
                                   .rearrange("o (q c) -> o q c", q=128),
                                   ZTB[:, hl, 16 * h:16 * (h + 1)],
                                   slot="zbk", deps=[t_hl])
                t_norm = {}
                for qt in range(NT):
                    for och in range(NCH):
                        bk = och % 2
                        dep0 = [t_zb, t_consts, ("s_act", p.counts["s_act"])]
                        if t_norm.get(bk) is not None:
                            dep0.append(t_norm[bk])
                        mm(bank(bk, 0, TT), PATZ[:, och, :],
                           ZRHW[:, qt * TT:(qt + 1) * TT],
                           start=True, stop=True, deps=dep0, inc="s_pe")
                        t_norm[bk] = dve(
                            lambda e, och=och, qt=qt, bk=bk: e.tensor_tensor(
                                ab[:, och, qt * TT:(qt + 1) * TT],
                                ab[:, och, qt * TT:(qt + 1) * TT],
                                bank(bk, 0, TT), ALU.mult),
                            deps=[p.tok("s_pe")])
                        vdrain()
                t_abn = p.tok("s_dve")
                if "stop_norm" in phases:
                    p.wait("sync", [t_abn, ("s_pe", p.counts["s_pe"])])
                    return (p.tok("s_dve"), p.tok("s_gp"))
                Cc = lambda j: STT[:, j:j + 16]
                m1, v1, r1, tmp1, mr1 = (Cc(32), Cc(48), Cc(144), Cc(80),
                                         Cc(112))
                dve(lambda e: e.tensor_scalar(v1, v1, LN_EPS, None, ALU.add),
                    deps=[t_abn, ("s_gp", p.counts["s_gp"]),
                          ("s_act", p.counts["s_act"])])
                vdrain()
                rsqrt_dve(r1, v1, tmp1)
                vdrain()
                dve(lambda e: e.tensor_tensor(mr1, m1, r1, ALU.mult))
                vdrain()
                hilo(STTB[:, 0:16], STTB[:, 16:32], r1)
                t_hl1 = hilo(STTB[:, 32:48], STTB[:, 48:64], mr1)
                vdrain()
                tbk = None
                for (q, hl, rm) in ((0, 0, 0), (1, 1, 0), (2, 0, 1),
                                    (3, 1, 1)):
                    for it in range(NT):
                        tbk = dma(ROWB[hl:hl + 1, rm, it * TT:(it + 1) * TT]
                                  .rearrange("o (pp c) -> o pp c", pp=128),
                                  STTB[:, 16 * q + 4 * it:
                                       16 * q + 4 * it + 4],
                                  slot="lnbk", deps=[t_hl1])
                _ln_state["bk"] = tbk
                if "stop_rows" in phases:
                    p.wait("sync", [tbk, ("s_pe", p.counts["s_pe"]),
                                    ("s_dve", p.counts["s_dve"])])
                    return (p.tok("s_dve"), p.tok("s_gp"))
                ev_pr = {}
                for whalf in range(2):
                    w2v = SC[:, 0:4096].rearrange("p (c w) -> p c w", c=NCH)
                    t_w2 = dma(w2v, dram["l1_wout"]
                               [1024:2048, 512 * whalf:512 * (whalf + 1)]
                               .rearrange("(c p) w -> p c w", p=128),
                               slot="wqp", queue="gp",
                               deps=[t_abn, ("s_act", p.counts["s_act"]),
                                     ("s_pe", p.counts["s_pe"])])
                    for qt in range(NT):
                        dep_b = [tbk, ("s_act", p.counts["s_act"]),
                                 ("s_dve", p.counts["s_dve"]),
                                 ("s_gp", p.counts["s_gp"])]
                        mm(bank(6, 0, TT), ONESR[:],
                           ROWB[:, 0, qt * TT:(qt + 1) * TT],
                           start=True, stop=False, deps=dep_b)
                        t_b6 = mm(bank(6, 0, TT), ONESR[0:1, :],
                                  ONESBF[0:1, :],
                                  start=False, stop=True, inc="s_pe")
                        t_b7 = mm(bank(7, 0, TT), ONESR[:],
                                  ROWB[:, 1, qt * TT:(qt + 1) * TT],
                                  start=True, stop=True, inc="s_pe")
                        t_c6 = acopy(ST[:, 0:TT], bank(6, 0, TT),
                                     deps=[t_b6])
                        t_c7 = acopy(ST[:, TT:2 * TT], bank(7, 0, TT),
                                     deps=[t_b7])
                        for oj in range(4):
                            och = 4 * whalf + oj
                            bk = 2 + oj % 2
                            dep0 = [t_w2, t_abn]
                            if ev_pr.get(bk) is not None:
                                dep0.append(ev_pr[bk])
                            for c in range(NCH):
                                mm(bank(bk, 0, TT),
                                   w2v[:, c, 128 * oj:128 * (oj + 1)],
                                   ab[:, c, qt * TT:(qt + 1) * TT],
                                   start=(c == 0), stop=(c == NCH - 1),
                                   deps=dep0 if c == 0 else (),
                                   inc="s_pe" if c == NCH - 1 else None)
                            eng_gp = oj % 2 == 1
                            tpr = ST[:, (2 + (1 if eng_gp else 0)) * TT:
                                     (3 + (1 if eng_gp else 0)) * TT]
                            t_t = act(tpr, bank(bk, 0, TT), AF.Identity,
                                      deps=[p.tok("s_pe"),
                                            ev_pr.get("g" if eng_gp
                                                      else "d")])
                            ev_pr[bk] = t_t
                            if "wnoep" in phases:
                                ev_pr["g" if eng_gp else "d"] = t_t
                                continue
                            xsl = X[:, och, qt * TT:(qt + 1) * TT]
                            if eng_gp:
                                gp(lambda e, tpr=tpr: e.tensor_tensor(
                                    tpr, tpr, ST[:, TT:2 * TT],
                                    ALU.subtract),
                                   deps=[t_t, t_c7, t_c6])
                                p.emit("gpsimd", lambda e: e.drain())
                                gp(lambda e, xsl=xsl: e.tensor_tensor(
                                    xsl, xsl, ST[:, 0:TT], ALU.mult))
                                p.emit("gpsimd", lambda e: e.drain())
                                tok = gp(lambda e, xsl=xsl, tpr=tpr:
                                         e.tensor_tensor(xsl, xsl, tpr,
                                                         ALU.add))
                                ev_pr["g"] = tok
                                p.emit("gpsimd", lambda e: e.drain())
                            else:
                                dve(lambda e, tpr=tpr: e.tensor_tensor(
                                    tpr, tpr, bank(7, 0, TT), ALU.subtract),
                                    deps=[t_t, t_b7])
                                vdrain()
                                dve(lambda e, xsl=xsl: e.tensor_tensor(
                                    xsl, xsl, bank(6, 0, TT), ALU.mult),
                                    deps=[t_b6])
                                vdrain()
                                tok = dve(lambda e, xsl=xsl, tpr=tpr:
                                          e.tensor_tensor(xsl, xsl, tpr,
                                                          ALU.add))
                                ev_pr["d"] = tok
                                vdrain()
                        p.wait("tensor", [ev_pr.get("d"), ev_pr.get("g"),
                                          ("s_dve", p.counts["s_dve"]),
                                          ("s_gp", p.counts["s_gp"])])
            t_wend = (p.tok("s_dve"), p.tok("s_gp"))
            if "l" not in phases:
                return t_wend

            A_sb = SC[:, 0:512]
            kfp = SC[:, 4416:4672].rearrange("p (c f) -> p c f", c=16)
            kfv = SC[0:16, 512:2560]
            kqh = SC[:, 2560:3648].rearrange("p (c f) -> p c f", c=8)
            DSB = [SC[0:97, 3648 + 256 * g:3648 + 256 * g + 241]
                   for g in range(3)]
            QF0 = SC[0:17, 4736:6784]
            QF16 = SC[0:16, 4736:6784]
            qB = SC[0:96, 6784:7296]
            qC = SC[0:40, 7296:7808]
            kf_tm = SC[:, 7808:8080].rearrange("p (c f) -> p c f", c=16)
            tyL = WA[:, W_A0["wq"]:W_A0["wq"] + 512][0:64, :]
            pbuf = SC[:, 512:2560]

            def dsb_piece(g, piece):
                base = SC[0:97, 3648 + 256 * g:3648 + 256 * g + 241]
                if piece == "A":
                    return base[0:17, 0:65]
                if piece == "B":
                    return base[0:96, 80:145]
                return base[0:40, 176:241]

            def kvps(g, piece):
                col = 512 * 6 + 256 * g if g < 2 else 512 * 7
                if piece == "A":
                    return PS[0:17, col:col + 65]
                if piece == "B":
                    return PS[0:96, col + 80:col + 145]
                return PS[0:40, col + 176:col + 241]

            ZRH2 = SC[0:32, 6144:8192][:, 0:2048]
            t_lin = {}
            t_zdma = None
            t_tydma = None
            kfv = pbuf[0:16, 0:2048]
            dep_all = [("s_act", p.counts["s_act"]),
                       ("s_pe", p.counts["s_pe"]),
                       ("s_dve", p.counts["s_dve"]),
                       ("s_gp", p.counts["s_gp"])]
            t_onq = dve(lambda e: e.memset(QF0[:], 1.0), deps=dep_all)
            vdrain()
            t_ones = dve(lambda e: e.memset(kf_tm[:, :, 16:17], 1.0))
            vdrain()
            t_lin[("qf", 0)] = t_onq
            t_lin[("qf", 1)] = t_onq
            for pj in range(8):
                for hh in range(2):
                    h = 2 * pj + hh
                    dep_pair = [("s_act", p.counts["s_act"]),
                                ("s_pe", p.counts["s_pe"])]
                    wqp = WA[:, W_A0["wq"] + 512:W_A0["wq"] + 1536].rearrange(
                        "p (c w) -> p c w", c=NCH)
                    t_w = dma(wqp, dram["l1_wq"][:, 128 * pj:128 * (pj + 1)]
                              .rearrange("(c p) w -> p c w", p=128),
                              slot="wqp", queue="gp", deps=dep_pair)
                    for qt in range(NT):
                        bk = qt % 2
                        dep0 = [t_w] + dep_pair
                        if t_lin.get(("pb", bk)) is not None:
                            dep0.append(t_lin[("pb", bk)])
                        for c in range(NCH):
                            mm(bank(bk, 0, TT), wqp[:, c, :],
                               hb[:, c, qt * TT:(qt + 1) * TT],
                               start=(c == 0), stop=(c == NCH - 1),
                               deps=dep0 if c == 0 else (),
                               inc="s_pe" if c == NCH - 1 else None)
                        t_lin[("pb", bk)] = acopy(
                            pbuf[:, qt * TT:(qt + 1) * TT], bank(bk, 0, TT),
                            deps=[p.tok("s_pe")])
                    t_pb = p.tok("s_act")
                    for qt in range(NT):
                        mm(bank(4 + qt % 2, 0, TT)[0:16, :],
                           WQF[64 * hh:64 * hh + 64, :],
                           pbuf[64 * hh:64 * (hh + 1),
                                qt * TT:(qt + 1) * TT],
                           start=True, stop=True,
                           deps=[t_pb, t_consts, t_lin.get(("qf", qt % 2))],
                           inc="s_pe")
                        t_lin[("qf", qt % 2)] = acopy(
                            QF16[:, qt * TT:(qt + 1) * TT],
                            bank(4 + qt % 2, 0, TT)[0:16, :],
                            deps=[p.tok("s_pe")])
                    t_qf = p.tok("s_act")
                    t_w = dma(wqp, dram["l1_wk"][:, 128 * pj:128 * (pj + 1)]
                              .rearrange("(c p) w -> p c w", p=128),
                              slot="wqp", queue="gp",
                              deps=[p.tok("s_pe"), p.tok("s_act")])
                    for qt in range(NT):
                        bk = qt % 2
                        dep0 = [t_w, t_lin[("pb", bk)], t_qf]
                        for c in range(NCH):
                            mm(bank(bk, 0, TT), wqp[:, c, :],
                               hb[:, c, qt * TT:(qt + 1) * TT],
                               start=(c == 0), stop=(c == NCH - 1),
                               deps=dep0 if c == 0 else (),
                               inc="s_pe" if c == NCH - 1 else None)
                        t_lin[("pb", bk)] = acopy(
                            pbuf[:, qt * TT:(qt + 1) * TT], bank(bk, 0, TT),
                            deps=[p.tok("s_pe")])
                    t_pb = p.tok("s_act")
                    for tc in range(16):
                        mm(bank(4 + tc % 2, 0, 32),
                           pbuf[:, tc * 128:(tc + 1) * 128],
                           KFBD[:], start=True, stop=True,
                           deps=[t_pb, t_consts, t_lin.get(("kt", tc % 2))],
                           inc="s_pe")
                        acopy(kf_tm[:, tc, 0:16],
                              bank(4 + tc % 2, 0, 32)
                              [:, 16 * hh:16 * (hh + 1)],
                              deps=[p.tok("s_pe")])
                        a2 = p.emit("scalar", lambda e, tc=tc, hh=hh:
                                    e.activation(
                                        kfp[:, tc, :],
                                        bank(4 + tc % 2, 0, 32)
                                        [:, 16 * hh:16 * (hh + 1)],
                                        AF.Identity,
                                        scale=float(np.sqrt(0.5))),
                                    inc="s_act")
                        t_lin[("kt", tc % 2)] = a2
                    t_ktm = p.tok("s_act")

                    def build_kq(khalf, t_ktm=t_ktm):
                        sl8 = slice(8 * khalf, 8 * khalf + 8)
                        t_q1 = dve(lambda e: e.tensor_tensor(
                            kqh[:, :, 0:16], kfp[:, sl8, :], kfp[:, sl8, :],
                            ALU.mult),
                            deps=[t_ktm, t_lin.get(("kq",)),
                                  ("s_pe", p.counts["s_pe"])])
                        vdrain()
                        dve(lambda e: e.tensor_scalar(
                            kqh[:, :, 0:16], kqh[:, :, 0:16], 0.5, None,
                            ALU.mult))
                        vdrain()
                        for i in range(15):
                            off = 16 + 15 * i - (i * (i - 1)) // 2
                            nw = 15 - i
                            dve(lambda e, i=i, off=off, nw=nw: e.tensor_tensor(
                                kqh[:, :, off:off + nw],
                                kfp[:, sl8, i:i + 1].broadcast_to(
                                    (128, 8, nw)),
                                kfp[:, sl8, i + 1:16], ALU.mult))
                        vdrain()
                        tk = dve(lambda e: e.engine_nop())
                        t_lin[("kq",)] = tk
                        return tk

                    t_kq = build_kq(0)
                    t_lin[("kq",)] = t_kq
                    for qt in range(NT):
                        mm(bank(4 + qt % 2, 0, TT)[0:16, :],
                           WKF[64 * hh:64 * hh + 64, :],
                           pbuf[64 * hh:64 * (hh + 1),
                                qt * TT:(qt + 1) * TT],
                           start=True, stop=True,
                           deps=[t_ktm, t_kq, t_consts,
                                 t_lin.get(("qf", qt % 2))],
                           inc="s_pe")
                        t_lin[("qf", qt % 2)] = acopy(
                            kfv[:, qt * TT:(qt + 1) * TT],
                            bank(4 + qt % 2, 0, TT)[0:16, :],
                            deps=[p.tok("s_pe")])
                    t_kf = p.tok("s_act")
                    if "nostate" in phases:
                        continue
                    t_yev = None
                    t_qbc = None
                    t_zev32 = None
                    t_dev = None
                    qfh = QF0
                    for qt in range(4):
                        sl5 = slice(qt * TT, (qt + 1) * TT)
                        ybank = 2 + (qt % 2)
                        ydst = PS[0:65, 512 * ybank:512 * ybank + 512]
                        if qt == 2:
                            t_kq = build_kq(1)
                        dep_q = [t_kf, t_consts,
                                 ("s_act", p.counts["s_act"])]
                        if t_qbc is not None:
                            dep_q.append(t_qbc)
                        mm(PS[0:96, 512 * 4:512 * 4 + TT], SEL[:, 0, 0:96],
                           QF16[:, sl5], start=True, stop=True,
                           deps=dep_q, inc="s_pe")
                        tb1 = acopy(qB[:], PS[0:96, 512 * 4:512 * 4 + TT],
                                    deps=[p.tok("s_pe")])
                        mm(PS[0:96, 512 * 5:512 * 5 + TT], SEL[:, 1, 0:96],
                           QF16[:, sl5], start=True, stop=True,
                           deps=[tb1], inc="s_pe")
                        tq1 = dve(lambda e: e.tensor_tensor(
                            qB[:], qB[:], PS[0:96, 512 * 5:512 * 5 + TT],
                            ALU.mult), deps=[p.tok("s_pe"), tb1])
                        vdrain()
                        mm(PS[0:40, 512 * 4:512 * 4 + TT], SEL[:, 0, 96:136],
                           QF16[:, sl5], start=True, stop=True,
                           deps=[tq1], inc="s_pe")
                        tb2 = acopy(qC[:], PS[0:40, 512 * 4:512 * 4 + TT],
                                    deps=[p.tok("s_pe")])
                        mm(PS[0:40, 512 * 5:512 * 5 + TT], SEL[:, 1, 96:136],
                           QF16[:, sl5], start=True, stop=True,
                           deps=[tb2], inc="s_pe")
                        t_qbc = dve(lambda e: e.tensor_tensor(
                            qC[:], qC[:], PS[0:40, 512 * 5:512 * 5 + TT],
                            ALU.mult), deps=[p.tok("s_pe"), tb2])
                        vdrain()
                        if "st_qphi" in phases:
                            t_yev = t_qbc
                            t_zev32 = t_qbc
                            continue
                        started = False
                        dep_y0 = [t_qbc, t_vo, t_ones]
                        if t_yev is not None:
                            dep_y0.append(t_yev)
                        if t_dev is not None:
                            dep_y0.append(t_dev)
                        for g in (range(qt) if "st_nointer" not in phases
                                  else ()):
                            mm(ydst, dsb_piece(g, "A"), qfh[:, sl5],
                               start=(not started), stop=False, deps=dep_y0)
                            dep_y0 = []
                            started = True
                            if "st_ia" in phases:
                                continue
                            mm(ydst, dsb_piece(g, "B"), qB[:],
                               start=False, stop=False)
                            if "st_iab" in phases:
                                continue
                            mm(ydst, dsb_piece(g, "C"), qC[:],
                               start=False, stop=False)

                        for sj in range(4):
                            sa = 4 * qt + sj
                            t_u = mm(bank(0, 0, TT),
                                     kfv[:, sa * 128:(sa + 1) * 128],
                                     QF16[:, sl5],
                                     start=True, stop=True,
                                     deps=[t_lin.get(("A",)), t_kf] + dep_y0,
                                     inc="s_pe")
                            dep_y0 = []
                            t_A = act(A_sb, bank(0, 0, TT), AF.Square,
                                      bias=1.0, scale=0.5, deps=[t_u])
                            if sj > 0:
                                t_z0 = dve(lambda e, sj=sj: e.memset(
                                    A_sb[:, 0:128 * sj], 0.0), deps=[t_A])
                            else:
                                t_z0 = t_A
                            t_Am = dve(lambda e, sj=sj: e.tensor_tensor(
                                A_sb[:, 128 * sj:128 * (sj + 1)],
                                A_sb[:, 128 * sj:128 * (sj + 1)],
                                TRIL[:], ALU.mult),
                                deps=[t_z0, t_consts, t_A])
                            vdrain()
                            t_lin[("A",)] = t_Am
                            if "st_noy" not in phases:
                                mm(ydst, vplus(sa, h), A_sb,
                                   start=(not started), stop=(sj == 3),
                                   deps=[t_Am],
                                   inc="s_pe" if sj == 3 else None)
                                started = True
                        t_ymm = p.tok("s_pe")
                        if "st_noy" in phases:
                            t_yev = t_Am
                            t_zev32 = t_Am
                            continue
                        if qt < 3 and "st_nokv" not in phases:
                            dep_kv = [t_ktm, t_kq, t_ones]
                            if t_dev is not None:
                                dep_kv.append(t_dev)
                            for sj in range(4):
                                sa = 4 * qt + sj
                                kqc = sa % 8
                                if sj < 3:
                                    mm(kvps(qt, "B"), kqh[:, kqc, 0:96],
                                       vplus(sa, h),
                                       start=(sj == 0), stop=False,
                                       deps=dep_kv)
                                    dep_kv = []
                                    mm(kvps(qt, "A"), kf_tm[:, sa, :],
                                       vplus(sa, h), start=False, stop=False)
                                    mm(kvps(qt, "C"), kqh[:, kqc, 96:136],
                                       vplus(sa, h), start=False, stop=False)
                                else:
                                    mm(kvps(qt, "A"), kf_tm[:, sa, :],
                                       vplus(sa, h), start=False, stop=False)
                                    mm(kvps(qt, "C"), kqh[:, kqc, 96:136],
                                       vplus(sa, h), start=False, stop=False)
                                    mm(kvps(qt, "B"), kqh[:, kqc, 0:96],
                                       vplus(sa, h), start=False, stop=True,
                                       inc="s_pe")
                            col = 512 * 6 + 256 * qt if qt < 2 else 512 * 7
                            t_dev = acopy(SC[0:96, 3648 + 256 * qt:
                                             3648 + 256 * qt + 241],
                                          PS[0:96, col:col + 241],
                                          deps=[p.tok("s_pe")])
                        ych = h // 2
                        ypart = 64 * (h % 2)
                        ysrc = ydst
                        if ypart == 0:
                            t_yev = acopy(
                                ab[:, ych, qt * TT:(qt + 1) * TT][0:64, :],
                                ysrc[0:64, :], deps=[t_ymm])
                        else:
                            tt1 = acopy(tyL[:, 0:TT], ysrc[0:64, :],
                                        deps=[t_ymm, t_tydma])
                            t_tydma = dma(
                                ab[:, ych, qt * TT:(qt + 1) * TT][64:128, :],
                                tyL[:, 0:TT], slot="tydma", deps=[tt1])
                            t_yev = tt1
                        t_zev32 = acopy(
                            ST[64:65, qt * TT:(qt + 1) * TT],
                            ysrc[64:65, :],
                            deps=[t_ymm] + ([t_zdma] if qt == 0 else []))
                    if "nostate" not in phases:
                        t_zdma = dma(ZT[:, 0, 16 * h:16 * (h + 1)],
                                     ST[64:65, :].rearrange(
                                         "o (q c) -> o q c", q=128),
                                     slot="zdma",
                                     deps=[t_zev32, p.tok(_slot_sem("zbk"))])

            if "stop_lpairs" in phases:
                p.wait("sync", [t_zdma if "nostate" not in phases else None,
                                ("s_pe", p.counts["s_pe"]),
                                ("s_act", p.counts["s_act"]),
                                ("s_dve", p.counts["s_dve"])])
                return (p.tok("s_dve"), p.tok("s_gp"))
            t_zr = dve(lambda e: e.reciprocal(ZT[:, 1, :], ZT[:, 0, :]),
                       deps=[t_zdma, t_tydma, p.tok(_slot_sem("zbk")),
                             ("s_pe", p.counts["s_pe"])])
            vdrain()
            t_hl = hilo(ZTB[:, 0, :], ZTB[:, 1, :], ZT[:, 1, :])
            vdrain()
            t_zb = None
            for h in range(HEADS):
                for hl in range(2):
                    t_zb = dma(ZRH2[h + 16 * hl:h + 16 * hl + 1, :]
                               .rearrange("o (q c) -> o q c", q=128),
                               ZTB[:, hl, 16 * h:16 * (h + 1)],
                               slot="zbk", deps=[t_hl])
            t_norm = {}
            for qt in range(NT):
                for och in range(NCH):
                    bk = och % 2
                    dep0 = [t_zb, t_consts, ("s_act", p.counts["s_act"])]
                    if t_norm.get(bk) is not None:
                        dep0.append(t_norm[bk])
                    mm(bank(bk, 0, TT), PATZ[:, och, :],
                       ZRH2[:, qt * TT:(qt + 1) * TT],
                       start=True, stop=True, deps=dep0, inc="s_pe")
                    t_norm[bk] = dve(
                        lambda e, och=och, qt=qt, bk=bk: e.tensor_tensor(
                            ab[:, och, qt * TT:(qt + 1) * TT],
                            ab[:, och, qt * TT:(qt + 1) * TT],
                            bank(bk, 0, TT), ALU.mult),
                        deps=[p.tok("s_pe")])
                    vdrain()
            t_abn = p.tok("s_dve")
            ev_pr = {}
            for whalf in range(2):
                w2v = SC[:, 0:4096].rearrange("p (c w) -> p c w", c=NCH)
                t_w2 = dma(w2v, dram["l1_wout"]
                           [0:1024, 512 * whalf:512 * (whalf + 1)]
                           .rearrange("(c p) w -> p c w", p=128),
                           slot="wqp", queue="gp",
                           deps=[t_abn, ("s_act", p.counts["s_act"]),
                                 ("s_pe", p.counts["s_pe"])])
                for qt in range(NT):
                    for oj in range(4):
                        och = 4 * whalf + oj
                        bk = 2 + oj % 2
                        dep0 = [t_w2, t_abn]
                        if ev_pr.get(bk) is not None:
                            dep0.append(ev_pr[bk])
                        for c in range(NCH):
                            mm(bank(bk, 0, TT),
                               w2v[:, c, 128 * oj:128 * (oj + 1)],
                               ab[:, c, qt * TT:(qt + 1) * TT],
                               start=(c == 0), stop=(c == NCH - 1),
                               deps=dep0 if c == 0 else (),
                               inc="s_pe" if c == NCH - 1 else None)
                        eng_gp = oj % 2 == 1
                        tpr = ST[:, (0 + (1 if eng_gp else 0)) * TT:
                                 (1 + (1 if eng_gp else 0)) * TT]
                        t_t = act(tpr, bank(bk, 0, TT), AF.Identity,
                                  deps=[p.tok("s_pe"),
                                        ev_pr.get("g" if eng_gp else "d")])
                        ev_pr[bk] = t_t
                        xsl = X[:, och, qt * TT:(qt + 1) * TT]
                        if eng_gp:
                            tok = gp(lambda e, xsl=xsl, tpr=tpr:
                                     e.tensor_tensor(xsl, xsl, tpr, ALU.add),
                                     deps=[t_t])
                            p.emit("gpsimd", lambda e: e.drain())
                            ev_pr["g"] = tok
                        else:
                            tok = dve(lambda e, xsl=xsl, tpr=tpr:
                                      e.tensor_tensor(xsl, xsl, tpr, ALU.add),
                                      deps=[t_t])
                            vdrain()
                            ev_pr["d"] = tok
            return (p.tok("s_dve"), p.tok("s_gp"))

        W_OFF = {"cn_w1": (0, 1024), "cn_w2": (8192, 512), "cn_w3": (12288, 256),
                 "cn_w4": (13312, 1), "gc_w1": (13328, 512), "gc_w2": (17424, 256),
                 "gc_w3": (18448, 1)}

        def wa_tile(name, c, m0, mlen):
            off, wid = W_OFF[name]
            return WA[:, off + c * wid + m0: off + c * wid + m0 + mlen]

        def load_ffn0_weights(deps=()):
            toks = []
            for name, hbm in [("cn_w1", "l0_cn_w1"), ("cn_w2", "l0_cn_w2"),
                              ("cn_w3", "l0_cn_w3"), ("cn_w4", "l0_cn_w4"),
                              ("gc_w1", "l0_gc_w1"), ("gc_w2", "l0_gc_w2"),
                              ("gc_w3", "l0_gc_w3")]:
                off, wid = W_OFF[name]
                src = dram[hbm]
                if wid == 1:
                    nch = src.shape[1]
                    toks.append(dma(WA[:, off: off + nch], src, slot="wff0",
                                    queue="gp", deps=deps))
                    continue
                nch = src.shape[0] // 128
                dst = WA[:, off: off + nch * wid].rearrange(
                    "p (c w) -> p c w", c=nch)
                toks.append(dma(dst, src.rearrange("(c p) w -> p c w", p=128),
                                slot="wff0", queue="gp", deps=deps))
            return toks[-1]

        pe_after_tile = []
        prev_sts = []
        for seq in range(B_PER_CORE):
            pe_after_tile = []
            evs = load_stream(seq, dep_st=prev_sts[-2:])
            last = [evs[-1]]
            for sl_name in sublayers:
                if sl_name in ("ffn0a", "ffn0b"):
                    last = list(ffn0(seq, dep_in=last))
                elif sl_name == "attn0":
                    last = list(attn0(seq, dep_in=last))
                elif sl_name == "attn1":
                    last = list(attn1(seq, dep_in=last))
                elif sl_name == "attn1w":
                    last = list(attn1(seq, dep_in=last, phases=("w",)))
                elif sl_name == "attn1v":
                    last = list(attn1(seq, dep_in=last, phases=("v",)))
                elif sl_name == "attn1ln":
                    last = list(attn1(seq, dep_in=last,
                                      phases=("l", "stop_lpairs",
                                              "st_nointer")))
                elif sl_name == "attn1we":
                    last = list(attn1(seq, dep_in=last, phases=("w", "wnoep")))
                elif sl_name == "attn1lq":
                    last = list(attn1(seq, dep_in=last,
                                      phases=("l", "stop_lpairs", "st_qphi")))
                elif sl_name == "attn1la":
                    last = list(attn1(seq, dep_in=last,
                                      phases=("l", "stop_lpairs", "st_noy")))
                elif sl_name == "attn1lk":
                    last = list(attn1(seq, dep_in=last,
                                      phases=("l", "stop_lpairs", "st_nokv")))
                elif sl_name == "attn1lb":
                    last = list(attn1(seq, dep_in=last,
                                      phases=("l", "stop_lpairs", "nostate")))
                elif sl_name == "attn1lp":
                    last = list(attn1(seq, dep_in=last,
                                      phases=("l", "stop_lpairs")))
                elif sl_name == "attn1wr":
                    last = list(attn1(seq, dep_in=last,
                                      phases=("w", "stop_rows")))
                elif sl_name == "attn1wn":
                    last = list(attn1(seq, dep_in=last,
                                      phases=("w", "stop_norm")))
                elif sl_name == "attn1wp":
                    last = list(attn1(seq, dep_in=last,
                                      phases=("w", "stop_pairs")))
                elif sl_name == "attn1li1":
                    last = list(attn1(seq, dep_in=last,
                                      phases=("l", "stop_lpairs", "st_ia")))
                elif sl_name == "attn1li2":
                    last = list(attn1(seq, dep_in=last,
                                      phases=("l", "stop_lpairs", "st_iab")))
                elif sl_name == "attn1li3":
                    last = list(attn1(seq, dep_in=last,
                                      phases=("l", "stop_lpairs", "st_iabc")))
                elif sl_name == "attn1l":
                    last = list(attn1(seq, dep_in=last, phases=("l",)))
                elif sl_name in ("ffn1a", "ffn1b"):
                    last = list(ffn1(seq, dep_in=last))
                elif sl_name == "lnonly":
                    last = list(ffn0(seq, dep_in=last, lnonly=True))
                elif sl_name.startswith("ffn0nc"):
                    last = list(ffn0(seq, dep_in=last,
                                     ncycles=int(sl_name[-1])))
                else:
                    raise NotImplementedError(sl_name)
            prev_sts = store_stream(seq, last)
        p.wait("sync", [p.tok(_slot_sem("st0")), p.tok(_slot_sem("st1"))])

        block = es.enter_context(nc.Block())

        @block.sync
        def _(e):
            p.replay("sync", e)

        @block.tensor
        def _(e):
            p.replay("tensor", e)

        @block.scalar
        def _(e):
            p.replay("scalar", e)

        @block.vector
        def _(e):
            p.replay("vector", e)

        @block.gpsimd
        def _(e):
            p.replay("gpsimd", e)

    return nc



def _prep_consts():
    ident = np.eye(128, dtype=np.float32)
    tril = np.triu(np.ones((128, 128), dtype=np.float32))
    win = np.zeros((128, 3, 256), dtype=np.float32)
    for di, delta in enumerate((-128, 0, 128)):
        pp = np.arange(128)[:, None]
        qq = np.arange(256)[None, :]
        k = pp + delta
        win[:, di, :] = ((k <= qq) & (k >= qq - WINDOW)).astype(np.float32)
    iu, ju = np.triu_indices(FEAT)
    sel1 = np.zeros((FEAT, NQUAD), dtype=np.float32)
    sel2 = np.zeros((FEAT, NQUAD), dtype=np.float32)
    for r in range(NQUAD):
        sel1[iu[r], r] = 1.0
        sel2[ju[r], r] = 1.0
    return ident, tril, win, sel1, sel2


def _prep_weights(inp):
    w = {}
    R = np.ascontiguousarray(inp["L0_attn_W_qkv"], dtype=np.float32).reshape(
        HEADS, 3, HD, D)
    w["l0_wq"] = R[:, 0].transpose(2, 0, 1).reshape(D, D)
    w["l0_wk"] = R[:, 1].transpose(2, 0, 1).reshape(D, D)
    w["l0_wv"] = R[:, 2].transpose(2, 0, 1).reshape(D, D)
    w["l0_wo"] = np.asarray(inp["L0_attn_W_out"]).T
    w["l0_bo"] = np.asarray(inp["L0_attn_b_out"]).reshape(8, 128).T
    w["l0_cn_w1"] = inp["L0_ffn_cn_w1"]
    w["l0_cn_w2"] = inp["L0_ffn_cn_w2"]
    w["l0_cn_w3"] = inp["L0_ffn_cn_w3"]
    w["l0_cn_w4"] = np.asarray(inp["L0_ffn_cn_w4"]).reshape(2, 128).T
    w["l0_cn_b1"] = np.asarray(inp["L0_ffn_cn_b1"]).reshape(8, 128).T
    w["l0_cn_b2"] = np.asarray(inp["L0_ffn_cn_b2"]).reshape(4, 128).T
    w["l0_cn_b3"] = np.asarray(inp["L0_ffn_cn_b3"]).reshape(2, 128).T
    w["l0_cn_b4"] = np.asarray(inp["L0_ffn_cn_b4"]).reshape(1, 1)
    w["l0_gc_w1"] = inp["L0_ffn_gc_w1"]
    w["l0_gc_w2"] = inp["L0_ffn_gc_w2"]
    w["l0_gc_w3"] = np.asarray(inp["L0_ffn_gc_w3"]).reshape(2, 128).T
    w["l0_gc_b1"] = np.asarray(inp["L0_ffn_gc_b1"]).reshape(4, 128).T
    w["l0_gc_b2"] = np.asarray(inp["L0_ffn_gc_b2"]).reshape(2, 128).T
    w["l0_gc_b3"] = np.asarray(inp["L0_ffn_gc_b3"]).reshape(1, 1)
    w["l0_ar"] = np.asarray(inp["L0_ffn_adaptive_range"]).reshape(1, 1)
    w["l1_wq"] = inp["L1_attn_Wq"]
    w["l1_wk"] = inp["L1_attn_Wk"]
    w["l1_wv"] = inp["L1_attn_Wv"]
    w["l1_wout"] = inp["L1_attn_Wout"]
    w["l1_wqf"] = np.vstack([np.asarray(inp["L1_attn_Wqf"])] * 2)
    w["l1_wkf"] = np.vstack([np.asarray(inp["L1_attn_Wkf"])] * 2)
    w["l1_normw"] = np.asarray(inp["L1_attn_norm_w"]).reshape(8, 128).T
    w["l1_w1"] = inp["L1_ffn_w1"]
    w["l1_b1"] = np.asarray(inp["L1_ffn_b1"]).reshape(32, 128).T
    w["l1_w2"] = inp["L1_ffn_w2"]
    w["l1_b2"] = np.asarray(inp["L1_ffn_b2"]).reshape(8, 128).T
    kfbd = np.zeros((128, 32), dtype=np.float32)
    kfbd[0:64, 0:16] = np.asarray(inp["L1_attn_Wkf"])
    kfbd[64:128, 16:32] = np.asarray(inp["L1_attn_Wkf"])
    w["kf_blockdiag"] = kfbd
    ident, tril, win, sel1, sel2 = _prep_consts()
    patz = np.zeros((32, NCH, 128), dtype=np.float32)
    for och in range(NCH):
        for mcol in range(128):
            hd_ = 2 * och + (mcol // 64)
            patz[hd_, och, mcol] = 1.0
            patz[16 + hd_, och, mcol] = 1.0
    w["patz"] = patz
    w["id2"] = np.eye(2, dtype=np.float32)
    w["ident"] = ident
    w["tril"] = tril
    w["winmask"] = win
    w["sel1"] = sel1
    w["sel2"] = sel2
    return {k: np.ascontiguousarray(v, dtype=np.float32) for k, v in w.items()}


def kernel(**inputs):
    x = np.ascontiguousarray(inputs["L0_attn_x"], dtype=np.float32)
    w = _prep_weights(inputs)
    nc = build_program()
    in_maps = []
    for c in range(N_CORES):
        m = {"x": x[c * B_PER_CORE:(c + 1) * B_PER_CORE]}
        m.update(w)
        in_maps.append(m)
    res = run_bass_kernel_spmd(nc, in_maps, core_ids=list(range(N_CORES)))
    out = np.concatenate([r["y"] for r in res.results], axis=0)
    return out.astype(np.float32)


# version 08 (final)
# speedup vs baseline: 1.0291x; 1.0237x over previous
import contextlib
import numpy as np

import concourse.bass as bass
import concourse.mybir as mybir
from concourse.bass_utils import run_bass_kernel_spmd

F32 = mybir.dt.float32
BF16 = mybir.dt.bfloat16
I32 = mybir.dt.int32
AF = mybir.ActivationFunctionType
ALU = mybir.AluOpType

D = 1024
NCH = 8
T = 2048
B_PER_CORE = 2
N_CORES = 8
HEADS = 16
HD = 64
LN_EPS = 1e-5
RMS_EPS = 1e-6
LIN_EPS = 1e-6
WINDOW = 64
CHUNK = 128
FEAT = 16
NQUAD = 136
NPHI = 153

ENGINES = ("sync", "tensor", "scalar", "vector", "gpsimd")

SUBLAYERS = ["ffn0a", "attn0", "ffn0b", "ffn1a", "attn1", "ffn1b"]


class Prog:

    def __init__(self, nc):
        self.nc = nc
        self.ops = []
        self.sems = {}
        self.counts = {}
        self.last_wait = {e: {} for e in ENGINES}
        self.es = contextlib.ExitStack()

    def sem(self, name):
        h = self.es.enter_context(self.nc.semaphore(name))
        self.sems[name] = h
        self.counts[name] = 0
        return name

    def tok(self, semname):
        return (semname, self.counts[semname])

    def emit(self, eng, fn, deps=(), inc=None, inc_n=1):
        deps = [d for d in deps if d is not None]
        needed = []
        for s, v in deps:
            if v <= 0:
                continue
            if self.last_wait[eng].get(s, 0) >= v:
                continue
            self.last_wait[eng][s] = v
            needed.append((s, v))
        if inc is not None:
            self.counts[inc] += inc_n
        tok = (inc, self.counts[inc]) if inc is not None else None
        sems = self.sems

        def run(e):
            for s, v in needed:
                e.wait_ge(sems[s], v)
            ins = fn(e)
            if inc is not None:
                if ins is None:
                    ins = e.nop()
                ins.then_inc(sems[inc], inc_n)

        self.ops.append((eng, run))
        return tok

    def wait(self, eng, deps):
        self.emit(eng, lambda e: None, deps=deps)

    def replay(self, eng_name, e):
        for name, fn in self.ops:
            if name == eng_name:
                fn(e)



def build_program(sublayers=None):
    if sublayers is None:
        sublayers = SUBLAYERS
    nc = bass.Bass("TRN2", target_bir_lowering=False)

    dram = {}

    def din(name, shape):
        dram[name] = nc.dram_tensor(name, shape, F32, kind="ExternalInput").ap()
        return dram[name]

    x_hbm = din("x", [B_PER_CORE, T, D])
    y_hbm = nc.dram_tensor("y", [B_PER_CORE, T, D], F32, kind="ExternalOutput").ap()

    din("l0_cn_w1", [D, D]);     din("l0_cn_b1", [128, 8])
    din("l0_cn_w2", [D, 512]);   din("l0_cn_b2", [128, 4])
    din("l0_cn_w3", [512, 256]); din("l0_cn_b3", [128, 2])
    din("l0_cn_w4", [128, 2]);   din("l0_cn_b4", [1, 1])
    din("l0_gc_w1", [D, 512]);   din("l0_gc_b1", [128, 4])
    din("l0_gc_w2", [512, 256]); din("l0_gc_b2", [128, 2])
    din("l0_gc_w3", [128, 2]);   din("l0_gc_b3", [1, 1])
    din("l0_ar", [1, 1])
    din("l0_wq", [D, D]); din("l0_wk", [D, D]); din("l0_wv", [D, D])
    din("l0_wo", [D, D]); din("l0_bo", [128, 8])
    din("l1_wq", [D, D]); din("l1_wk", [D, D]); din("l1_wv", [D, D])
    din("l1_wout", [2 * D, D])
    din("l1_wqf", [2 * HD, FEAT]); din("l1_wkf", [2 * HD, FEAT])
    din("l1_normw", [128, 8])
    din("l1_w1", [D, 4 * D]); din("l1_b1", [128, 32])
    din("l1_w2", [4 * D, D]); din("l1_b2", [128, 8])
    din("ident", [128, 128])
    din("kf_blockdiag", [128, 32])
    din("tril", [128, 128])
    din("winmask", [128, 3, 256])
    din("sel1", [FEAT, NQUAD])
    din("sel2", [FEAT, NQUAD])
    din("patz", [32, NCH, 128])
    din("id2", [2, 2])

    p = Prog(nc)
    es = p.es

    def sb(name, shape, dtype=F32):
        return es.enter_context(nc.sbuf_tensor(name, shape, dtype))

    with es:
        X = sb("X", [128, NCH, T])
        O = sb("O", [128, NCH, T])
        Ob = O.ap().bitcast(BF16)
        WA = sb("WA", [128, 18464], BF16)
        SC = sb("SC", [128, 8192], BF16)
        ST = sb("ST", [128, 2048])

        STT = sb("STT", [128, 160])
        STTB = sb("STTB", [128, 64], BF16)
        ROWB = sb("ROWB", [2, 2, 2048], BF16)
        STG = ST.ap()[0:1, 0:1024]
        CROWB = ROWB.ap()[0:1, 0, 0:1024]
        ONESC = sb("ONESC", [128, 2], BF16)
        ONESR = sb("ONESR", [2, 128], BF16)
        IDENT = sb("IDENT", [128, 128])
        BIAS = sb("BIAS", [128, 88])
        SMALL = sb("SMALL", [1, 64])
        GCH = sb("GCH", [128, 16], BF16)
        HALFT = sb("HALFT", [128, 512])
        TRIL = sb("TRIL", [128, 128], BF16)
        WMASK = sb("WMASK", [128, 3, 256], BF16)
        SEL = sb("SEL", [16, 2, NQUAD], BF16)
        KFBD = sb("KFBD", [128, 32], BF16)
        WQF = sb("WQF", [128, 16], BF16)
        WKF = sb("WKF", [128, 16], BF16)
        PATZ = sb("PATZ", [32, NCH, 128], BF16)
        ID2 = sb("ID2", [2, 2], BF16)
        ONESH = sb("ONESH", [128, 2], BF16)
        ONESBF = sb("ONESBF", [65, 512], BF16)
        ZT = sb("ZT", [128, 2, 256])
        ZTB = ZT.ap()[:, 0, :].bitcast(BF16).rearrange(
            "p (q c) -> p q c", q=2)
        ZRH = SC.ap()[0:32, 0:2048]
        BNDS = sb("BNDS", [128, 4])
        BSC = sb("BSC", [2, 16])
        BSCB = sb("BSCB", [2, 4], BF16)
        BROW = sb("BROW", [1, 8], BF16)

        PS = es.enter_context(nc.psum_tensor("PS", [128, 4096], F32))

        def bank(i, o0=0, n=512):
            return PS[:, 512 * i + o0: 512 * i + o0 + n]

        s_pe = p.sem("s_pe")
        s_act = p.sem("s_act")
        s_dve = p.sem("s_dve")
        s_gp = p.sem("s_gp")

        _dma_slots = {}

        def _slot_sem(slot):
            if slot not in _dma_slots:
                _dma_slots[slot] = p.sem("dq_" + slot)
            return _dma_slots[slot]

        def dma(dst, src, slot="init", queue="sync", deps=()):
            semname = _slot_sem(slot)
            eng = "sync" if queue == "sync" else "gpsimd"
            return p.emit(eng, lambda e: e.dma_start(out=dst, in_=src),
                          deps=deps, inc=semname, inc_n=16)

        def store(dst, src, slot="st0", deps=()):
            semname = _slot_sem(slot)
            return p.emit("sync", lambda e: e.dma_start(out=dst, in_=src),
                          deps=deps, inc=semname, inc_n=16)

        def mm(out, lhsT, rhs, start, stop, deps=(), inc=None):
            return p.emit(
                "tensor",
                lambda e: e.matmul(out, lhsT, rhs, start=start, stop=stop),
                deps=deps, inc=inc)

        def mmt(out, in_, deps=(), inc=None):
            return p.emit("tensor", lambda e: e.transpose(out, in_, IDENT[:]),
                          deps=deps, inc=inc)

        def act(out, in_, func, bias=0.0, scale=1.0, accum=None, deps=(), inc="s_act"):
            return p.emit(
                "scalar",
                lambda e: e.activation(out, in_, func, bias=bias, scale=scale,
                                       accum_out=accum),
                deps=deps, inc=inc)

        def acopy(out, in_, deps=(), inc="s_act"):
            return p.emit("scalar", lambda e: e.copy(out, in_), deps=deps, inc=inc)

        def dve(fn, deps=(), inc="s_dve"):
            return p.emit("vector", fn, deps=deps, inc=inc)

        def gp(fn, deps=(), inc="s_gp"):
            return p.emit("gpsimd", fn, deps=deps, inc=inc)

        def vdrain():
            p.emit("vector", lambda e: e.drain())

        t_id = dma(IDENT[:], dram["ident"][:])
        dve(lambda e: e.memset(ONESC[:], 1.0))
        dve(lambda e: e.memset(ONESR[:], 1.0))
        t_tril = dma(TRIL[:], dram["tril"][:], slot="initg", queue="gp")
        t_wmask = dma(WMASK[:], dram["winmask"][:], slot="initg", queue="gp")
        t_sel1 = dma(SEL[:, 0], dram["sel1"][:], slot="initg", queue="gp")
        t_sel2 = dma(SEL[:, 1], dram["sel2"][:], slot="initg", queue="gp")
        t_kfbd = dma(KFBD[:], dram["kf_blockdiag"][:], slot="initg", queue="gp")
        t_wqf = dma(WQF[:], dram["l1_wqf"][:], slot="initg", queue="gp")
        dma(WKF[:], dram["l1_wkf"][:], slot="initg", queue="gp")
        dma(PATZ[:], dram["patz"][:], slot="initg", queue="gp")
        dma(ID2[:], dram["id2"][:], slot="initg", queue="gp")
        dve(lambda e: e.memset(ONESH[:, :], 0.0))
        vdrain()
        dve(lambda e: e.memset(ONESH[0:64, 0:1], 1.0))
        dve(lambda e: e.memset(ONESH[64:128, 1:2], 1.0))
        dve(lambda e: e.memset(ONESBF[:, :], 1.0))
        gp(lambda e: e.memset(HALFT[:], 0.5))

        BIAS_MAP = {}
        _bias_col = [0]

        def load_bias(name, ncols):
            c0 = _bias_col[0]
            _bias_col[0] += ncols
            assert _bias_col[0] <= 76
            BIAS_MAP[name] = c0
            return dma(BIAS[:, c0:c0 + ncols], dram[name][:])

        for nm, wd in [("l0_cn_b1", 8), ("l0_cn_b2", 4), ("l0_cn_b3", 2),
                       ("l0_gc_b1", 4), ("l0_gc_b2", 2), ("l0_bo", 8),
                       ("l1_normw", 8), ("l1_b1", 32), ("l1_b2", 8)]:
            load_bias(nm, wd)

        def bias_ap(name, j):
            c = BIAS_MAP[name] + j
            return BIAS[:, c:c + 1]

        t_hb2 = None
        dma(SMALL[:, 0:1], dram["l0_cn_b4"][:])
        dma(SMALL[:, 1:2], dram["l0_gc_b3"][:])
        dma(SMALL[:, 2:3], dram["l0_ar"][:])
        t_params = p.tok(_slot_sem("init"))
        t_consts = p.tok(_slot_sem("initg"))
        t_id = t_params
        b2c = BIAS_MAP["l1_b2"]
        t_hb2 = dve(lambda e: e.tensor_scalar(
            BIAS[:, 76:84], BIAS[:, b2c:b2c + 8], 0.5, None, ALU.mult),
            deps=[t_params])
        vdrain()

        def rsqrt_dve(out, in_, tmp, deps=()):
            oi = out.bitcast(I32)
            vi = in_.bitcast(I32)
            dve(lambda e: e.tensor_scalar(oi, vi, 1, None,
                                          ALU.logical_shift_right), deps=deps)
            vdrain()
            dve(lambda e: e.tensor_scalar(oi, oi, 0x5F3759DF, None, ALU.subtract))
            vdrain()
            dve(lambda e: e.tensor_scalar(oi, oi, -1, None, ALU.mult))
            t0 = None
            for _ in range(3):
                vdrain()
                dve(lambda e: e.tensor_tensor(tmp, out, out, ALU.mult))
                vdrain()
                dve(lambda e: e.tensor_tensor(tmp, tmp, in_, ALU.mult))
                vdrain()
                dve(lambda e: e.tensor_scalar(tmp, tmp, -0.5, 1.5, ALU.mult, ALU.add))
                vdrain()
                t0 = dve(lambda e: e.tensor_tensor(out, out, tmp, ALU.mult))
            return t0

        def hilo(dst_hi, dst_lo, src, deps=()):
            dve(lambda e: e.tensor_copy(dst_hi, src), deps=deps)
            vdrain()
            t2 = dve(lambda e: e.tensor_tensor(dst_lo, src, dst_hi, ALU.subtract))
            return t2

        _ln_state = {"bk": None}

        def ln_stats_seq(src, b_stat, dep_src=(), rms_fuse=False):
            TT = 512
            NT = T // TT
            t_dma = None
            for it in range(NT):
                sl = slice(it * TT, (it + 1) * TT)
                xin = src[:, :, sl]
                xbf = SC[:, 0:NCH * TT].rearrange("p (c t) -> p c t", c=NCH)
                sq = SC[:, NCH * TT:2 * NCH * TT].rearrange(
                    "p (c t) -> p c t", c=NCH)
                dsrc = list(dep_src) if it == 0 else [p.tok("s_pe")]
                t_cast = dve(lambda e, xbf=xbf, xin=xin: e.tensor_copy(xbf, xin),
                             deps=dsrc)
                t_sq = act(sq, xbf, AF.Square, deps=[t_cast])
                tsum = None
                for c in range(NCH):
                    tsum = mm(bank(b_stat, 0, TT)[0:1, :], ONESC[:, 0:1],
                              xbf[:, c, :],
                              start=(c == 0), stop=(c == NCH - 1),
                              deps=[t_cast, p.tok("s_act")] if c == 0 else (),
                              inc="s_pe" if c == NCH - 1 else None)
                tsq = None
                for c in range(NCH):
                    tsq = mm(bank(b_stat + 1, 0, TT)[0:1, :], ONESC[:, 0:1],
                             sq[:, c, :],
                             start=(c == 0), stop=(c == NCH - 1),
                             deps=[t_sq] if c == 0 else (),
                             inc="s_pe" if c == NCH - 1 else None)
                ev1 = acopy(STG[0:1, 0:TT], bank(b_stat, 0, TT)[0:1, :],
                            deps=[tsum, t_dma])
                ev2 = acopy(STG[0:1, TT:2 * TT], bank(b_stat + 1, 0, TT)[0:1, :],
                            deps=[tsq])
                dma(STT[:, 4 * it:4 * it + 4],
                    STG[0:1, 0:TT].rearrange("o (p c) -> o p c", p=128),
                    slot="lnst", deps=[ev1])
                t_dma = dma(STT[:, 16 + 4 * it:16 + 4 * it + 4],
                            STG[0:1, TT:2 * TT].rearrange("o (p c) -> o p c", p=128),
                            slot="lnst", deps=[ev2])
            C = lambda j: STT[:, j:j + 16]
            m, v, r, tmp, mr, tmp2, r2 = (C(32), C(48), C(64), C(80), C(96),
                                          C(112), C(128))
            dep0 = [t_dma, _ln_state["bk"]]
            dve(lambda e: e.tensor_scalar(m, C(0), 1.0 / D, None, ALU.mult),
                deps=dep0)
            dve(lambda e: e.tensor_scalar(v, C(16), 1.0 / D, None, ALU.mult))
            vdrain()
            dve(lambda e: e.tensor_tensor(tmp, m, m, ALU.mult))
            vdrain()
            dve(lambda e: e.tensor_tensor(v, v, tmp, ALU.subtract))
            vdrain()
            dve(lambda e: e.tensor_scalar(v, v, LN_EPS, None, ALU.add))
            vdrain()
            rsqrt_dve(r, v, tmp)
            if rms_fuse:
                vdrain()
                dve(lambda e: e.tensor_tensor(tmp2, r, r, ALU.mult))
                vdrain()
                dve(lambda e: e.tensor_scalar(v, v, -LN_EPS, None, ALU.add))
                vdrain()
                dve(lambda e: e.tensor_tensor(tmp2, tmp2, v, ALU.mult))
                vdrain()
                dve(lambda e: e.tensor_scalar(tmp2, tmp2, RMS_EPS, None, ALU.add))
                vdrain()
                rsqrt_dve(r2, tmp2, tmp)
                vdrain()
                dve(lambda e: e.tensor_tensor(r, r, r2, ALU.mult))
            vdrain()
            dve(lambda e: e.tensor_tensor(mr, m, r, ALU.mult))
            vdrain()
            hilo(STTB[:, 0:16], STTB[:, 16:32], r)
            t_hl = hilo(STTB[:, 32:48], STTB[:, 48:64], mr)
            vdrain()
            tbk = None
            for (q, hl, rm) in ((0, 0, 0), (1, 1, 0), (2, 0, 1), (3, 1, 1)):
                for it in range(NT):
                    tbk = dma(ROWB[hl:hl + 1, rm, it * TT:(it + 1) * TT]
                              .rearrange("o (p c) -> o p c", p=128),
                              STTB[:, 16 * q + 4 * it:16 * q + 4 * it + 4],
                              slot="lnbk", deps=[t_hl])
            _ln_state["bk"] = tbk
            return tbk

        def ln_bcast(it, b_bc, dep_rows, dep_banks=()):
            TT = 512
            sl = slice(it * TT, (it + 1) * TT)
            t_rb = mm(bank(b_bc, 0, TT), ONESR[:], ROWB[:, 0, sl],
                      start=True, stop=True,
                      deps=[dep_rows] + list(dep_banks), inc="s_pe")
            t_mrb = mm(bank(b_bc + 1, 0, TT), ONESR[:], ROWB[:, 1, sl],
                       start=True, stop=True, inc="s_pe")
            return t_rb, t_mrb

        def ln_apply(src, it, b_bc, dst, t_rb, t_mrb, split_gp=True, dep_dst=(),
                     dep_war=(), scale_vec=None):
            TT = 512
            sl = slice(it * TT, (it + 1) * TT)
            rb = bank(b_bc, 0, TT)
            mrb = bank(b_bc + 1, 0, TT)
            toks = []
            st_r = ST[:, 0:TT]
            st_mr = ST[:, TT:2 * TT]
            ta = acopy(st_r, rb, deps=[t_rb] + list(dep_war))
            tb = acopy(st_mr, mrb, deps=[t_mrb])
            t_xread = None
            for c in range(NCH):
                eng_gp = split_gp and (c >= NCH // 2)
                xin = src[:, c, sl]
                zout = dst[:, c, sl]
                tmp = ST[:, (2 + (0 if eng_gp else 1)) * TT:
                          (3 + (0 if eng_gp else 1)) * TT]
                if eng_gp:
                    gp(lambda e, tmp=tmp, xin=xin: e.tensor_tensor(
                        tmp, xin, st_r, ALU.mult),
                       deps=[ta, tb, t_xread] + list(dep_dst))
                    p.emit("gpsimd", lambda e: e.drain())
                    if scale_vec is None:
                        t2 = gp(lambda e, zout=zout, tmp=tmp: e.tensor_tensor(
                            zout, tmp, st_mr, ALU.subtract))
                    else:
                        tg = gp(lambda e, tmp=tmp: e.tensor_tensor(
                            tmp, tmp, st_mr, ALU.subtract))
                        t2 = dve(lambda e, zout=zout, tmp=tmp, c=c:
                                 e.tensor_scalar(zout, tmp,
                                                 bias_ap(scale_vec, c), None,
                                                 ALU.mult), deps=[tg])
                        vdrain()
                        t_xread = t2
                    p.emit("gpsimd", lambda e: e.drain())
                else:
                    dve(lambda e, tmp=tmp, xin=xin: e.tensor_tensor(
                        tmp, xin, st_r, ALU.mult), deps=[ta, tb] + list(dep_dst))
                    vdrain()
                    if scale_vec is None:
                        t2 = dve(lambda e, zout=zout, tmp=tmp: e.tensor_tensor(
                            zout, tmp, st_mr, ALU.subtract))
                    else:
                        dve(lambda e, tmp=tmp: e.tensor_tensor(
                            tmp, tmp, st_mr, ALU.subtract))
                        vdrain()
                        t2 = dve(lambda e, zout=zout, tmp=tmp, c=c:
                                 e.tensor_scalar(zout, tmp,
                                                 bias_ap(scale_vec, c), None,
                                                 ALU.mult))
                    vdrain()
                toks.append(t2)
            if split_gp:
                return [toks[NCH // 2 - 1], toks[-1]]
            return [toks[-1]]

        def load_stream(seq, dep_st=()):
            evs = []
            for tt in range(T // 128):
                half = tt % 2
                tile = ST[:, half * 1024:(half + 1) * 1024]
                dep_war = [("s_pe", pe_after_tile[tt - 2])] if tt >= 2 else []
                if tt < 2:
                    dep_war = dep_war + [("s_pe", p.counts["s_pe"])] + list(dep_st)
                t_ld = dma(tile, x_hbm[seq, tt * 128:(tt + 1) * 128, :],
                           slot="xin%d" % half, deps=dep_war)
                for c in range(NCH):
                    dst = PS[:, 1024 * half + 128 * c: 1024 * half + 128 * c + 128]
                    deps = [t_ld, t_id] if c == 0 else ()
                    if tt >= 2 and c == 0:
                        deps = list(deps) + [evs[tt - 2]]
                    mmt(dst, tile[:, c * 128:(c + 1) * 128], deps=deps,
                        inc="s_pe" if c == NCH - 1 else None)
                pe_after_tile.append(p.counts["s_pe"])
                t_pe = p.tok("s_pe")
                dst = X[:, :, tt * 128:(tt + 1) * 128]
                src = PS[:, half * 1024: half * 1024 + 1024].rearrange(
                    "p (c t) -> p c t", c=NCH)
                evs.append(dve(lambda e, dst=dst, src=src: e.tensor_copy(dst, src),
                               deps=[t_pe]))
            return evs

        def store_stream(seq, deps):
            sts = []
            evs = []
            for tt in range(T // 128):
                half = tt % 2
                for c in range(NCH):
                    dst = PS[:, 2048 + 1024 * half + 128 * c:
                             2048 + 1024 * half + 128 * c + 128]
                    dps = list(deps) if c == 0 else []
                    if tt >= 2 and c == 0:
                        dps += [evs[tt - 2]]
                    mmt(dst, X[:, c, tt * 128:(tt + 1) * 128], deps=dps,
                        inc="s_pe" if c == NCH - 1 else None)
                t_pe = p.tok("s_pe")
                tile = ST[:, half * 1024:(half + 1) * 1024]
                src = PS[:, 2048 + half * 1024: 2048 + half * 1024 + 1024]
                dep_war = [sts[tt - 2]] if tt >= 2 else []
                evs.append(acopy(tile, src, deps=[t_pe] + dep_war))
                sts.append(store(y_hbm[seq, tt * 128:(tt + 1) * 128, :], tile,
                                 slot="st%d" % half, deps=[evs[-1]]))
            return sts

        def ffn0(seq, dep_in, ncycles=3, lnonly=False):
            TT = 512
            NT = T // TT
            phase0 = [("s_act", p.counts["s_act"]), ("s_pe", p.counts["s_pe"]),
                      ("s_dve", p.counts["s_dve"]), ("s_gp", p.counts["s_gp"])]
            wtoks = load_ffn0_weights(deps=phase0)
            vdrain()
            t_rows = ln_stats_seq(X, 4, dep_src=list(dep_in) + phase0)
            prev_apply = []
            for it in range(NT):
                t_rb, t_mrb = ln_bcast(it, 6, t_rows,
                                       dep_banks=prev_apply[-2:])
                prev_apply = ln_apply(X, it, 6, O, t_rb, t_mrb,
                                      dep_war=prev_apply[-2:])
            t_ln = (p.tok("s_dve"), p.tok("s_gp"))
            if lnonly:
                for it in range(NT):
                    sl = slice(it * TT, (it + 1) * TT)
                    dve(lambda e, sl=sl: e.tensor_copy(X[:, :, sl], O[:, :, sl]),
                        deps=[t_ln[0], t_ln[1]])
                    vdrain()
                return (p.tok("s_dve"), p.tok("s_gp"))
            g32 = ST[:, 0:NCH]
            t_g = dve(lambda e: e.tensor_reduce(
                g32, O[:, :, :], mybir.AxisListType.X, ALU.add),
                deps=[t_ln[0], t_ln[1]])
            vdrain()
            g16 = GCH[:, 0:NCH]
            t_g16 = dve(lambda e: e.tensor_scalar(g16, g32, 1.0 / T, None, ALU.mult))
            for j in range(4):
                for c in range(NCH):
                    mm(PS[:, 512 * 7 + j: 512 * 7 + j + 1],
                       wa_tile("gc_w1", c, j * 128, 128),
                       g16[:, c:c + 1],
                       start=(c == 0), stop=(c == NCH - 1),
                       deps=[t_g16, wtoks] if c == 0 else (),
                       inc="s_pe" if (j == 3 and c == NCH - 1) else None)
            t1 = p.tok("s_pe")
            for j in range(4):
                act(GCH[:, 8 + j:9 + j], PS[:, 512 * 7 + j:512 * 7 + j + 1],
                    AF.Gelu, bias=bias_ap("l0_gc_b1", j), deps=[t1])
            t_h1g = p.tok("s_act")
            for j2 in range(2):
                for c in range(4):
                    mm(PS[:, 512 * 7 + 16 + j2: 512 * 7 + 17 + j2],
                       wa_tile("gc_w2", c, j2 * 128, 128),
                       GCH[:, 8 + c:9 + c],
                       start=(c == 0), stop=(c == 3),
                       deps=[t_h1g] if c == 0 else (),
                       inc="s_pe" if (j2 == 1 and c == 3) else None)
            t2 = p.tok("s_pe")
            for j2 in range(2):
                act(GCH[:, 12 + j2:13 + j2],
                    PS[:, 512 * 7 + 16 + j2:512 * 7 + 17 + j2],
                    AF.Gelu, bias=bias_ap("l0_gc_b2", j2), deps=[t2])
            t_h2g = p.tok("s_act")
            for c in range(2):
                mm(PS[0:1, 512 * 7 + 24:512 * 7 + 25],
                   wa_tile("gc_w3", c, 0, 1), GCH[:, 12 + c:13 + c],
                   start=(c == 0), stop=(c == 1),
                   deps=[t_h2g] if c == 0 else (),
                   inc="s_pe" if c == 1 else None)
            t3 = p.tok("s_pe")
            tsm = dve(lambda e: e.tensor_scalar(SMALL[:, 8:9], SMALL[:, 1:2],
                                                0.5, None, ALU.mult),
                      deps=[t_params])
            t_tg = act(SMALL[:, 9:10], PS[0:1, 512 * 7 + 24:512 * 7 + 25],
                       AF.Tanh, bias=SMALL[:, 8:9], scale=0.5, deps=[t3, tsm])
            dve(lambda e: e.tensor_tensor(SMALL[:, 10:11], SMALL[:, 2:3],
                                          SMALL[:, 9:10], ALU.mult), deps=[t_tg])
            vdrain()
            dve(lambda e: e.tensor_scalar(SMALL[:, 10:11], SMALL[:, 10:11],
                                          0.7, 0.1, ALU.mult, ALU.add))
            t_c0 = dve(lambda e: e.tensor_scalar(SMALL[:, 11:12], SMALL[:, 2:3],
                                                 0.3, None, ALU.mult))
            vdrain()
            t_b4 = dve(lambda e: e.tensor_scalar(SMALL[:, 12:13], SMALL[:, 0:1],
                                                 0.5, None, ALU.mult))

            ev_prev = {}
            for cyc in range(ncycles):
                for it in range(NT):
                    sl = slice(it * TT, (it + 1) * TT)
                    obf = SC[:, 0:NCH * TT].rearrange("p (c t) -> p c t", c=NCH)
                    dep_o = [("s_dve", p.counts["s_dve"]),
                             ("s_gp", p.counts["s_gp"]),
                             ("s_pe", p.counts["s_pe"])]
                    t_obf = dve(lambda e, obf=obf, sl=sl: e.tensor_copy(
                        obf, O[:, :, sl]), deps=dep_o)
                    h1 = SC[:, NCH * TT:2 * NCH * TT].rearrange(
                        "p (c t) -> p c t", c=NCH)
                    for j in range(NCH):
                        bi = j % 2
                        dep0 = [t_obf, wtoks] if j == 0 else []
                        if ev_prev.get(("cn1", bi)) is not None:
                            dep0.append(ev_prev[("cn1", bi)])
                        for c in range(NCH):
                            mm(bank(bi, 0, TT), wa_tile("cn_w1", c, j * 128, 128),
                               obf[:, c, :], start=(c == 0), stop=(c == NCH - 1),
                               deps=dep0 if c == 0 else (),
                               inc="s_pe" if c == NCH - 1 else None)
                        ev = act(h1[:, j, :], bank(bi, 0, TT), AF.Gelu,
                                 bias=bias_ap("l0_cn_b1", j),
                                 deps=[p.tok("s_pe")])
                        ev_prev[("cn1", bi)] = ev
                    h2 = SC[:, 0:4 * TT].rearrange("p (c t) -> p c t", c=4)
                    for j in range(4):
                        bi = 2 + (j % 2)
                        dep0 = []
                        if j == 0:
                            dep0 += [ev_prev[("cn1", 0)], ev_prev[("cn1", 1)]]
                        if ev_prev.get(("b", bi)) is not None:
                            dep0.append(ev_prev[("b", bi)])
                        for c in range(NCH):
                            mm(bank(bi, 0, TT), wa_tile("cn_w2", c, j * 128, 128),
                               h1[:, c, :], start=(c == 0), stop=(c == NCH - 1),
                               deps=dep0 if c == 0 else (),
                               inc="s_pe" if c == NCH - 1 else None)
                        ev = act(h2[:, j, :], bank(bi, 0, TT), AF.Gelu,
                                 bias=bias_ap("l0_cn_b2", j),
                                 deps=[p.tok("s_pe")])
                        ev_prev[("b", bi)] = ev
                    h3 = SC[:, 4 * TT:6 * TT].rearrange("p (c t) -> p c t", c=2)
                    for j in range(2):
                        bi = 2 + (j % 2)
                        dep0 = []
                        if j == 0:
                            dep0 += [ev_prev[("b", 2)], ev_prev[("b", 3)]]
                        if ev_prev.get(("b", bi)) is not None:
                            dep0.append(ev_prev[("b", bi)])
                        for c in range(4):
                            mm(bank(bi, 0, TT), wa_tile("cn_w3", c, j * 128, 128),
                               h2[:, c, :], start=(c == 0), stop=(c == 3),
                               deps=dep0 if c == 0 else (),
                               inc="s_pe" if c == 3 else None)
                        ev = act(h3[:, j, :], bank(bi, 0, TT), AF.Gelu,
                                 bias=bias_ap("l0_cn_b3", j),
                                 deps=[p.tok("s_pe")])
                        ev_prev[("b", bi)] = ev
                    dep0 = [ev_prev[("b", 2)], ev_prev[("b", 3)]]
                    for c in range(2):
                        mm(bank(2, 0, TT)[0:1, :],
                           wa_tile("cn_w4", c, 0, 1), h3[:, c, :],
                           start=(c == 0), stop=(c == 1),
                           deps=dep0 if c == 0 else (),
                           inc="s_pe" if c == 1 else None)
                    t_cn4 = p.tok("s_pe")
                    crow = STG[0:1, 0:TT]
                    t_tf = act(crow, bank(2, 0, TT)[0:1, :], AF.Tanh,
                               bias=SMALL[:, 12:13], scale=0.5,
                               deps=[t_cn4, t_b4, t_c0])
                    ev_prev[("b", 2)] = t_tf
                    t_c = dve(lambda e: e.tensor_scalar(
                        crow, crow, SMALL[:, 11:12], SMALL[:, 10:11],
                        ALU.mult, ALU.add), deps=[t_tf])
                    vdrain()
                    t_hilo = hilo(CROWB[0:1, 0:TT], CROWB[0:1, TT:2 * TT], crow)
                    dep_cb = [t_hilo]
                    if ev_prev.get(("cb",)) is not None:
                        dep_cb.append(ev_prev[("cb",)])
                    mm(bank(4, 0, TT), ONESR[0:1, :], CROWB[0:1, 0:TT],
                       start=True, stop=False, deps=dep_cb)
                    t_cb = mm(bank(4, 0, TT), ONESR[0:1, :], CROWB[0:1, TT:2 * TT],
                              start=False, stop=True, inc="s_pe")
                    cb_st = ST[:, 0:TT]
                    t_cbst = acopy(cb_st, bank(4, 0, TT),
                                   deps=[t_cb, t_ln[0], t_ln[1],
                                         ev_prev.get(("tw_dve",)),
                                         ev_prev.get(("tw_gp",))])
                    ev_prev[("cb",)] = t_cbst
                    tmp_a = ST[:, 1 * TT:2 * TT]
                    tmp_b = ST[:, 2 * TT:3 * TT]
                    tmp_g = ST[:, 3 * TT:4 * TT]
                    tmp_g2 = SC[:, 6 * TT:8 * TT].bitcast(F32)
                    def _sgn(j):
                        return ALU.add if j in (0, 1, 6, 7) else ALU.subtract
                    for g in (0, 1, 3):
                        j1, j2 = g, g + 4
                        dve(lambda e, j2=j2, sl=sl: e.tensor_tensor(
                            tmp_a, O[:, j2, sl], cb_st, ALU.mult),
                            deps=[t_cbst, t_obf])
                        dve(lambda e, j1=j1, sl=sl: e.tensor_tensor(
                            tmp_b, O[:, j1, sl], cb_st, ALU.mult))
                        vdrain()
                        dve(lambda e, j1=j1, sl=sl: e.tensor_tensor(
                            O[:, j1, sl], O[:, j1, sl], tmp_a, _sgn(j1)))
                        dve(lambda e, j2=j2, sl=sl: e.tensor_tensor(
                            O[:, j2, sl], O[:, j2, sl], tmp_b, _sgn(j2)))
                        vdrain()
                    j1, j2 = 2, 6
                    gp(lambda e, j2=j2, sl=sl: e.tensor_tensor(
                        tmp_g, O[:, j2, sl], cb_st, ALU.mult),
                        deps=[t_cbst, t_obf, ev_prev.get(("b", 2)),
                              ev_prev.get(("b", 3))])
                    gp(lambda e, j1=j1, sl=sl: e.tensor_tensor(
                        tmp_g2, O[:, j1, sl], cb_st, ALU.mult))
                    p.emit("gpsimd", lambda e: e.drain())
                    gp(lambda e, j1=j1, sl=sl: e.tensor_tensor(
                        O[:, j1, sl], O[:, j1, sl], tmp_g, _sgn(j1)))
                    gp(lambda e, j2=j2, sl=sl: e.tensor_tensor(
                        O[:, j2, sl], O[:, j2, sl], tmp_g2, _sgn(j2)))
                    p.emit("gpsimd", lambda e: e.drain())
                    ev_prev[("tw_dve",)] = p.tok("s_dve")
                    ev_prev[("tw_gp",)] = p.tok("s_gp")
            for it in range(NT):
                sl = slice(it * TT, (it + 1) * TT)
                dep_o = [("s_dve", p.counts["s_dve"]), ("s_gp", p.counts["s_gp"])]
                for c in range(NCH):
                    if c % 2 == 0:
                        dve(lambda e, c=c, sl=sl: e.scalar_tensor_tensor(
                            X[:, c, sl], O[:, c, sl], 0.5, X[:, c, sl],
                            ALU.mult, ALU.add), deps=dep_o if c == 0 else ())
                    else:
                        gtmp = ST[:, 0:TT]
                        gp(lambda e, c=c, sl=sl, gtmp=gtmp: e.tensor_tensor(
                            gtmp, O[:, c, sl], HALFT[:], ALU.mult),
                           deps=dep_o if c == 1 else ())
                        p.emit("gpsimd", lambda e: e.drain())
                        gp(lambda e, c=c, sl=sl, gtmp=gtmp: e.tensor_tensor(
                            X[:, c, sl], X[:, c, sl], gtmp, ALU.add))
                        p.emit("gpsimd", lambda e: e.drain())
            return (p.tok("s_dve"), p.tok("s_gp"))

        W_A0 = {"v": 0, "wq": 16640, "wo": 0}

        def attn0(seq, dep_in):
            TT = 512
            NT = T // TT
            phase0 = [("s_act", p.counts["s_act"]), ("s_pe", p.counts["s_pe"]),
                      ("s_dve", p.counts["s_dve"]), ("s_gp", p.counts["s_gp"])]
            zb = Ob[:, :, 0:T]
            ab = Ob[:, :, T:2 * T]
            vdrain()
            t_rows = ln_stats_seq(X, 4, dep_src=list(dep_in) + phase0)
            prev_apply = []
            for it in range(NT):
                t_rb, t_mrb = ln_bcast(it, 6, t_rows, dep_banks=prev_apply[-2:])
                prev_apply = ln_apply(X, it, 6, zb, t_rb, t_mrb,
                                      dep_war=prev_apply[-2:])
            t_ln = (p.tok("s_dve"), p.tok("s_gp"))

            ev_v = {}
            for half in range(2):
                wvh = SC[:, 0:NCH * TT].rearrange("p (c w) -> p c w", c=NCH)
                t_wv = dma(wvh, dram["l0_wv"][:, 512 * half:512 * (half + 1)]
                           .rearrange("(c p) w -> p c w", p=128),
                           slot="wv0", queue="gp",
                           deps=[t_ln[0], t_ln[1], ("s_pe", p.counts["s_pe"])])
                for tt in range(16):
                    bk = (tt % 4)
                    dep0 = [t_wv]
                    if ev_v.get(bk) is not None:
                        dep0.append(ev_v[bk])
                    for c in range(NCH):
                        mm(bank(bk, 0, TT),
                           zb[:, c, tt * 128:(tt + 1) * 128],
                           wvh[:, c, :],
                           start=(c == 0), stop=(c == NCH - 1),
                           deps=dep0 if c == 0 else (),
                           inc="s_pe" if c == NCH - 1 else None)
                    dst = WA[:, 1040 * tt + 65 * 8 * half:
                             1040 * tt + 65 * 8 * half + 65 * 8]
                    dst = dst.rearrange("p (h d) -> p h d", h=8)[:, :, 0:64]
                    src = bank(bk, 0, TT).rearrange("p (h d) -> p h d", h=8)
                    ev_v[bk] = acopy(dst, src, deps=[p.tok("s_pe")])
            vap = WA[:, 64:64 + 256 * 65].rearrange(
                "p (n o) -> p n o", n=256, o=65)[:, :, 0:1]
            t_vo = dve(lambda e: e.memset(vap, 1.0),
                       deps=[ev_v[0], ev_v[1], ev_v[2], ev_v[3]])
            t_vo2 = t_vo
            vdrain()

            t_zevac = {}
            t_zdma = None
            t_yevac = None
            t_zev32 = None
            t_tydma = None
            for pj in range(8):
                dep_pair = [("s_act", p.counts["s_act"]),
                            ("s_pe", p.counts["s_pe"])]
                wqp = WA[:, W_A0["wq"]:W_A0["wq"] + 1024].rearrange(
                    "p (c w) -> p c w", c=NCH)
                q2 = SC[:, 0:2048]
                k2 = SC[:, 2048:4096]
                t_wq = dma(wqp, dram["l0_wq"][:, 128 * pj:128 * (pj + 1)]
                           .rearrange("(c p) w -> p c w", p=128),
                           slot="wqp", queue="gp", deps=dep_pair)
                for qt in range(NT):
                    bk = qt % 2
                    dep0 = [t_wq] + dep_pair
                    if t_zevac.get(("q", bk)) is not None:
                        dep0.append(t_zevac[("q", bk)])
                    for c in range(NCH):
                        mm(bank(bk, 0, TT), wqp[:, c, :],
                           zb[:, c, qt * TT:(qt + 1) * TT],
                           start=(c == 0), stop=(c == NCH - 1),
                           deps=dep0 if c == 0 else (),
                           inc="s_pe" if c == NCH - 1 else None)
                    t_zevac[("q", bk)] = acopy(
                        q2[:, qt * TT:(qt + 1) * TT], bank(bk, 0, TT),
                        deps=[p.tok("s_pe")])
                t_wk = dma(wqp, dram["l0_wk"][:, 128 * pj:128 * (pj + 1)]
                           .rearrange("(c p) w -> p c w", p=128),
                           slot="wqp", queue="gp", deps=[p.tok("s_pe")])
                for qt in range(NT):
                    bk = 2 + qt % 2
                    dep0 = [t_wk]
                    if t_zevac.get(("k", bk)) is not None:
                        dep0.append(t_zevac[("k", bk)])
                    for c in range(NCH):
                        mm(bank(bk, 0, TT), wqp[:, c, :],
                           zb[:, c, qt * TT:(qt + 1) * TT],
                           start=(c == 0), stop=(c == NCH - 1),
                           deps=dep0 if c == 0 else (),
                           inc="s_pe" if c == NCH - 1 else None)
                    t_zevac[("k", bk)] = acopy(
                        k2[:, qt * TT:(qt + 1) * TT], bank(bk, 0, TT),
                        deps=[p.tok("s_pe")])
                t_qk = p.tok("s_act")

                psq = SC[:, 4096:6144]
                t_red = None
                for wi, buf in ((0, q2), (1, k2)):
                    t_sq = act(psq, buf, AF.Square,
                               deps=[t_qk, ("s_pe", p.counts["s_pe"])])
                    for qt in range(NT):
                        mm(bank(0, 0, TT)[0:2, :], ONESH[:],
                           psq[:, qt * TT:(qt + 1) * TT],
                           start=True, stop=True,
                           deps=[t_sq] + ([t_red] if t_red else []),
                           inc="s_pe")
                        t_red = dve(lambda e, qt=qt: e.reduce_max(
                            BSC[0:2, 0:1] if qt == 0 else BSC[0:2, 1:2],
                            bank(0, 0, TT)[0:2, :],
                            mybir.AxisListType.X), deps=[p.tok("s_pe")])
                        vdrain()
                        if qt > 0:
                            t_red = dve(lambda e: e.tensor_tensor(
                                BSC[0:2, 0:1], BSC[0:2, 0:1], BSC[0:2, 1:2],
                                ALU.max))
                            vdrain()
                    dve(lambda e, wi=wi: e.tensor_copy(
                        BSC[0:2, 2 + wi:3 + wi], BSC[0:2, 0:1]))
                    vdrain()
                dve(lambda e: e.tensor_tensor(BSC[0:2, 4:5], BSC[0:2, 2:3],
                                              BSC[0:2, 3:4], ALU.mult))
                vdrain()
                rsqrt_dve(BSC[0:2, 5:6], BSC[0:2, 4:5], BSC[0:2, 6:7])
                vdrain()
                dve(lambda e: e.tensor_tensor(BSC[0:2, 5:6], BSC[0:2, 5:6],
                                              BSC[0:2, 4:5], ALU.mult))
                vdrain()
                t_bnd = dve(lambda e: e.tensor_scalar(
                    BSCB[0:2, 0:1], BSC[0:2, 5:6], -1.0 / 32.0, None, ALU.mult))
                mm(bank(0, 0, 2)[0:1, :], BSCB[0:2, 0:1], ID2[:],
                   start=True, stop=True, deps=[t_bnd, t_consts], inc="s_pe")
                t_br = acopy(BROW[0:1, 0:2], bank(0, 0, 2)[0:1, :],
                             deps=[p.tok("s_pe")])
                mm(bank(0, 0, 2), ONESR[0:1, :], BROW[0:1, 0:2],
                   start=True, stop=True, deps=[t_br], inc="s_pe")
                t_bb = acopy(BNDS[:, 0:2], bank(0, 0, 2), deps=[p.tok("s_pe")])

                pb = SC[:, 4096:6144]
                for hh in range(2):
                    h = 2 * pj + hh
                    yps = PS[:, 2048:4096]
                    t_exp = {}
                    t_pv = None
                    t_scA = t_scB = None
                    for sc in range(16):
                        for qt in range(NT):
                            dep0 = []
                            if qt == 0:
                                dep0 = [t_bb, t_qk]
                                if t_exp.get(0) is not None:
                                    dep0.append(t_exp[0])
                            if qt == 2 and t_exp.get(1) is not None:
                                dep0 = [t_exp[1]]
                            mm(bank(qt, 0, TT),
                               k2[64 * hh:64 * (hh + 1),
                                  sc * 128:(sc + 1) * 128],
                               q2[64 * hh:64 * (hh + 1),
                                  qt * TT:(qt + 1) * TT],
                               start=True, stop=True, deps=dep0,
                               inc="s_pe" if qt in (1, 3) else None)
                            if qt == 1:
                                t_scA = p.tok("s_pe")
                            if qt == 3:
                                t_scB = p.tok("s_pe")
                        dep_pv = [t_pv] if t_pv is not None else []
                        t_exp[0] = act(pb[:, 0:1024], PS[:, 0:1024], AF.Exp,
                                       bias=BNDS[:, hh:hh + 1], scale=1 / 32.0,
                                       deps=[t_scA] + dep_pv)
                        t_exp[1] = act(pb[:, 1024:2048], PS[:, 1024:2048],
                                       AF.Exp, bias=BNDS[:, hh:hh + 1],
                                       scale=1 / 32.0, deps=[t_scB])
                        for qt in range(NT):
                            dep0 = ()
                            if qt == 0:
                                dep0 = [t_exp[0], t_vo, t_vo2]
                                if sc == 0 and t_yevac is not None:
                                    dep0 = dep0 + [t_yevac, t_zev32]
                                if sc == 0 and t_zdma is not None:
                                    dep0 = dep0 + [t_zdma]
                            elif qt == 2:
                                dep0 = [t_exp[1]]
                            mm(yps[0:65, qt * TT:(qt + 1) * TT],
                               WA[:, 1040 * sc + 65 * h:
                                  1040 * sc + 65 * h + 65],
                               pb[:, qt * TT:(qt + 1) * TT],
                               start=(sc == 0), stop=(sc == 15),
                               deps=dep0,
                               inc="s_pe" if qt == 3 else None)
                        t_pv = p.tok("s_pe")
                    ych = h // 2
                    ypart = 64 * (h % 2)
                    if ypart == 0:
                        t_yevac = acopy(ab[:, ych, :][0:64, :], yps[0:64, :],
                                        deps=[t_pv])
                    else:
                        ty = SC[:, 6144:8192][0:64, :]
                        t_ty = acopy(ty, yps[0:64, :], deps=[t_pv, t_tydma])
                        t_tydma = dma(ab[:, ych, :][64:128, :], ty,
                                      slot="tydma", deps=[t_ty])
                        t_yevac = t_ty
                    t_zev32 = acopy(ST[64:65, :], yps[64:65, :],
                                    deps=[t_pv, t_zdma])
                    t_zdma = dma(ZT[:, 0, 16 * h:16 * (h + 1)],
                                 ST[64:65, :].rearrange("o (q c) -> o q c",
                                                        q=128),
                                 slot="zdma", deps=[t_zev32])

            t_zr = dve(lambda e: e.reciprocal(ZT[:, 1, :], ZT[:, 0, :]),
                       deps=[t_zdma, t_tydma, p.tok(_slot_sem("zbk"))])
            vdrain()
            t_hl = hilo(ZTB[:, 0, :], ZTB[:, 1, :], ZT[:, 1, :])
            vdrain()
            t_zb = None
            for h in range(HEADS):
                for hl in range(2):
                    t_zb = dma(ZRH[h + 16 * hl:h + 16 * hl + 1, :]
                               .rearrange("o (q c) -> o q c", q=128),
                               ZTB[:, hl, 16 * h:16 * (h + 1)],
                               slot="zbk", deps=[t_hl])
            t_wo = dma(WA[:, 0:8192].rearrange("p (c w) -> p c w", c=NCH),
                       dram["l0_wo"][:].rearrange("(c p) w -> p c w", p=128),
                       slot="wo0", queue="gp", deps=[("s_pe", p.counts["s_pe"])])
            t_norm = {}
            for qt in range(NT):
                for och in range(NCH):
                    bk = och % 2
                    dep0 = [t_zb, t_consts]
                    if t_norm.get(bk) is not None:
                        dep0.append(t_norm[bk])
                    mm(bank(bk, 0, TT), PATZ[:, och, :],
                       ZRH[:, qt * TT:(qt + 1) * TT],
                       start=True, stop=True, deps=dep0, inc="s_pe")
                    t_norm[bk] = dve(
                        lambda e, och=och, qt=qt, bk=bk: e.tensor_tensor(
                            ab[:, och, qt * TT:(qt + 1) * TT],
                            ab[:, och, qt * TT:(qt + 1) * TT],
                            bank(bk, 0, TT), ALU.mult),
                        deps=[p.tok("s_pe")])
                    vdrain()
            t_abn = (p.tok("s_dve"),)

            wo = WA[:, 0:8192].rearrange("p (c w) -> p c w", c=NCH)
            ev_ep = {}
            for qt in range(NT):
                dep_b = [t_rows, ("s_act", p.counts["s_act"])]
                mm(bank(6, 0, TT), ONESR[:], ROWB[:, 0, qt * TT:(qt + 1) * TT],
                   start=True, stop=False, deps=dep_b)
                t_b6 = mm(bank(6, 0, TT), ONESR[0:1, :], ONESBF[0:1, :],
                          start=False, stop=True, inc="s_pe")
                t_b7 = mm(bank(7, 0, TT), ONESR[:],
                          ROWB[:, 1, qt * TT:(qt + 1) * TT],
                          start=True, stop=True, inc="s_pe")
                t_c6 = acopy(ST[:, 0:TT], bank(6, 0, TT), deps=[t_b6])
                t_c7 = acopy(ST[:, TT:2 * TT], bank(7, 0, TT), deps=[t_b7])
                for och in range(NCH):
                    bk = 2 + och % 2
                    dep0 = [t_abn[0], t_wo]
                    if ev_ep.get(bk) is not None:
                        dep0.append(ev_ep[bk])
                    for c in range(NCH):
                        mm(bank(bk, 0, TT), wo[:, c, 128 * och:128 * (och + 1)],
                           ab[:, c, qt * TT:(qt + 1) * TT],
                           start=(c == 0), stop=(c == NCH - 1),
                           deps=dep0 if c == 0 else (),
                           inc="s_pe" if c == NCH - 1 else None)
                    eng_gp = och >= 4
                    tpr = ST[:, (2 + (1 if eng_gp else 0)) * TT:
                             (3 + (1 if eng_gp else 0)) * TT]
                    t_t = act(tpr, bank(bk, 0, TT), AF.Identity,
                              bias=bias_ap("l0_bo", och),
                              deps=[p.tok("s_pe"),
                                    ev_ep.get("tg" if eng_gp else "td")])
                    xsl = X[:, och, qt * TT:(qt + 1) * TT]
                    if eng_gp:
                        gp(lambda e, tpr=tpr: e.tensor_tensor(
                            tpr, tpr, ST[:, TT:2 * TT], ALU.subtract),
                           deps=[t_t, t_c7, t_c6])
                        p.emit("gpsimd", lambda e: e.drain())
                        gp(lambda e, xsl=xsl: e.tensor_tensor(
                            xsl, xsl, ST[:, 0:TT], ALU.mult))
                        p.emit("gpsimd", lambda e: e.drain())
                        ev_ep[bk] = gp(lambda e, xsl=xsl, tpr=tpr: e.tensor_tensor(
                            xsl, xsl, tpr, ALU.add))
                        ev_ep["tg"] = ev_ep[bk]
                        p.emit("gpsimd", lambda e: e.drain())
                    else:
                        dve(lambda e, tpr=tpr, bk=bk: e.tensor_tensor(
                            tpr, tpr, bank(7, 0, TT), ALU.subtract),
                            deps=[t_t, t_b7])
                        vdrain()
                        dve(lambda e, xsl=xsl: e.tensor_tensor(
                            xsl, xsl, bank(6, 0, TT), ALU.mult), deps=[t_b6])
                        vdrain()
                        ev_ep[bk] = dve(lambda e, xsl=xsl, tpr=tpr: e.tensor_tensor(
                            xsl, xsl, tpr, ALU.add))
                        ev_ep["td"] = ev_ep[bk]
                        vdrain()
                p.wait("tensor", [ev_ep.get(2), ev_ep.get(3),
                                  ("s_gp", p.counts["s_gp"]),
                                  ("s_dve", p.counts["s_dve"])])
            return (p.tok("s_dve"), p.tok("s_gp"))

        def ffn1(seq, dep_in):
            TT = 512
            NT = T // TT
            phase0 = [("s_act", p.counts["s_act"]), ("s_pe", p.counts["s_pe"]),
                      ("s_dve", p.counts["s_dve"]), ("s_gp", p.counts["s_gp"])]
            zbw = WA[:, 0:NCH * T].rearrange("p (c t) -> p c t", c=NCH)
            w1h = Ob[:, :, 0:T]
            w2h = Ob[:, :, T:2 * T].rearrange(
                "p c (k w) -> p c k w", k=2)

            def w2t(kc, j):
                return w2h[:, kc // 2, kc % 2, 128 * j:128 * (j + 1)]
            vdrain()
            t_rows = ln_stats_seq(X, 4, dep_src=list(dep_in) + phase0)
            prev_apply = []
            for it in range(NT):
                t_rb, t_mrb = ln_bcast(it, 6, t_rows, dep_banks=prev_apply[-2:])
                prev_apply = ln_apply(X, it, 6, zbw, t_rb, t_mrb,
                                      dep_war=prev_apply[-2:])
            t_ln = (p.tok("s_dve"), p.tok("s_gp"))
            h1sb = SC[:, 0:8192].rearrange("p (c t) -> p c t", c=16)
            ev_h1 = {}
            ev_out = {}
            for half in range(2):
                dep_w = [t_ln[0], t_ln[1], ("s_pe", p.counts["s_pe"])]
                t_w1 = dma(w1h, dram["l1_w1"][:, 2048 * half:2048 * (half + 1)]
                           .rearrange("(c p) w -> p c w", p=128),
                           slot="w1h", queue="gp", deps=dep_w)
                src2 = dram["l1_w2"][2048 * half:2048 * (half + 1), :].rearrange(
                    "(c k p) w -> p c k w", p=128, k=2)
                dma(w2h[:, :, 0, :], src2[:, :, 0, :],
                    slot="w2h", queue="gp", deps=dep_w)
                t_w2 = dma(w2h[:, :, 1, :], src2[:, :, 1, :],
                           slot="w2h", queue="gp", deps=dep_w)
                for it in range(NT):
                    sl = slice(it * TT, (it + 1) * TT)
                    for j in range(16):
                        bk = j % 2
                        dep0 = [t_w1]
                        if ev_h1.get(bk) is not None:
                            dep0.append(ev_h1[bk])
                        for c in range(NCH):
                            mm(bank(bk, 0, TT),
                               w1h[:, c, 128 * j:128 * (j + 1)],
                               zbw[:, c, sl],
                               start=(c == 0), stop=(c == NCH - 1),
                               deps=dep0 if c == 0 else (),
                               inc="s_pe" if c == NCH - 1 else None)
                        ev_h1[bk] = act(h1sb[:, j, :], bank(bk, 0, TT),
                                        AF.Relu,
                                        bias=bias_ap("l1_b1", 16 * half + j),
                                        deps=[p.tok("s_pe")])
                    t_h1 = p.tok("s_act")
                    for j in range(NCH):
                        bk = 2 + j % 2
                        dep0 = [t_w2, t_h1]
                        if ev_out.get(bk) is not None:
                            dep0.append(ev_out[bk])
                        for c in range(16):
                            mm(bank(bk, 0, TT),
                               w2t(c, j),
                               h1sb[:, c, :],
                               start=(c == 0), stop=(c == 15),
                               deps=dep0 if c == 0 else (),
                               inc="s_pe" if c == 15 else None)
                        eng_gp = j % 2 == 1
                        tpr = ST[:, (0 + (1 if eng_gp else 0)) * TT:
                                 (1 + (1 if eng_gp else 0)) * TT]
                        if half == 0:
                            t_t = act(tpr, bank(bk, 0, TT), AF.Identity,
                                      bias=BIAS[:, 76 + j:77 + j], scale=0.5,
                                      deps=[p.tok("s_pe"),
                                            ev_out.get("g" if eng_gp else "d"),
                                            t_hb2])
                        else:
                            t_t = act(tpr, bank(bk, 0, TT), AF.Identity,
                                      scale=0.5,
                                      deps=[p.tok("s_pe"),
                                            ev_out.get("g" if eng_gp else "d")])
                        xsl = X[:, j, sl]
                        if eng_gp:
                            tok = gp(lambda e, xsl=xsl, tpr=tpr: e.tensor_tensor(
                                xsl, xsl, tpr, ALU.add), deps=[t_t])
                            p.emit("gpsimd", lambda e: e.drain())
                            ev_out["g"] = tok
                        else:
                            tok = dve(lambda e, xsl=xsl, tpr=tpr: e.tensor_tensor(
                                xsl, xsl, tpr, ALU.add), deps=[t_t])
                            vdrain()
                            ev_out["d"] = tok
                        ev_out[bk] = t_t
            return (p.tok("s_dve"), p.tok("s_gp"))

        def attn1(seq, dep_in, phases=("w", "l")):
            TT = 512
            NT = T // TT
            phase0 = [("s_act", p.counts["s_act"]), ("s_pe", p.counts["s_pe"]),
                      ("s_dve", p.counts["s_dve"]), ("s_gp", p.counts["s_gp"])]
            hb = Ob[:, :, 0:T]
            ab = Ob[:, :, T:2 * T]
            vdrain()
            t_rows = ln_stats_seq(X, 4, dep_src=list(dep_in) + phase0,
                                  rms_fuse=True)
            prev_apply = []
            for it in range(NT):
                t_rb, t_mrb = ln_bcast(it, 6, t_rows, dep_banks=prev_apply[-2:])
                prev_apply = ln_apply(X, it, 6, hb, t_rb, t_mrb,
                                      dep_war=prev_apply[-2:],
                                      scale_vec="l1_normw")
            t_ln = (p.tok("s_dve"), p.tok("s_gp"))

            ev_v = {}
            for half in range(2):
                wvh = SC[:, 0:NCH * TT].rearrange("p (c w) -> p c w", c=NCH)
                t_wv = dma(wvh, dram["l1_wv"][:, 512 * half:512 * (half + 1)]
                           .rearrange("(c p) w -> p c w", p=128),
                           slot="wv0", queue="gp",
                           deps=[t_ln[0], t_ln[1], ("s_pe", p.counts["s_pe"])])
                for tt in range(16):
                    bk = (tt % 4)
                    dep0 = [t_wv]
                    if ev_v.get(bk) is not None:
                        dep0.append(ev_v[bk])
                    for c in range(NCH):
                        mm(bank(bk, 0, TT),
                           hb[:, c, tt * 128:(tt + 1) * 128],
                           wvh[:, c, :],
                           start=(c == 0), stop=(c == NCH - 1),
                           deps=dep0 if c == 0 else (),
                           inc="s_pe" if c == NCH - 1 else None)
                    dst = WA[:, 1040 * tt + 65 * 8 * half:
                             1040 * tt + 65 * 8 * half + 65 * 8]
                    dst = dst.rearrange("p (h d) -> p h d", h=8)[:, :, 0:64]
                    src = bank(bk, 0, TT).rearrange("p (h d) -> p h d", h=8)
                    ev_v[bk] = acopy(dst, src, deps=[p.tok("s_pe")])
            vap = WA[:, 64:64 + 256 * 65].rearrange(
                "p (n o) -> p n o", n=256, o=65)[:, :, 0:1]
            t_vo = dve(lambda e: e.memset(vap, 1.0),
                       deps=[ev_v[0], ev_v[1], ev_v[2], ev_v[3]])
            vdrain()
            if "v" in phases:
                return (p.tok("s_dve"), p.tok("s_gp"))

            def vplus(sc, h):
                return WA[:, 1040 * sc + 65 * h:1040 * sc + 65 * h + 65]

            skip_w = "w" not in phases
            ZRHW = SC[0:32, 4096:6144]
            t_zevac = {}
            t_zdma = None
            t_tydma = None
            t_yevac = None
            t_zev32 = None
            for pj in (range(8) if not skip_w else ()):
                dep_pair = [("s_act", p.counts["s_act"]),
                            ("s_pe", p.counts["s_pe"])]
                wqp = WA[:, W_A0["wq"]:W_A0["wq"] + 1024].rearrange(
                    "p (c w) -> p c w", c=NCH)
                q2 = SC[:, 0:2048]
                k2 = SC[:, 2048:4096]
                t_wq = dma(wqp, dram["l1_wq"][:, 128 * pj:128 * (pj + 1)]
                           .rearrange("(c p) w -> p c w", p=128),
                           slot="wqp", queue="gp", deps=dep_pair)
                for qt in range(NT):
                    bk = qt % 2
                    dep0 = [t_wq] + dep_pair
                    if t_zevac.get(("q", bk)) is not None:
                        dep0.append(t_zevac[("q", bk)])
                    for c in range(NCH):
                        mm(bank(bk, 0, TT), wqp[:, c, :],
                           hb[:, c, qt * TT:(qt + 1) * TT],
                           start=(c == 0), stop=(c == NCH - 1),
                           deps=dep0 if c == 0 else (),
                           inc="s_pe" if c == NCH - 1 else None)
                    t_zevac[("q", bk)] = acopy(
                        q2[:, qt * TT:(qt + 1) * TT], bank(bk, 0, TT),
                        deps=[p.tok("s_pe")])
                t_wk = dma(wqp, dram["l1_wk"][:, 128 * pj:128 * (pj + 1)]
                           .rearrange("(c p) w -> p c w", p=128),
                           slot="wqp", queue="gp", deps=[p.tok("s_pe")])
                for qt in range(NT):
                    bk = 2 + qt % 2
                    dep0 = [t_wk]
                    if t_zevac.get(("k", bk)) is not None:
                        dep0.append(t_zevac[("k", bk)])
                    for c in range(NCH):
                        mm(bank(bk, 0, TT), wqp[:, c, :],
                           hb[:, c, qt * TT:(qt + 1) * TT],
                           start=(c == 0), stop=(c == NCH - 1),
                           deps=dep0 if c == 0 else (),
                           inc="s_pe" if c == NCH - 1 else None)
                    t_zevac[("k", bk)] = acopy(
                        k2[:, qt * TT:(qt + 1) * TT], bank(bk, 0, TT),
                        deps=[p.tok("s_pe")])
                t_qk = p.tok("s_act")

                psq = SC[:, 4096:6144]
                t_red = None
                for wi, buf in ((0, q2), (1, k2)):
                    t_sq = act(psq, buf, AF.Square,
                               deps=[t_qk, ("s_pe", p.counts["s_pe"])])
                    for qt in range(NT):
                        mm(bank(0, 0, TT)[0:2, :], ONESH[:],
                           psq[:, qt * TT:(qt + 1) * TT],
                           start=True, stop=True,
                           deps=[t_sq] + ([t_red] if t_red else []),
                           inc="s_pe")
                        t_red = dve(lambda e, qt=qt: e.reduce_max(
                            BSC[0:2, 0:1] if qt == 0 else BSC[0:2, 1:2],
                            bank(0, 0, TT)[0:2, :],
                            mybir.AxisListType.X), deps=[p.tok("s_pe")])
                        vdrain()
                        if qt > 0:
                            t_red = dve(lambda e: e.tensor_tensor(
                                BSC[0:2, 0:1], BSC[0:2, 0:1], BSC[0:2, 1:2],
                                ALU.max))
                            vdrain()
                    dve(lambda e, wi=wi: e.tensor_copy(
                        BSC[0:2, 2 + wi:3 + wi], BSC[0:2, 0:1]))
                    vdrain()
                dve(lambda e: e.tensor_tensor(BSC[0:2, 4:5], BSC[0:2, 2:3],
                                              BSC[0:2, 3:4], ALU.mult))
                vdrain()
                rsqrt_dve(BSC[0:2, 5:6], BSC[0:2, 4:5], BSC[0:2, 6:7])
                vdrain()
                dve(lambda e: e.tensor_tensor(BSC[0:2, 5:6], BSC[0:2, 5:6],
                                              BSC[0:2, 4:5], ALU.mult))
                vdrain()
                t_bnd = dve(lambda e: e.tensor_scalar(
                    BSCB[0:2, 0:1], BSC[0:2, 5:6], -1.0 / 8.0, None, ALU.mult))
                mm(bank(0, 0, 2)[0:1, :], BSCB[0:2, 0:1], ID2[:],
                   start=True, stop=True, deps=[t_bnd, t_consts], inc="s_pe")
                t_br = acopy(BROW[0:1, 0:2], bank(0, 0, 2)[0:1, :],
                             deps=[p.tok("s_pe")])
                mm(bank(0, 0, 2), ONESR[0:1, :], BROW[0:1, 0:2],
                   start=True, stop=True, deps=[t_br], inc="s_pe")
                t_bb = acopy(BNDS[:, 0:2], bank(0, 0, 2), deps=[p.tok("s_pe")])

                pw = SC[:, 4096:4864].rearrange("p (i t) -> p i t", i=3)
                for hh in range(2):
                    h = 2 * pj + hh
                    t_mask = None
                    for qt in range(8):
                        kcs = [(i, qt * 2 - 1 + i) for i in range(3)
                               if 0 <= qt * 2 - 1 + i < 16]
                        sdep = [t_bb, t_qk]
                        if t_mask is not None:
                            sdep.append(t_mask)
                        t_sc = None
                        for i, kc in kcs:
                            dst = (bank(0, 0, TT)[:, 256 * i:256 * (i + 1)]
                                   if i < 2 else bank(1, 0, 256))
                            t_sc = mm(dst,
                                      k2[64 * hh:64 * (hh + 1),
                                         kc * 128:(kc + 1) * 128],
                                      q2[64 * hh:64 * (hh + 1),
                                         qt * 256:(qt + 1) * 256],
                                      start=True, stop=True,
                                      deps=sdep, inc="s_pe")
                            sdep = []
                        t_exps = []
                        first_i = kcs[0][0]
                        for i, kc in kcs:
                            src = (bank(0, 0, TT)[:, 256 * i:256 * (i + 1)]
                                   if i < 2 else bank(1, 0, 256))
                            t_exps.append(act(pw[:, i, :], src, AF.Exp,
                                              bias=BNDS[:, hh:hh + 1],
                                              scale=1 / 8.0,
                                              deps=[t_sc] if i == first_i
                                              else ()))
                        ni = len(kcs)
                        t_mask = dve(
                            lambda e, first_i=first_i, ni=ni: e.tensor_tensor(
                                pw[:, first_i:first_i + ni, :],
                                pw[:, first_i:first_i + ni, :],
                                WMASK[:, first_i:first_i + ni, :], ALU.mult),
                            deps=t_exps + [t_consts])
                        vdrain()
                        ydst = PS[0:65, 2048 + 512 * (qt // 2) + 256 * (qt % 2):
                                  2048 + 512 * (qt // 2) + 256 * (qt % 2) + 256]
                        for ji, (i, kc) in enumerate(kcs):
                            dep0 = []
                            if ji == 0:
                                dep0 = [t_mask, t_vo]
                                if qt == 0 and t_yevac is not None:
                                    dep0 += [t_yevac, t_zev32]
                            mm(ydst, vplus(kc, h), pw[:, i, :],
                               start=(ji == 0), stop=(ji == len(kcs) - 1),
                               deps=dep0,
                               inc="s_pe" if ji == len(kcs) - 1 else None)
                        t_pv = p.tok("s_pe")
                    ych = h // 2
                    ypart = 64 * (h % 2)
                    yps = PS[:, 2048:4096]
                    if ypart == 0:
                        t_yevac = acopy(ab[:, ych, :][0:64, :], yps[0:64, :],
                                        deps=[t_pv])
                    else:
                        ty = SC[:, 6144:8192][0:64, :]
                        t_ty = acopy(ty, yps[0:64, :], deps=[t_pv, t_tydma])
                        t_tydma = dma(ab[:, ych, :][64:128, :], ty,
                                      slot="tydma", deps=[t_ty])
                        t_yevac = t_ty
                    t_zev32 = acopy(ST[64:65, :], yps[64:65, :],
                                    deps=[t_pv, t_zdma])
                    t_zdma = dma(ZT[:, 0, 16 * h:16 * (h + 1)],
                                 ST[64:65, :].rearrange("o (q c) -> o q c",
                                                        q=128),
                                 slot="zdma", deps=[t_zev32])

            if not skip_w:
                t_zr = dve(lambda e: e.reciprocal(ZT[:, 1, :], ZT[:, 0, :]),
                           deps=[t_zdma, t_tydma, p.tok(_slot_sem("zbk"))])
                vdrain()
                t_hl = hilo(ZTB[:, 0, :], ZTB[:, 1, :], ZT[:, 1, :])
                vdrain()
                t_zb = None
                for h in range(HEADS):
                    for hl in range(2):
                        t_zb = dma(ZRHW[h + 16 * hl:h + 16 * hl + 1, :]
                                   .rearrange("o (q c) -> o q c", q=128),
                                   ZTB[:, hl, 16 * h:16 * (h + 1)],
                                   slot="zbk", deps=[t_hl])
                t_norm = {}
                for qt in range(NT):
                    for och in range(NCH):
                        bk = och % 2
                        dep0 = [t_zb, t_consts, ("s_act", p.counts["s_act"])]
                        if t_norm.get(bk) is not None:
                            dep0.append(t_norm[bk])
                        mm(bank(bk, 0, TT), PATZ[:, och, :],
                           ZRHW[:, qt * TT:(qt + 1) * TT],
                           start=True, stop=True, deps=dep0, inc="s_pe")
                        t_norm[bk] = dve(
                            lambda e, och=och, qt=qt, bk=bk: e.tensor_tensor(
                                ab[:, och, qt * TT:(qt + 1) * TT],
                                ab[:, och, qt * TT:(qt + 1) * TT],
                                bank(bk, 0, TT), ALU.mult),
                            deps=[p.tok("s_pe")])
                        vdrain()
                t_abn = p.tok("s_dve")
                if "stop_norm" in phases:
                    p.wait("sync", [t_abn, ("s_pe", p.counts["s_pe"])])
                    return (p.tok("s_dve"), p.tok("s_gp"))
                Cc = lambda j: STT[:, j:j + 16]
                m1, v1, r1, tmp1, mr1 = (Cc(32), Cc(48), Cc(144), Cc(80),
                                         Cc(112))
                dve(lambda e: e.tensor_scalar(v1, v1, LN_EPS, None, ALU.add),
                    deps=[t_abn, ("s_gp", p.counts["s_gp"]),
                          ("s_act", p.counts["s_act"])])
                vdrain()
                rsqrt_dve(r1, v1, tmp1)
                vdrain()
                dve(lambda e: e.tensor_tensor(mr1, m1, r1, ALU.mult))
                vdrain()
                hilo(STTB[:, 0:16], STTB[:, 16:32], r1)
                t_hl1 = hilo(STTB[:, 32:48], STTB[:, 48:64], mr1)
                vdrain()
                tbk = None
                for (q, hl, rm) in ((0, 0, 0), (1, 1, 0), (2, 0, 1),
                                    (3, 1, 1)):
                    for it in range(NT):
                        tbk = dma(ROWB[hl:hl + 1, rm, it * TT:(it + 1) * TT]
                                  .rearrange("o (pp c) -> o pp c", pp=128),
                                  STTB[:, 16 * q + 4 * it:
                                       16 * q + 4 * it + 4],
                                  slot="lnbk", deps=[t_hl1])
                _ln_state["bk"] = tbk
                if "stop_rows" in phases:
                    p.wait("sync", [tbk, ("s_pe", p.counts["s_pe"]),
                                    ("s_dve", p.counts["s_dve"])])
                    return (p.tok("s_dve"), p.tok("s_gp"))
                ev_pr = {}
                for whalf in range(2):
                    w2v = SC[:, 0:4096].rearrange("p (c w) -> p c w", c=NCH)
                    t_w2 = dma(w2v, dram["l1_wout"]
                               [1024:2048, 512 * whalf:512 * (whalf + 1)]
                               .rearrange("(c p) w -> p c w", p=128),
                               slot="wqp", queue="gp",
                               deps=[t_abn, ("s_act", p.counts["s_act"]),
                                     ("s_pe", p.counts["s_pe"])])
                    for qt in range(NT):
                        dep_b = [tbk, ("s_act", p.counts["s_act"]),
                                 ("s_dve", p.counts["s_dve"]),
                                 ("s_gp", p.counts["s_gp"])]
                        mm(bank(6, 0, TT), ONESR[:],
                           ROWB[:, 0, qt * TT:(qt + 1) * TT],
                           start=True, stop=False, deps=dep_b)
                        t_b6 = mm(bank(6, 0, TT), ONESR[0:1, :],
                                  ONESBF[0:1, :],
                                  start=False, stop=True, inc="s_pe")
                        t_b7 = mm(bank(7, 0, TT), ONESR[:],
                                  ROWB[:, 1, qt * TT:(qt + 1) * TT],
                                  start=True, stop=True, inc="s_pe")
                        t_c6 = acopy(ST[:, 0:TT], bank(6, 0, TT),
                                     deps=[t_b6])
                        t_c7 = acopy(ST[:, TT:2 * TT], bank(7, 0, TT),
                                     deps=[t_b7])
                        for oj in range(4):
                            och = 4 * whalf + oj
                            bk = 2 + oj % 2
                            dep0 = [t_w2, t_abn]
                            if ev_pr.get(bk) is not None:
                                dep0.append(ev_pr[bk])
                            for c in range(NCH):
                                mm(bank(bk, 0, TT),
                                   w2v[:, c, 128 * oj:128 * (oj + 1)],
                                   ab[:, c, qt * TT:(qt + 1) * TT],
                                   start=(c == 0), stop=(c == NCH - 1),
                                   deps=dep0 if c == 0 else (),
                                   inc="s_pe" if c == NCH - 1 else None)
                            eng_gp = oj % 2 == 1
                            tpr = ST[:, (2 + (1 if eng_gp else 0)) * TT:
                                     (3 + (1 if eng_gp else 0)) * TT]
                            t_t = act(tpr, bank(bk, 0, TT), AF.Identity,
                                      deps=[p.tok("s_pe"),
                                            ev_pr.get("g" if eng_gp
                                                      else "d")])
                            ev_pr[bk] = t_t
                            if "wnoep" in phases:
                                ev_pr["g" if eng_gp else "d"] = t_t
                                continue
                            xsl = X[:, och, qt * TT:(qt + 1) * TT]
                            if eng_gp:
                                gp(lambda e, tpr=tpr: e.tensor_tensor(
                                    tpr, tpr, ST[:, TT:2 * TT],
                                    ALU.subtract),
                                   deps=[t_t, t_c7, t_c6])
                                p.emit("gpsimd", lambda e: e.drain())
                                gp(lambda e, xsl=xsl: e.tensor_tensor(
                                    xsl, xsl, ST[:, 0:TT], ALU.mult))
                                p.emit("gpsimd", lambda e: e.drain())
                                tok = gp(lambda e, xsl=xsl, tpr=tpr:
                                         e.tensor_tensor(xsl, xsl, tpr,
                                                         ALU.add))
                                ev_pr["g"] = tok
                                p.emit("gpsimd", lambda e: e.drain())
                            else:
                                dve(lambda e, tpr=tpr: e.tensor_tensor(
                                    tpr, tpr, bank(7, 0, TT), ALU.subtract),
                                    deps=[t_t, t_b7])
                                vdrain()
                                dve(lambda e, xsl=xsl: e.tensor_tensor(
                                    xsl, xsl, bank(6, 0, TT), ALU.mult),
                                    deps=[t_b6])
                                vdrain()
                                tok = dve(lambda e, xsl=xsl, tpr=tpr:
                                          e.tensor_tensor(xsl, xsl, tpr,
                                                          ALU.add))
                                ev_pr["d"] = tok
                                vdrain()
                        p.wait("tensor", [ev_pr.get("d"), ev_pr.get("g"),
                                          ("s_dve", p.counts["s_dve"]),
                                          ("s_gp", p.counts["s_gp"])])
            t_wend = (p.tok("s_dve"), p.tok("s_gp"))
            if "l" not in phases:
                return t_wend

            A_sb = SC[:, 0:512]
            kfp = SC[:, 4416:4672].rearrange("p (c f) -> p c f", c=16)
            kfv = SC[0:16, 512:2560]
            kqh = SC[:, 2560:3648].rearrange("p (c f) -> p c f", c=8)
            DSB = [SC[0:97, 3648 + 256 * g:3648 + 256 * g + 241]
                   for g in range(3)]
            QF0 = SC[0:17, 4736:6784]
            QF16 = SC[0:16, 4736:6784]
            qB = SC[0:96, 6784:7296]
            qC = SC[0:40, 7296:7808]
            kf_tm = SC[:, 7808:8080].rearrange("p (c f) -> p c f", c=16)
            tyL = WA[:, W_A0["wq"]:W_A0["wq"] + 512][0:64, :]
            pbuf = SC[:, 512:2560]

            def dsb_piece(g, piece):
                base = SC[0:97, 3648 + 256 * g:3648 + 256 * g + 241]
                if piece == "A":
                    return base[0:17, 0:65]
                if piece == "B":
                    return base[0:96, 80:145]
                return base[0:40, 176:241]

            def kvps(g, piece):
                col = 512 * 6 + 256 * g if g < 2 else 512 * 7
                if piece == "A":
                    return PS[0:17, col:col + 65]
                if piece == "B":
                    return PS[0:96, col + 80:col + 145]
                return PS[0:40, col + 176:col + 241]

            ZRH2 = SC[0:32, 6144:8192][:, 0:2048]
            t_lin = {}
            t_zdma = None
            t_tydma = None
            kfv = pbuf[0:16, 0:2048]
            dep_all = [("s_act", p.counts["s_act"]),
                       ("s_pe", p.counts["s_pe"]),
                       ("s_dve", p.counts["s_dve"]),
                       ("s_gp", p.counts["s_gp"])]
            t_onq = dve(lambda e: e.memset(QF0[:], 1.0), deps=dep_all)
            vdrain()
            t_ones = dve(lambda e: e.memset(kf_tm[:, :, 16:17], 1.0))
            vdrain()
            t_lin[("qf", 0)] = t_onq
            t_lin[("qf", 1)] = t_onq
            for pj in range(8):
                for hh in range(2):
                    h = 2 * pj + hh
                    dep_pair = [("s_act", p.counts["s_act"]),
                                ("s_pe", p.counts["s_pe"])]
                    wqp = WA[:, W_A0["wq"] + 512:W_A0["wq"] + 1536].rearrange(
                        "p (c w) -> p c w", c=NCH)
                    t_w = dma(wqp, dram["l1_wq"][:, 128 * pj:128 * (pj + 1)]
                              .rearrange("(c p) w -> p c w", p=128),
                              slot="wqp", queue="gp", deps=dep_pair)
                    for qt in range(NT):
                        bk = qt % 2
                        dep0 = [t_w] + dep_pair
                        if t_lin.get(("pb", bk)) is not None:
                            dep0.append(t_lin[("pb", bk)])
                        for c in range(NCH):
                            mm(bank(bk, 0, TT), wqp[:, c, :],
                               hb[:, c, qt * TT:(qt + 1) * TT],
                               start=(c == 0), stop=(c == NCH - 1),
                               deps=dep0 if c == 0 else (),
                               inc="s_pe" if c == NCH - 1 else None)
                        t_lin[("pb", bk)] = acopy(
                            pbuf[:, qt * TT:(qt + 1) * TT], bank(bk, 0, TT),
                            deps=[p.tok("s_pe")])
                    t_pb = p.tok("s_act")
                    for qt in range(NT):
                        mm(bank(4 + qt % 2, 0, TT)[0:16, :],
                           WQF[64 * hh:64 * hh + 64, :],
                           pbuf[64 * hh:64 * (hh + 1),
                                qt * TT:(qt + 1) * TT],
                           start=True, stop=True,
                           deps=[t_pb, t_consts, t_lin.get(("qf", qt % 2))],
                           inc="s_pe")
                        t_lin[("qf", qt % 2)] = acopy(
                            QF16[:, qt * TT:(qt + 1) * TT],
                            bank(4 + qt % 2, 0, TT)[0:16, :],
                            deps=[p.tok("s_pe")])
                    t_qf = p.tok("s_act")
                    t_w = dma(wqp, dram["l1_wk"][:, 128 * pj:128 * (pj + 1)]
                              .rearrange("(c p) w -> p c w", p=128),
                              slot="wqp", queue="gp",
                              deps=[p.tok("s_pe"), p.tok("s_act")])
                    for qt in range(NT):
                        bk = qt % 2
                        dep0 = [t_w, t_lin[("pb", bk)], t_qf]
                        for c in range(NCH):
                            mm(bank(bk, 0, TT), wqp[:, c, :],
                               hb[:, c, qt * TT:(qt + 1) * TT],
                               start=(c == 0), stop=(c == NCH - 1),
                               deps=dep0 if c == 0 else (),
                               inc="s_pe" if c == NCH - 1 else None)
                        t_lin[("pb", bk)] = acopy(
                            pbuf[:, qt * TT:(qt + 1) * TT], bank(bk, 0, TT),
                            deps=[p.tok("s_pe")])
                    t_pb = p.tok("s_act")
                    for tc in range(16):
                        mm(bank(4 + tc % 2, 0, 32),
                           pbuf[:, tc * 128:(tc + 1) * 128],
                           KFBD[:], start=True, stop=True,
                           deps=[t_pb, t_consts, t_lin.get(("kt", tc % 2))],
                           inc="s_pe")
                        acopy(kf_tm[:, tc, 0:16],
                              bank(4 + tc % 2, 0, 32)
                              [:, 16 * hh:16 * (hh + 1)],
                              deps=[p.tok("s_pe")])
                        a2 = p.emit("scalar", lambda e, tc=tc, hh=hh:
                                    e.activation(
                                        kfp[:, tc, :],
                                        bank(4 + tc % 2, 0, 32)
                                        [:, 16 * hh:16 * (hh + 1)],
                                        AF.Identity,
                                        scale=float(np.sqrt(0.5))),
                                    inc="s_act")
                        t_lin[("kt", tc % 2)] = a2
                    t_ktm = p.tok("s_act")

                    def build_kq(khalf, t_ktm=t_ktm):
                        sl8 = slice(8 * khalf, 8 * khalf + 8)
                        t_q1 = dve(lambda e: e.tensor_tensor(
                            kqh[:, :, 0:16], kfp[:, sl8, :], kfp[:, sl8, :],
                            ALU.mult),
                            deps=[t_ktm, t_lin.get(("kq",)),
                                  ("s_pe", p.counts["s_pe"])])
                        vdrain()
                        dve(lambda e: e.tensor_scalar(
                            kqh[:, :, 0:16], kqh[:, :, 0:16], 0.5, None,
                            ALU.mult))
                        vdrain()
                        for i in range(15):
                            off = 16 + 15 * i - (i * (i - 1)) // 2
                            nw = 15 - i
                            dve(lambda e, i=i, off=off, nw=nw: e.tensor_tensor(
                                kqh[:, :, off:off + nw],
                                kfp[:, sl8, i:i + 1].broadcast_to(
                                    (128, 8, nw)),
                                kfp[:, sl8, i + 1:16], ALU.mult))
                        vdrain()
                        tk = dve(lambda e: e.engine_nop())
                        t_lin[("kq",)] = tk
                        return tk

                    t_kq = build_kq(0)
                    t_lin[("kq",)] = t_kq
                    for qt in range(NT):
                        mm(bank(4 + qt % 2, 0, TT)[0:16, :],
                           WKF[64 * hh:64 * hh + 64, :],
                           pbuf[64 * hh:64 * (hh + 1),
                                qt * TT:(qt + 1) * TT],
                           start=True, stop=True,
                           deps=[t_ktm, t_kq, t_consts,
                                 t_lin.get(("qf", qt % 2))],
                           inc="s_pe")
                        t_lin[("qf", qt % 2)] = acopy(
                            kfv[:, qt * TT:(qt + 1) * TT],
                            bank(4 + qt % 2, 0, TT)[0:16, :],
                            deps=[p.tok("s_pe")])
                    t_kf = p.tok("s_act")
                    if "nostate" in phases:
                        continue
                    t_yev = None
                    t_qbc = None
                    t_zev32 = None
                    t_dev = None
                    qfh = QF0
                    for qt in range(4):
                        sl5 = slice(qt * TT, (qt + 1) * TT)
                        ybank = 2 + (qt % 2)
                        ydst = PS[0:65, 512 * ybank:512 * ybank + 512]
                        if qt == 2:
                            t_kq = build_kq(1)
                        dep_q = [t_kf, t_consts,
                                 ("s_act", p.counts["s_act"])]
                        if t_qbc is not None:
                            dep_q.append(t_qbc)
                        mm(PS[0:96, 512 * 4:512 * 4 + TT], SEL[:, 0, 0:96],
                           QF16[:, sl5], start=True, stop=True,
                           deps=dep_q, inc="s_pe")
                        tb1 = acopy(qB[:], PS[0:96, 512 * 4:512 * 4 + TT],
                                    deps=[p.tok("s_pe")])
                        mm(PS[0:96, 512 * 5:512 * 5 + TT], SEL[:, 1, 0:96],
                           QF16[:, sl5], start=True, stop=True,
                           deps=[tb1], inc="s_pe")
                        tq1 = dve(lambda e: e.tensor_tensor(
                            qB[:], qB[:], PS[0:96, 512 * 5:512 * 5 + TT],
                            ALU.mult), deps=[p.tok("s_pe"), tb1])
                        vdrain()
                        mm(PS[0:40, 512 * 4:512 * 4 + TT], SEL[:, 0, 96:136],
                           QF16[:, sl5], start=True, stop=True,
                           deps=[tq1], inc="s_pe")
                        tb2 = acopy(qC[:], PS[0:40, 512 * 4:512 * 4 + TT],
                                    deps=[p.tok("s_pe")])
                        mm(PS[0:40, 512 * 5:512 * 5 + TT], SEL[:, 1, 96:136],
                           QF16[:, sl5], start=True, stop=True,
                           deps=[tb2], inc="s_pe")
                        t_qbc = dve(lambda e: e.tensor_tensor(
                            qC[:], qC[:], PS[0:40, 512 * 5:512 * 5 + TT],
                            ALU.mult), deps=[p.tok("s_pe"), tb2])
                        vdrain()
                        if "st_qphi" in phases:
                            t_yev = t_qbc
                            t_zev32 = t_qbc
                            continue
                        started = False
                        dep_y0 = [t_qbc, t_vo, t_ones]
                        if t_yev is not None:
                            dep_y0.append(t_yev)
                        if t_dev is not None:
                            dep_y0.append(t_dev)
                        for g in (range(qt) if "st_nointer" not in phases
                                  else ()):
                            mm(ydst, dsb_piece(g, "A"), qfh[:, sl5],
                               start=(not started), stop=False, deps=dep_y0)
                            dep_y0 = []
                            started = True
                            if "st_ia" in phases:
                                continue
                            mm(ydst, dsb_piece(g, "B"), qB[:],
                               start=False, stop=False)
                            if "st_iab" in phases:
                                continue
                            mm(ydst, dsb_piece(g, "C"), qC[:],
                               start=False, stop=False)

                        for sj in range(4):
                            sa = 4 * qt + sj
                            cs = 128 * sj
                            nn = TT - cs
                            t_u = mm(bank(0, cs, nn),
                                     kfv[:, sa * 128:(sa + 1) * 128],
                                     QF16[:, qt * TT + cs:(qt + 1) * TT],
                                     start=True, stop=True,
                                     deps=[t_lin.get(("A",)), t_kf] + dep_y0,
                                     inc="s_pe")
                            dep_y0 = []
                            t_A = act(A_sb[:, cs:TT], bank(0, cs, nn),
                                      AF.Square,
                                      bias=1.0, scale=0.5, deps=[t_u])
                            t_Am = dve(lambda e, sj=sj: e.tensor_tensor(
                                A_sb[:, 128 * sj:128 * (sj + 1)],
                                A_sb[:, 128 * sj:128 * (sj + 1)],
                                TRIL[:], ALU.mult),
                                deps=[t_consts, t_A])
                            vdrain()
                            t_lin[("A",)] = t_Am
                            if "st_noy" not in phases:
                                mm(PS[0:65, 512 * ybank + cs:
                                      512 * ybank + TT],
                                   vplus(sa, h), A_sb[:, cs:TT],
                                   start=(not started), stop=(sj == 3),
                                   deps=[t_Am],
                                   inc="s_pe" if sj == 3 else None)
                                started = True
                        t_ymm = p.tok("s_pe")
                        if "st_noy" in phases:
                            t_yev = t_Am
                            t_zev32 = t_Am
                            continue
                        if qt < 3 and "st_nokv" not in phases:
                            dep_kv = [t_ktm, t_kq, t_ones]
                            if t_dev is not None:
                                dep_kv.append(t_dev)
                            for sj in range(4):
                                sa = 4 * qt + sj
                                kqc = sa % 8
                                if sj < 3:
                                    mm(kvps(qt, "B"), kqh[:, kqc, 0:96],
                                       vplus(sa, h),
                                       start=(sj == 0), stop=False,
                                       deps=dep_kv)
                                    dep_kv = []
                                    mm(kvps(qt, "A"), kf_tm[:, sa, :],
                                       vplus(sa, h), start=False, stop=False)
                                    mm(kvps(qt, "C"), kqh[:, kqc, 96:136],
                                       vplus(sa, h), start=False, stop=False)
                                else:
                                    mm(kvps(qt, "A"), kf_tm[:, sa, :],
                                       vplus(sa, h), start=False, stop=False)
                                    mm(kvps(qt, "C"), kqh[:, kqc, 96:136],
                                       vplus(sa, h), start=False, stop=False)
                                    mm(kvps(qt, "B"), kqh[:, kqc, 0:96],
                                       vplus(sa, h), start=False, stop=True,
                                       inc="s_pe")
                            col = 512 * 6 + 256 * qt if qt < 2 else 512 * 7
                            t_dev = acopy(SC[0:96, 3648 + 256 * qt:
                                             3648 + 256 * qt + 241],
                                          PS[0:96, col:col + 241],
                                          deps=[p.tok("s_pe")])
                        ych = h // 2
                        ypart = 64 * (h % 2)
                        ysrc = ydst
                        if ypart == 0:
                            t_yev = acopy(
                                ab[:, ych, qt * TT:(qt + 1) * TT][0:64, :],
                                ysrc[0:64, :], deps=[t_ymm])
                        else:
                            tt1 = acopy(tyL[:, 0:TT], ysrc[0:64, :],
                                        deps=[t_ymm, t_tydma])
                            t_tydma = dma(
                                ab[:, ych, qt * TT:(qt + 1) * TT][64:128, :],
                                tyL[:, 0:TT], slot="tydma", deps=[tt1])
                            t_yev = tt1
                        t_zev32 = acopy(
                            ST[64:65, qt * TT:(qt + 1) * TT],
                            ysrc[64:65, :],
                            deps=[t_ymm] + ([t_zdma] if qt == 0 else []))
                    if "nostate" not in phases:
                        t_zdma = dma(ZT[:, 0, 16 * h:16 * (h + 1)],
                                     ST[64:65, :].rearrange(
                                         "o (q c) -> o q c", q=128),
                                     slot="zdma",
                                     deps=[t_zev32, p.tok(_slot_sem("zbk"))])

            if "stop_lpairs" in phases:
                p.wait("sync", [t_zdma if "nostate" not in phases else None,
                                ("s_pe", p.counts["s_pe"]),
                                ("s_act", p.counts["s_act"]),
                                ("s_dve", p.counts["s_dve"])])
                return (p.tok("s_dve"), p.tok("s_gp"))
            t_zr = dve(lambda e: e.reciprocal(ZT[:, 1, :], ZT[:, 0, :]),
                       deps=[t_zdma, t_tydma, p.tok(_slot_sem("zbk")),
                             ("s_pe", p.counts["s_pe"])])
            vdrain()
            t_hl = hilo(ZTB[:, 0, :], ZTB[:, 1, :], ZT[:, 1, :])
            vdrain()
            t_zb = None
            for h in range(HEADS):
                for hl in range(2):
                    t_zb = dma(ZRH2[h + 16 * hl:h + 16 * hl + 1, :]
                               .rearrange("o (q c) -> o q c", q=128),
                               ZTB[:, hl, 16 * h:16 * (h + 1)],
                               slot="zbk", deps=[t_hl])
            t_norm = {}
            for qt in range(NT):
                for och in range(NCH):
                    bk = och % 2
                    dep0 = [t_zb, t_consts, ("s_act", p.counts["s_act"])]
                    if t_norm.get(bk) is not None:
                        dep0.append(t_norm[bk])
                    mm(bank(bk, 0, TT), PATZ[:, och, :],
                       ZRH2[:, qt * TT:(qt + 1) * TT],
                       start=True, stop=True, deps=dep0, inc="s_pe")
                    t_norm[bk] = dve(
                        lambda e, och=och, qt=qt, bk=bk: e.tensor_tensor(
                            ab[:, och, qt * TT:(qt + 1) * TT],
                            ab[:, och, qt * TT:(qt + 1) * TT],
                            bank(bk, 0, TT), ALU.mult),
                        deps=[p.tok("s_pe")])
                    vdrain()
            t_abn = p.tok("s_dve")
            ev_pr = {}
            for whalf in range(2):
                w2v = SC[:, 0:4096].rearrange("p (c w) -> p c w", c=NCH)
                t_w2 = dma(w2v, dram["l1_wout"]
                           [0:1024, 512 * whalf:512 * (whalf + 1)]
                           .rearrange("(c p) w -> p c w", p=128),
                           slot="wqp", queue="gp",
                           deps=[t_abn, ("s_act", p.counts["s_act"]),
                                 ("s_pe", p.counts["s_pe"])])
                for qt in range(NT):
                    for oj in range(4):
                        och = 4 * whalf + oj
                        bk = 2 + oj % 2
                        dep0 = [t_w2, t_abn]
                        if ev_pr.get(bk) is not None:
                            dep0.append(ev_pr[bk])
                        for c in range(NCH):
                            mm(bank(bk, 0, TT),
                               w2v[:, c, 128 * oj:128 * (oj + 1)],
                               ab[:, c, qt * TT:(qt + 1) * TT],
                               start=(c == 0), stop=(c == NCH - 1),
                               deps=dep0 if c == 0 else (),
                               inc="s_pe" if c == NCH - 1 else None)
                        eng_gp = oj % 2 == 1
                        tpr = ST[:, (0 + (1 if eng_gp else 0)) * TT:
                                 (1 + (1 if eng_gp else 0)) * TT]
                        t_t = act(tpr, bank(bk, 0, TT), AF.Identity,
                                  deps=[p.tok("s_pe"),
                                        ev_pr.get("g" if eng_gp else "d")])
                        ev_pr[bk] = t_t
                        xsl = X[:, och, qt * TT:(qt + 1) * TT]
                        if eng_gp:
                            tok = gp(lambda e, xsl=xsl, tpr=tpr:
                                     e.tensor_tensor(xsl, xsl, tpr, ALU.add),
                                     deps=[t_t])
                            p.emit("gpsimd", lambda e: e.drain())
                            ev_pr["g"] = tok
                        else:
                            tok = dve(lambda e, xsl=xsl, tpr=tpr:
                                      e.tensor_tensor(xsl, xsl, tpr, ALU.add),
                                      deps=[t_t])
                            vdrain()
                            ev_pr["d"] = tok
            return (p.tok("s_dve"), p.tok("s_gp"))

        W_OFF = {"cn_w1": (0, 1024), "cn_w2": (8192, 512), "cn_w3": (12288, 256),
                 "cn_w4": (13312, 1), "gc_w1": (13328, 512), "gc_w2": (17424, 256),
                 "gc_w3": (18448, 1)}

        def wa_tile(name, c, m0, mlen):
            off, wid = W_OFF[name]
            return WA[:, off + c * wid + m0: off + c * wid + m0 + mlen]

        def load_ffn0_weights(deps=()):
            toks = []
            for name, hbm in [("cn_w1", "l0_cn_w1"), ("cn_w2", "l0_cn_w2"),
                              ("cn_w3", "l0_cn_w3"), ("cn_w4", "l0_cn_w4"),
                              ("gc_w1", "l0_gc_w1"), ("gc_w2", "l0_gc_w2"),
                              ("gc_w3", "l0_gc_w3")]:
                off, wid = W_OFF[name]
                src = dram[hbm]
                if wid == 1:
                    nch = src.shape[1]
                    toks.append(dma(WA[:, off: off + nch], src, slot="wff0",
                                    queue="gp", deps=deps))
                    continue
                nch = src.shape[0] // 128
                dst = WA[:, off: off + nch * wid].rearrange(
                    "p (c w) -> p c w", c=nch)
                toks.append(dma(dst, src.rearrange("(c p) w -> p c w", p=128),
                                slot="wff0", queue="gp", deps=deps))
            return toks[-1]

        pe_after_tile = []
        prev_sts = []
        for seq in range(B_PER_CORE):
            pe_after_tile = []
            evs = load_stream(seq, dep_st=prev_sts[-2:])
            last = [evs[-1]]
            for sl_name in sublayers:
                if sl_name in ("ffn0a", "ffn0b"):
                    last = list(ffn0(seq, dep_in=last))
                elif sl_name == "attn0":
                    last = list(attn0(seq, dep_in=last))
                elif sl_name == "attn1":
                    last = list(attn1(seq, dep_in=last))
                elif sl_name == "attn1w":
                    last = list(attn1(seq, dep_in=last, phases=("w",)))
                elif sl_name == "attn1v":
                    last = list(attn1(seq, dep_in=last, phases=("v",)))
                elif sl_name == "attn1ln":
                    last = list(attn1(seq, dep_in=last,
                                      phases=("l", "stop_lpairs",
                                              "st_nointer")))
                elif sl_name == "attn1we":
                    last = list(attn1(seq, dep_in=last, phases=("w", "wnoep")))
                elif sl_name == "attn1lq":
                    last = list(attn1(seq, dep_in=last,
                                      phases=("l", "stop_lpairs", "st_qphi")))
                elif sl_name == "attn1la":
                    last = list(attn1(seq, dep_in=last,
                                      phases=("l", "stop_lpairs", "st_noy")))
                elif sl_name == "attn1lk":
                    last = list(attn1(seq, dep_in=last,
                                      phases=("l", "stop_lpairs", "st_nokv")))
                elif sl_name == "attn1lb":
                    last = list(attn1(seq, dep_in=last,
                                      phases=("l", "stop_lpairs", "nostate")))
                elif sl_name == "attn1lp":
                    last = list(attn1(seq, dep_in=last,
                                      phases=("l", "stop_lpairs")))
                elif sl_name == "attn1wr":
                    last = list(attn1(seq, dep_in=last,
                                      phases=("w", "stop_rows")))
                elif sl_name == "attn1wn":
                    last = list(attn1(seq, dep_in=last,
                                      phases=("w", "stop_norm")))
                elif sl_name == "attn1wp":
                    last = list(attn1(seq, dep_in=last,
                                      phases=("w", "stop_pairs")))
                elif sl_name == "attn1li1":
                    last = list(attn1(seq, dep_in=last,
                                      phases=("l", "stop_lpairs", "st_ia")))
                elif sl_name == "attn1li2":
                    last = list(attn1(seq, dep_in=last,
                                      phases=("l", "stop_lpairs", "st_iab")))
                elif sl_name == "attn1li3":
                    last = list(attn1(seq, dep_in=last,
                                      phases=("l", "stop_lpairs", "st_iabc")))
                elif sl_name == "attn1l":
                    last = list(attn1(seq, dep_in=last, phases=("l",)))
                elif sl_name in ("ffn1a", "ffn1b"):
                    last = list(ffn1(seq, dep_in=last))
                elif sl_name == "lnonly":
                    last = list(ffn0(seq, dep_in=last, lnonly=True))
                elif sl_name.startswith("ffn0nc"):
                    last = list(ffn0(seq, dep_in=last,
                                     ncycles=int(sl_name[-1])))
                else:
                    raise NotImplementedError(sl_name)
            prev_sts = store_stream(seq, last)
        p.wait("sync", [p.tok(_slot_sem("st0")), p.tok(_slot_sem("st1"))])

        block = es.enter_context(nc.Block())

        @block.sync
        def _(e):
            p.replay("sync", e)

        @block.tensor
        def _(e):
            p.replay("tensor", e)

        @block.scalar
        def _(e):
            p.replay("scalar", e)

        @block.vector
        def _(e):
            p.replay("vector", e)

        @block.gpsimd
        def _(e):
            p.replay("gpsimd", e)

    return nc



def _prep_consts():
    ident = np.eye(128, dtype=np.float32)
    tril = np.triu(np.ones((128, 128), dtype=np.float32))
    win = np.zeros((128, 3, 256), dtype=np.float32)
    for di, delta in enumerate((-128, 0, 128)):
        pp = np.arange(128)[:, None]
        qq = np.arange(256)[None, :]
        k = pp + delta
        win[:, di, :] = ((k <= qq) & (k >= qq - WINDOW)).astype(np.float32)
    iu, ju = np.triu_indices(FEAT)
    sel1 = np.zeros((FEAT, NQUAD), dtype=np.float32)
    sel2 = np.zeros((FEAT, NQUAD), dtype=np.float32)
    for r in range(NQUAD):
        sel1[iu[r], r] = 1.0
        sel2[ju[r], r] = 1.0
    return ident, tril, win, sel1, sel2


def _prep_weights(inp):
    w = {}
    R = np.ascontiguousarray(inp["L0_attn_W_qkv"], dtype=np.float32).reshape(
        HEADS, 3, HD, D)
    w["l0_wq"] = R[:, 0].transpose(2, 0, 1).reshape(D, D)
    w["l0_wk"] = R[:, 1].transpose(2, 0, 1).reshape(D, D)
    w["l0_wv"] = R[:, 2].transpose(2, 0, 1).reshape(D, D)
    w["l0_wo"] = np.asarray(inp["L0_attn_W_out"]).T
    w["l0_bo"] = np.asarray(inp["L0_attn_b_out"]).reshape(8, 128).T
    w["l0_cn_w1"] = inp["L0_ffn_cn_w1"]
    w["l0_cn_w2"] = inp["L0_ffn_cn_w2"]
    w["l0_cn_w3"] = inp["L0_ffn_cn_w3"]
    w["l0_cn_w4"] = np.asarray(inp["L0_ffn_cn_w4"]).reshape(2, 128).T
    w["l0_cn_b1"] = np.asarray(inp["L0_ffn_cn_b1"]).reshape(8, 128).T
    w["l0_cn_b2"] = np.asarray(inp["L0_ffn_cn_b2"]).reshape(4, 128).T
    w["l0_cn_b3"] = np.asarray(inp["L0_ffn_cn_b3"]).reshape(2, 128).T
    w["l0_cn_b4"] = np.asarray(inp["L0_ffn_cn_b4"]).reshape(1, 1)
    w["l0_gc_w1"] = inp["L0_ffn_gc_w1"]
    w["l0_gc_w2"] = inp["L0_ffn_gc_w2"]
    w["l0_gc_w3"] = np.asarray(inp["L0_ffn_gc_w3"]).reshape(2, 128).T
    w["l0_gc_b1"] = np.asarray(inp["L0_ffn_gc_b1"]).reshape(4, 128).T
    w["l0_gc_b2"] = np.asarray(inp["L0_ffn_gc_b2"]).reshape(2, 128).T
    w["l0_gc_b3"] = np.asarray(inp["L0_ffn_gc_b3"]).reshape(1, 1)
    w["l0_ar"] = np.asarray(inp["L0_ffn_adaptive_range"]).reshape(1, 1)
    w["l1_wq"] = inp["L1_attn_Wq"]
    w["l1_wk"] = inp["L1_attn_Wk"]
    w["l1_wv"] = inp["L1_attn_Wv"]
    w["l1_wout"] = inp["L1_attn_Wout"]
    w["l1_wqf"] = np.vstack([np.asarray(inp["L1_attn_Wqf"])] * 2)
    w["l1_wkf"] = np.vstack([np.asarray(inp["L1_attn_Wkf"])] * 2)
    w["l1_normw"] = np.asarray(inp["L1_attn_norm_w"]).reshape(8, 128).T
    w["l1_w1"] = inp["L1_ffn_w1"]
    w["l1_b1"] = np.asarray(inp["L1_ffn_b1"]).reshape(32, 128).T
    w["l1_w2"] = inp["L1_ffn_w2"]
    w["l1_b2"] = np.asarray(inp["L1_ffn_b2"]).reshape(8, 128).T
    kfbd = np.zeros((128, 32), dtype=np.float32)
    kfbd[0:64, 0:16] = np.asarray(inp["L1_attn_Wkf"])
    kfbd[64:128, 16:32] = np.asarray(inp["L1_attn_Wkf"])
    w["kf_blockdiag"] = kfbd
    ident, tril, win, sel1, sel2 = _prep_consts()
    patz = np.zeros((32, NCH, 128), dtype=np.float32)
    for och in range(NCH):
        for mcol in range(128):
            hd_ = 2 * och + (mcol // 64)
            patz[hd_, och, mcol] = 1.0
            patz[16 + hd_, och, mcol] = 1.0
    w["patz"] = patz
    w["id2"] = np.eye(2, dtype=np.float32)
    w["ident"] = ident
    w["tril"] = tril
    w["winmask"] = win
    w["sel1"] = sel1
    w["sel2"] = sel2
    return {k: np.ascontiguousarray(v, dtype=np.float32) for k, v in w.items()}


def kernel(**inputs):
    x = np.ascontiguousarray(inputs["L0_attn_x"], dtype=np.float32)
    w = _prep_weights(inputs)
    nc = build_program()
    in_maps = []
    for c in range(N_CORES):
        m = {"x": x[c * B_PER_CORE:(c + 1) * B_PER_CORE]}
        m.update(w)
        in_maps.append(m)
    res = run_bass_kernel_spmd(nc, in_maps, core_ids=list(range(N_CORES)))
    out = np.concatenate([r["y"] for r in res.results], axis=0)
    return out.astype(np.float32)
